# version 03
# baseline (speedup 1.0000x reference)
import math
import contextlib
import numpy as np
import concourse.bass as bass
import concourse.mybir as mybir
from concourse.bass_utils import run_bass_kernel_spmd

F32 = mybir.dt.float32
BF16 = mybir.dt.bfloat16
I32 = mybir.dt.int32
AF = mybir.ActivationFunctionType
ALU = mybir.AluOpType
AX = mybir.AxisListType

NCORES = 8
B, T, D = 4, 4096, 1024
NH, HD = 16, 64
T_OWN = 2048
NT_OWN = 16
NV = 48
V0 = 16
OWN0 = 32
INNER0 = 4096
INNER1 = 2816

SLOPES = [2.0 ** (-0.5 * (h + 1)) for h in range(NH)]
BAND = [min(33, int(math.ceil(45.0 / SLOPES[h] / 128.0)) + 2) for h in range(NH)]

DEBUG = True


def _w1shard(wg1p, wu1p, wd1, c):
    sh = np.zeros((7, 128, INNER1), dtype=np.float32)
    for i in range(3):
        sh[i, :, 0:1024] = wg1p[(3 * c + i) * 128:(3 * c + i + 1) * 128, :]
        sh[3 + i, :, 0:1024] = wu1p[(3 * c + i) * 128:(3 * c + i + 1) * 128, :]
    sh[6, :, :] = wd1[c * 128:(c + 1) * 128, :]
    return sh


def _host_consts():
    c = {"identity": np.eye(128, dtype=np.float32)}
    j = np.arange(128)
    f = np.arange(129)
    ang = 2.0 * np.pi * np.outer(j, f) / 256.0
    c["F_re"] = np.cos(ang).astype(np.float32)
    c["F_im"] = (-np.sin(ang)).astype(np.float32)
    w = np.where((f == 0) | (f == 128), 1.0, 2.0)
    angT = 2.0 * np.pi * np.outer(f, j) / 256.0
    c["G_re"] = (w[:, None] * np.cos(angT) / 256.0).astype(np.float32)
    c["G_im"] = (-w[:, None] * np.sin(angT) / 256.0).astype(np.float32)
    m = np.zeros((128, 2688), dtype=np.float32)
    kj = np.arange(128)[:, None]
    off = 0
    for W, nd in ((512, 4), (256, 2), (128, 1)):
        for d in range(nd):
            ti = np.arange(W)[None, :]
            m[:, off:off + W] = (ti - d * 128 - kj >= 0).astype(np.float32)
            off += W
    c["attn_mask"] = m
    return c


CHUNKW = [128 if h < 2 else (256 if h < 4 else 512) for h in range(NH)]


def _attn_schedule():
    sched = []
    bias_idx = {}
    nb = 0
    for h in range(NH):
        W = CHUNKW[h]
        nqc = T_OWN // W
        nts = W // 128
        for qc in range(nqc):
            qs_lo = qc * nts
            qs_hi = (qc + 1) * nts - 1
            vj_lo = max(V0, qs_lo + 32 + 1 - BAND[h])
            vj_hi = qs_hi + 32
            vjs = list(range(vj_lo, vj_hi + 1))
            for vj in vjs:
                bias_idx[(h, qc, vj)] = nb
                nb += 1
            sched.append((h, qc, vjs))
    return sched, bias_idx, nb


ATT_SCHED, ATT_BIDX, ATT_NB = _attn_schedule()


def _attn_bias_host(half):
    tab = np.zeros((128, ATT_NB), dtype=np.float32)
    for (h, qc, vj), col in ATT_BIDX.items():
        W = CHUNKW[h]
        q0g = half * 2048 + qc * W
        ti_center = q0g + (W - 1) - 30.0 / SLOPES[h]
        kj0g = (vj - 32) * 128 + half * 2048
        kjg = kj0g + np.arange(128)
        bias = SLOPES[h] * (kjg - ti_center)
        bias = np.where((kjg < 0) | (kjg > q0g + W - 1), -1e30, bias)
        tab[:, col] = bias.astype(np.float32)
    return tab


def _attn_fden_host(half):
    tab = np.zeros((128, 256), dtype=np.float32)
    for h in range(NH):
        W = CHUNKW[h]
        nts = W // 128
        nqc = T_OWN // W
        for qc in range(nqc):
            q0g = half * 2048 + qc * W
            ti_center = q0g + (W - 1) - 30.0 / SLOPES[h]
            for tt in range(nts):
                tig = q0g + tt * 128 + np.arange(128)
                tab[:, h * 16 + qc * nts + tt] = np.exp(
                    -SLOPES[h] * (tig - ti_center)).astype(np.float32)
    return tab


def build_program():
    nc = bass.Bass("TRN2", target_bir_lowering=False)
    ops = {e: [] for e in ("tensor", "vector", "scalar", "gpsimd", "sync")}

    def emit(eng, fn):
        ops[eng].append(fn)

    ap = {}

    def din(name, shape, dtype=F32):
        ap[name] = nc.dram_tensor(name, list(shape), dtype, kind="ExternalInput").ap()
        return ap[name]

    def dout(name, shape, dtype=F32):
        ap[name] = nc.dram_tensor(name, list(shape), dtype, kind="ExternalOutput").ap()
        return ap[name]

    def dint(name, shape, dtype=F32):
        ap[name] = nc.dram_tensor(name, list(shape), dtype, kind="Internal").ap()
        return ap[name]

    din("xv", (NV * 128, D))
    din("wqkv", (D, 3 * D))
    din("wgT", (D, INNER0))
    din("wuT", (D, INNER0))
    din("wdT", (INNER0, D))
    din("w1shard", (7, 128, INNER1))
    din("ssm_logdt", (1, 1024))
    din("ssm_aim1", (1, 1024))
    din("ssm_are0", (1, 1024))
    din("ssm_AR2", (128, 256))
    din("ssm_AI2", (128, 256))
    din("ssm_CR2", (128, 256))
    din("ssm_CI2", (128, 256))
    din("ssm_LD2", (128, 256))
    din("ssm_flast", (128, 1))
    din("ssm_D", (1, 1))
    din("ssm_lrows", (4, 128))
    din("ssm_h4mask", (128, 4))
    din("ssm_blkA", (128, 256))
    din("ssm_blkB", (128, 256))
    din("F_re", (128, 129))
    din("F_im", (128, 129))
    din("G_re", (129, 128))
    din("G_im", (129, 128))
    din("identity", (128, 128))
    din("attn_bias", (128, ATT_NB))
    din("attn_fden", (128, 256))
    din("attn_mask", (128, 2688))
    din("wout", (D, D))
    dout("out_x", (T_OWN, D))

    dbgs = []

    def dbg(name, shape, dtype=F32):
        dbgs.append(name)
        return dout("dbg_" + name, shape, dtype)

    if DEBUG:
        dbg("xn0", (512, D))
        dbg("stat", (128, 64))
        dbg("xg0", (128, D))
        dbg("nx_t0", (128, D), BF16)
        dbg("g_t1", (128, D))
        dbg("wgq_c2", (128, 512))
        dbg("ao_t2", (128, D))
        dbg("x1_t3", (128, D))
        dbg("u_t2", (128, D))
        dbg("xq_t2", (128, D))
        dbg("g2_t1", (128, D))
        dbg("qT_h3", (64, 128))
        dbg("kT_h5", (64, 128))
        dbg("v_t40", (128, 256))
        dbg("ssm_tre", (128, 512))
        dbg("ssm_sp", (128, 512))
        dbg("ssm_y0", (128, 1024))
        dbg("ssm_kf", (128, 1024))
        dbg("ssm_slabf", (128, 512))
        dbg("ssm_slabr", (128, 512))
        dbg("ssm_ug", (128, 512))

    kT_sp = dint("kT_sp", (NH, NV, 64, 128))
    wgq_sp = dint("wgq_sp", (8, 128, INNER0), BF16)
    wuq_sp = dint("wuq_sp", (8, 128, INNER0), BF16)
    wdq_sp = dint("wdq_sp", (32, 128, D), BF16)
    g_sp = dint("g_sp", (NT_OWN, 128, D))
    nxt_sp = dint("nxt_sp", (8, 128, T_OWN), BF16)
    u_sp = dint("u_sp", (NT_OWN, 128, D))
    W1RANK = 6 * 128 * 1024 + 128 * INNER1
    w1agi = dint("w1agi", (W1RANK,))
    w1ago = dint("w1ago", (NCORES * W1RANK,))
    wg1q_sp = dint("wg1q_sp", (2, 8, 128, INNER1), BF16)
    wu1q_sp = dint("wu1q_sp", (2, 8, 128, INNER1), BF16)
    wd1q_sp = dint("wd1q_sp", (2, 22, 128, D), BF16)
    g2_sp = dint("g2_sp", (NT_OWN, 128, D))
    hq_sp = dint("hq_sp", (2, 22, 128, T_OWN), BF16)
    fwdT_sp = dint("fwdT_sp", (8, 16, 128, 512))
    sag_i = dint("sag_i", (2 * 128 * 256,))
    sag_o = dint("sag_o", (2 * 2 * 128 * 256,))
    v_sp = dint("v_sp", (NV, 128, D))
    qT_sp = dint("qT_sp", (NH, NT_OWN, 64, 128))
    xn_sp = dint("xn_sp", (NT_OWN, 128, D))
    ao_sp = dint("ao_sp", (NT_OWN, 128, D))
    x1_sp = dint("x1_sp", (NT_OWN, 128, D))

    with contextlib.ExitStack() as stack:
        def sb(name, cols, dtype=F32):
            return stack.enter_context(nc.sbuf_tensor(name, [128, cols], dtype))

        ident = sb("ident", 128)
        identb = sb("identb", 128, BF16)
        AW = sb("AW", 16384)
        AACT = sb("AACT", 13440)
        AS = sb("AS", 8192)
        ABF = sb("ABF", 10240, BF16)
        nxtg = sb("nxtg", 4096, BF16)
        AI = sb("AI", 1024, I32)
        blkq = sb("blkq", 576)
        btab = sb("btab", ATT_NB)
        ften = sb("ften", 256)
        mtab = sb("mtab", 2688)
        aev = sb("aev", 256)
        rsx = sb("rsx", 16)
        rsh = sb("rsh", 16)
        wsc = sb("wsc", 16)
        stat = sb("stat", 64)
        padb = sb("padb", 128)
        padb2 = sb("padb2", 128)

        psum = [stack.enter_context(nc.psum_tensor(f"ps{i}", [128, 512], F32))
                for i in range(8)]

        sems, cnt = {}, {}
        for s in ("ldx", "ldw", "tE", "vE", "sE", "spl", "stf"):
            sems[s] = stack.enter_context(nc.semaphore(s))
            cnt[s] = 0

        def bump(s, by=1):
            cnt[s] += by
            return cnt[s]

        def wqkv_slice(kc, col0, w):
            if kc < 5:
                return AW[:, kc * 3072 + col0: kc * 3072 + col0 + w]
            return AACT[:, (kc - 5) * 3072 + col0: (kc - 5) * 3072 + col0 + w]

        xg_buf = AS[:, 0:4096]
        xn_buf = AS[:, 4096:8192]
        xnT_buf = AACT[:, 9216:13312]

        def ph1_loads(sync):
            sync.dma_start(out=ident[:, :], in_=ap["identity"]).then_inc(sems["ldw"], 16)
            for kc in range(8):
                sync.dma_start(out=wqkv_slice(kc, 0, 3072),
                               in_=ap["wqkv"][kc * 128:(kc + 1) * 128, :]).then_inc(sems["ldw"], 16)
        emit("sync", ph1_loads)
        ldw_wqkv = bump("ldw", 16 * 9)

        def f_once_v(vector, need=cnt["ldw"] + 16 * 9 if False else 16):
            vector.wait_ge(sems["ldw"], 16)
            vector.tensor_copy(identb[:, :], ident[:, :])
            vector.memset(wsc[:, 15:16], 1.0)
        emit("vector", f_once_v)

        NGRP = 8
        prev = {"vE": 0, "spl": 0, "tr1": 0, "xnspl": 0}

        for g in range(NGRP):
            own = g >= 4
            vbase = V0 + g * 4

            def f_sync(sync, g=g, vbase=vbase, pv=prev["vE"], ps=prev["spl"]):
                if g > 0:
                    sync.wait_ge(sems["vE"], pv)
                    sync.wait_ge(sems["spl"], ps)
                sync.dma_start(out=xg_buf.rearrange("p (t d) -> p t d", d=D),
                               in_=ap["xv"][vbase * 128:(vbase + 4) * 128, :]
                               .rearrange("(t p) d -> p t d", p=128)).then_inc(sems["ldx"], 16)
            emit("sync", f_sync)
            ldx_need = bump("ldx", 16)

            def f_vec1(vector, ldx_need=ldx_need):
                vector.wait_ge(sems["ldx"], ldx_need)
                for i in range(4):
                    vector.tensor_reduce(stat[:, i:i + 1], xg_buf[:, i * D:(i + 1) * D],
                                         AX.X, ALU.add)
            emit("vector", f_vec1)

            def f_sc1(scalar, ldx_need=ldx_need, ptr=prev["tr1"], pxs=prev["xnspl"]):
                scalar.wait_ge(sems["ldx"], ldx_need)
                if ptr:
                    scalar.wait_ge(sems["tE"], ptr)
                if pxs:
                    scalar.wait_ge(sems["spl"], pxs)
                ins = None
                for i in range(4):
                    ins = scalar.activation(xn_buf[:, i * D:(i + 1) * D],
                                            xg_buf[:, i * D:(i + 1) * D], AF.Square)
                ins.then_inc(sems["sE"], 1)
            emit("scalar", f_sc1)
            sq_done = bump("sE")

            def f_vec2(vector, sq_done=sq_done):
                vector.wait_ge(sems["sE"], sq_done)
                for i in range(4):
                    vector.tensor_reduce(stat[:, 4 + i:5 + i],
                                         xn_buf[:, i * D:(i + 1) * D], AX.X, ALU.add)
                vector.tensor_scalar(stat[:, 8:12], stat[:, 0:4], 1.0 / D, None, ALU.mult)
                vector.tensor_scalar(stat[:, 12:16], stat[:, 4:8], 1.0 / D, None, ALU.mult)
                vector.tensor_copy(padb[:, :], ident[:, :])
                vector.tensor_tensor(stat[:, 16:20], stat[:, 8:12], stat[:, 8:12], ALU.mult)
                vector.tensor_copy(padb[:, :], ident[:, :])
                vector.tensor_tensor(stat[:, 20:24], stat[:, 12:16], stat[:, 16:20], ALU.subtract)
                vector.tensor_copy(padb[:, :], ident[:, :])
                ins = vector.tensor_scalar(stat[:, 16:20], stat[:, 20:24], 1e-5, None, ALU.add)
                ins.then_inc(sems["vE"], 1)
            emit("vector", f_vec2)
            var_done = bump("vE")

            def f_sc2(scalar, var_done=var_done):
                scalar.wait_ge(sems["vE"], var_done)
                scalar.activation(stat[:, 36:40], stat[:, 16:20], AF.Ln)
                scalar.copy(padb2[:, :], ident[:, :])
                ins = scalar.activation(stat[:, 24:28], stat[:, 36:40], AF.Exp, scale=-0.5)
                ins.then_inc(sems["sE"], 1)
            emit("scalar", f_sc2)
            rstd_done = bump("sE")

            def f_vec3(vector, rstd_done=rstd_done):
                vector.wait_ge(sems["sE"], rstd_done)
                vector.tensor_tensor(stat[:, 32:36], stat[:, 8:12], stat[:, 24:28], ALU.mult)
                vector.tensor_copy(padb[:, :], ident[:, :])
                ins = vector.tensor_scalar(stat[:, 28:32], stat[:, 32:36], -1.0, None, ALU.mult)
                ins.then_inc(sems["vE"], 1)
            emit("vector", f_vec3)
            negmr_done = bump("vE")

            def f_vec3b(vector, ptr=prev["tr1"], pxs=prev["xnspl"]):
                if ptr:
                    vector.wait_ge(sems["tE"], ptr)
                if pxs:
                    vector.wait_ge(sems["spl"], pxs)
                vector.tensor_copy(padb[:, :], ident[:, :])
                ins = None
                for i in range(4):
                    ins = vector.tensor_scalar(xn_buf[:, i * D:(i + 1) * D],
                                               xg_buf[:, i * D:(i + 1) * D],
                                               stat[:, 24 + i:25 + i],
                                               stat[:, 28 + i:29 + i],
                                               ALU.mult, ALU.add)
                ins.then_inc(sems["vE"], 1)
            emit("vector", f_vec3b)
            xn_done = bump("vE")
            XNSEM = "vE"

            xnspl_done = prev["xnspl"]
            if own:
                def f_g1(gpsimd, g=g, xn_done=xn_done):
                    gpsimd.wait_ge(sems["vE"], xn_done)
                    t0 = (g - 4) * 4
                    gpsimd.dma_start(
                        out=xn_sp[t0:t0 + 4, :, :].rearrange("t p d -> p t d"),
                        in_=xn_buf.rearrange("p (t d) -> p t d", d=D),
                    ).then_inc(sems["spl"], 16)
                emit("gpsimd", f_g1)
                xnspl_done = bump("spl", 16)
            if DEBUG and g == 4:
                def f_gdbg(gpsimd, xn_done=xn_done):
                    gpsimd.wait_ge(sems["vE"], xn_done)
                    gpsimd.dma_start(
                        out=ap["dbg_xn0"].rearrange("(t p) d -> p t d", p=128),
                        in_=xn_buf.rearrange("p (t d) -> p t d", d=D),
                    ).then_inc(sems["spl"], 16)
                    gpsimd.dma_start(out=ap["dbg_stat"], in_=stat[:, :]).then_inc(sems["spl"], 16)
                    gpsimd.dma_start(out=ap["dbg_xg0"], in_=xg_buf[:, 0:D]).then_inc(sems["spl"], 16)
                emit("gpsimd", f_gdbg)
                xnspl_done = bump("spl", 48)

            def f_t1a(tensor, xn_done=xn_done, pv=prev["vE"]):
                tensor.wait_ge(sems["vE"], xn_done)
                if pv:
                    tensor.wait_ge(sems["vE"], pv)
                ins = None
                for kc in range(4):
                    for i in range(4):
                        ins = tensor.transpose(
                            psum[kc][:, i * 128:(i + 1) * 128],
                            xn_buf[:, i * D + kc * 128: i * D + kc * 128 + 128],
                            ident[:, :])
                ins.then_inc(sems["tE"], 1)
            emit("tensor", f_t1a)
            tr0_done = bump("tE")

            def f_vec4a(vector, tr0=tr0_done):
                vector.wait_ge(sems["tE"], tr0)
                ins = None
                for kc in range(4):
                    ins = vector.tensor_copy(xnT_buf[:, kc * 512:(kc + 1) * 512],
                                             psum[kc][:, :])
                ins.then_inc(sems["vE"], 1)
            emit("vector", f_vec4a)
            ev0_done = bump("vE")

            def f_t1b(tensor, ev0=ev0_done):
                tensor.wait_ge(sems["vE"], ev0)
                ins = None
                for kc4 in range(4):
                    kc = 4 + kc4
                    for i in range(4):
                        ins = tensor.transpose(
                            psum[kc4][:, i * 128:(i + 1) * 128],
                            xn_buf[:, i * D + kc * 128: i * D + kc * 128 + 128],
                            ident[:, :])
                ins.then_inc(sems["tE"], 1)
            emit("tensor", f_t1b)
            tr1_done = bump("tE")

            def f_vec4b(vector, tr1=tr1_done):
                vector.wait_ge(sems["tE"], tr1)
                ins = None
                for kc4 in range(4):
                    kc = 4 + kc4
                    ins = vector.tensor_copy(xnT_buf[:, kc * 512:(kc + 1) * 512],
                                             psum[kc4][:, :])
                ins.then_inc(sems["vE"], 1)
            emit("vector", f_vec4b)
            ev1_done = bump("vE")

            if own:
                t_abs0 = (g - 4) * 4

                def f_qv1(vector, xn_done=xn_done, g=g):
                    for i in range(4):
                        vector.tensor_reduce(stat[:, 44 + i:45 + i],
                                             xn_buf[:, i * D:(i + 1) * D], AX.X,
                                             ALU.max, apply_absolute_value=True)
                    vector.tensor_copy(padb[:, :], ident[:, :])
                    ins = vector.tensor_scalar(stat[:, 48:52], stat[:, 44:48], 1e-5,
                                               None, ALU.max)
                    ins.then_inc(sems["vE"], 1)
                emit("vector", f_qv1)
                qv1_done = bump("vE")

                def f_qs1(scalar, qv1=qv1_done):
                    scalar.wait_ge(sems["vE"], qv1)
                    scalar.activation(stat[:, 52:56], stat[:, 48:52], AF.Ln)
                    scalar.copy(padb2[:, :], ident[:, :])
                    ins = scalar.activation(stat[:, 56:60], stat[:, 52:56], AF.Exp,
                                            scale=-1.0)
                    ins.then_inc(sems["sE"], 1)
                emit("scalar", f_qs1)
                qs1_done = bump("sE")

                def f_qv2(vector, qs1=qs1_done, g=g):
                    vector.wait_ge(sems["sE"], qs1)
                    vector.tensor_scalar(stat[:, 60:64], stat[:, 56:60], 127.0, None,
                                         ALU.mult)
                    vector.tensor_copy(padb[:, :], ident[:, :])
                    ins = vector.tensor_scalar(rsx[:, (g - 4) * 4:(g - 4) * 4 + 4],
                                               stat[:, 48:52], 1.0 / 127.0, None,
                                               ALU.mult)
                    ins.then_inc(sems["vE"], 1)
                emit("vector", f_qv2)
                qv2_done = bump("vE")

                nf32_dones = []
                prev_qev = ev1_done
                prev_nxt_spl = 0
                for i in range(4):
                    def f_qv3(vector, i=i, qv2=qv2_done):
                        if i == 0:
                            vector.wait_ge(sems["vE"], qv2)
                        xt = xn_buf[:, i * D:(i + 1) * D]
                        tbuf = AS[:, 0:1024]
                        kf = AS[:, 1024:2048]
                        dl = AS[:, 2048:3072]
                        c1 = AS[:, 3072:4096]
                        vector.tensor_scalar(tbuf, xt, stat[:, 60 + i:61 + i], None,
                                             ALU.mult)
                        vector.tensor_copy(AI[:, 0:1024], tbuf)
                        vector.tensor_copy(kf, AI[:, 0:1024])
                        vector.tensor_tensor(dl, tbuf, kf, ALU.subtract)
                        vector.tensor_scalar(c1, dl, 0.5, None, ALU.is_gt)
                        vector.tensor_scalar(tbuf, dl, -0.5, None, ALU.is_lt)
                        vector.tensor_tensor(c1, c1, tbuf, ALU.subtract)
                        ins = vector.tensor_tensor(dl, kf, c1, ALU.add)
                        ins.then_inc(sems["vE"], 1)
                    emit("vector", f_qv3)
                    nf32_done = bump("vE")
                    nf32_dones.append(nf32_done)

                    def f_qt(tensor, i=i, nf=nf32_done, g=g, pev=prev_qev):
                        tensor.wait_ge(sems["vE"], nf)
                        tensor.wait_ge(sems["vE"], pev)
                        ins = None
                        for kc in range(8):
                            ins = tensor.transpose(
                                psum[kc // 4][:, (kc % 4) * 128:(kc % 4) * 128 + 128],
                                AS[:, 2048 + kc * 128:2048 + kc * 128 + 128],
                                ident[:, :])
                        ins.then_inc(sems["tE"], 1)
                    emit("tensor", f_qt)
                    qt_done = bump("tE")

                    def f_qev(vector, i=i, qt=qt_done, g=g, pspl=prev_nxt_spl):
                        vector.wait_ge(sems["tE"], qt)
                        if pspl:
                            vector.wait_ge(sems["spl"], pspl)
                        ins = None
                        for half in range(2):
                            for kc4 in range(4):
                                kc = half * 4 + kc4
                                ins = vector.tensor_copy(
                                    ABF[:, kc * 128: kc * 128 + 128],
                                    psum[half][:, kc4 * 128:kc4 * 128 + 128])
                        ins.then_inc(sems["vE"], 1)
                    emit("vector", f_qev)
                    prev_qev = bump("vE")

                    def f_qsp(gpsimd, i=i, g=g, qe=prev_qev):
                        gpsimd.wait_ge(sems["vE"], qe)
                        t_abs = (g - 4) * 4 + i
                        gpsimd.dma_start(
                            out=nxt_sp[:, :, t_abs * 128:(t_abs + 1) * 128]
                            .rearrange("kc p f -> p kc f"),
                            in_=ABF[:, 0:1024].rearrange("p (kc f) -> p kc f", f=128),
                        ).then_inc(sems["spl"], 16)
                    emit("gpsimd", f_qsp)
                    prev_nxt_spl = bump("spl", 16)
                nxt_ev = cnt["vE"]

                if DEBUG and g == 4:
                    def f_dbg_nx(gpsimd, ns=prev_nxt_spl):
                        gpsimd.wait_ge(sems["spl"], ns)
                        gpsimd.dma_start(
                            out=ap["dbg_nx_t0"].rearrange("a (kc f) -> a kc f", f=128),
                            in_=nxt_sp[:, :, 0:128].rearrange("kc p f -> p kc f"),
                        ).then_inc(sems["spl"], 16)
                    emit("gpsimd", f_dbg_nx)
                    bump("spl", 16)
                ev1_done = nxt_ev

            def f_t2(tensor, ev1=ev1_done, wq_need=ldw_wqkv):
                tensor.wait_ge(sems["vE"], ev1)
                tensor.wait_ge(sems["ldw"], wq_need)
                ins = None
                for dc in range(8):
                    for kc in range(8):
                        ins = tensor.matmul(psum[dc][:, :],
                                            wqkv_slice(kc, 1024 + dc * 128, 128),
                                            xnT_buf[:, kc * 512:(kc + 1) * 512],
                                            start=(kc == 0), stop=(kc == 7))
                ins.then_inc(sems["tE"], 1)
            emit("tensor", f_t2)
            kmm_done = bump("tE")

            def f_vec5(vector, kmm=kmm_done):
                vector.wait_ge(sems["tE"], kmm)
                ins = None
                for dc in range(8):
                    ins = vector.tensor_copy(AS[:, dc * 512:(dc + 1) * 512], psum[dc][:, :])
                ins.then_inc(sems["vE"], 1)
            emit("vector", f_vec5)
            kev_done = bump("vE")

            def f_g2(gpsimd, kev=kev_done, vbase=vbase):
                gpsimd.wait_ge(sems["vE"], kev)
                for dc in range(8):
                    src = AS[:, dc * 512:(dc + 1) * 512].rearrange("p (t f) -> p t f", f=128)
                    for hh in range(2):
                        gpsimd.dma_start(
                            out=kT_sp[2 * dc + hh, vbase:vbase + 4, :, :]
                            .rearrange("t q f -> q t f"),
                            in_=src[hh * 64:(hh + 1) * 64, :, :],
                        ).then_inc(sems["spl"], 16)
            emit("gpsimd", f_g2)
            kspl_done = bump("spl", 16 * 16)

            def f_t3(tensor, kev=kev_done):
                tensor.wait_ge(sems["vE"], kev)
                ins = None
                for ti in range(4):
                    for dv in range(2):
                        pb = psum[ti * 2 + dv]
                        for kc in range(8):
                            ins = tensor.matmul(
                                pb[:, :],
                                xnT_buf[:, kc * 512 + ti * 128: kc * 512 + ti * 128 + 128],
                                wqkv_slice(kc, 2048 + dv * 512, 512),
                                start=(kc == 0), stop=(kc == 7))
                ins.then_inc(sems["tE"], 1)
            emit("tensor", f_t3)
            vmm_done = bump("tE")

            def f_vec6(vector, vmm=vmm_done, kspl=kspl_done):
                vector.wait_ge(sems["tE"], vmm)
                vector.wait_ge(sems["spl"], kspl)
                ins = None
                for ti in range(4):
                    for dv in range(2):
                        ins = vector.tensor_copy(
                            AS[:, (ti * 2 + dv) * 512:(ti * 2 + dv) * 512 + 512],
                            psum[ti * 2 + dv][:, :])
                ins.then_inc(sems["vE"], 1)
            emit("vector", f_vec6)
            vev_done = bump("vE")

            def f_g3(gpsimd, vev=vev_done, vbase=vbase):
                gpsimd.wait_ge(sems["vE"], vev)
                for ti in range(4):
                    gpsimd.dma_start(
                        out=v_sp[vbase + ti, :, :],
                        in_=AS[:, ti * 1024: ti * 1024 + 1024],
                    ).then_inc(sems["spl"], 16)
            emit("gpsimd", f_g3)
            vspl_done = bump("spl", 16 * 4)

            if own:
                def f_t4(tensor, vev=vev_done):
                    tensor.wait_ge(sems["vE"], vev)
                    ins = None
                    for dc in range(8):
                        for kc in range(8):
                            ins = tensor.matmul(psum[dc][:, :],
                                                wqkv_slice(kc, dc * 128, 128),
                                                xnT_buf[:, kc * 512:(kc + 1) * 512],
                                                start=(kc == 0), stop=(kc == 7))
                    ins.then_inc(sems["tE"], 1)
                emit("tensor", f_t4)
                qmm_done = bump("tE")

                def f_vec7(vector, qmm=qmm_done, vspl=vspl_done):
                    vector.wait_ge(sems["tE"], qmm)
                    vector.wait_ge(sems["spl"], vspl)
                    ins = None
                    for dc in range(8):
                        ins = vector.tensor_copy(AS[:, dc * 512:(dc + 1) * 512],
                                                 psum[dc][:, :])
                    ins.then_inc(sems["vE"], 1)
                emit("vector", f_vec7)
                qev_done = bump("vE")

                def f_g4(gpsimd, g=g, qev=qev_done):
                    gpsimd.wait_ge(sems["vE"], qev)
                    t0 = (g - 4) * 4
                    for dc in range(8):
                        src = AS[:, dc * 512:(dc + 1) * 512].rearrange("p (t f) -> p t f", f=128)
                        for hh in range(2):
                            gpsimd.dma_start(
                                out=qT_sp[2 * dc + hh, t0:t0 + 4, :, :]
                                .rearrange("t q f -> q t f"),
                                in_=src[hh * 64:(hh + 1) * 64, :, :],
                            ).then_inc(sems["spl"], 16)
                emit("gpsimd", f_g4)
                qspl_done = bump("spl", 16 * 16)
                prev = {"vE": qev_done, "spl": qspl_done, "tr1": tr1_done,
                        "xnspl": xnspl_done}
            else:
                prev = {"vE": vev_done, "spl": vspl_done, "tr1": tr1_done,
                        "xnspl": xnspl_done}

        ph1_spl = cnt["spl"]

        if DEBUG:
            def f_dbg2(gpsimd, need=ph1_spl):
                gpsimd.wait_ge(sems["spl"], need)
                gpsimd.dma_start(out=ap["dbg_qT_h3"], in_=qT_sp[3, 2, :, :]).then_inc(sems["stf"], 16)
                gpsimd.dma_start(out=ap["dbg_kT_h5"], in_=kT_sp[5, 33, :, :]).then_inc(sems["stf"], 16)
                gpsimd.dma_start(out=ap["dbg_v_t40"], in_=v_sp[40, :, 256:512]).then_inc(sems["stf"], 16)
            emit("gpsimd", f_dbg2)
            bump("stf", 48)


        ph1_vE = prev["vE"]
        ph1_spl = cnt["spl"]

        wchunks = ([("wgT", i, 4096) for i in range(8)] +
                   [("wuT", i, 4096) for i in range(8)] +
                   [("wdT", i, 1024) for i in range(32)])

        def f2_lda(sync, pv=ph1_vE):
            sync.wait_ge(sems["vE"], pv)
            for idx, (nm, i, w) in enumerate(wchunks):
                buf = AW[:, (idx % 2) * 4096:(idx % 2) * 4096 + w]
                if idx >= 2:
                    sync.wait_ge(sems["vE"], ph1_vE + idx - 1)
                sync.dma_start(out=buf, in_=ap[nm][i * 128:(i + 1) * 128, :]).then_inc(sems["ldw"], 16)
        emit("sync", f2_lda)
        ldw_passA0 = cnt["ldw"]
        bump("ldw", 16 * len(wchunks))

        def f2_va(vector, base=ldw_passA0, pv=ph1_vE):
            vector.memset(wsc[:, 8:11], 0.0)
            vector.tensor_copy(padb[:, :], ident[:, :])
            for idx, (nm, i, w) in enumerate(wchunks):
                vector.wait_ge(sems["ldw"], base + 16 * (idx + 1))
                buf = AW[:, (idx % 2) * 4096:(idx % 2) * 4096 + w]
                col = 8 + ("wgT", "wuT", "wdT").index(nm)
                vector.tensor_reduce(stat[:, 40:41], buf, AX.X, ALU.add,
                                     apply_absolute_value=True)
                vector.tensor_copy(padb[:, :], ident[:, :])
                ins = vector.tensor_tensor(wsc[:, col:col + 1], wsc[:, col:col + 1],
                                           stat[:, 40:41], ALU.add)
                ins.then_inc(sems["vE"], 1)
        emit("vector", f2_va)
        va_dones = [ph1_vE + i + 1 for i in range(len(wchunks))]
        cnt["vE"] = va_dones[-1]
        va_all = cnt["vE"]

        def f2_tsum(tensor, va=va_all):
            tensor.wait_ge(sems["vE"], va)
            ins = tensor.matmul(psum[0][0:1, 0:3], wsc[:, 15:16], wsc[:, 8:11],
                                start=True, stop=True)
            ins.then_inc(sems["tE"], 1)
        emit("tensor", f2_tsum)
        tsum_done = bump("tE")

        def f2_vsc(vector, ts=tsum_done):
            vector.wait_ge(sems["tE"], ts)
            vector.tensor_copy(stat[0:1, 40:43], psum[0][0:1, 0:3])
            vector.tensor_copy(padb[:, :], ident[:, :])
            vector.tensor_scalar(stat[0:1, 44:47], stat[0:1, 40:43],
                                 1.0 / (D * INNER0), None, ALU.mult)
            vector.tensor_copy(padb[:, :], ident[:, :])
            ins = vector.tensor_scalar(stat[0:1, 48:51], stat[0:1, 44:47], 1e-5,
                                       None, ALU.max)
            ins.then_inc(sems["vE"], 1)
        emit("vector", f2_vsc)
        vsc_done = bump("vE")

        def f2_vones(vector, vd=vsc_done):
            ins = vector.memset(padb2[0:1, :], 1.0)
            ins.then_inc(sems["vE"], 1)
        emit("vector", f2_vones)
        vones_done = bump("vE")

        def f2_gbc(tensor, vd=vsc_done, vo=vones_done):
            tensor.wait_ge(sems["vE"], vd)
            tensor.wait_ge(sems["vE"], vo)
            ins = tensor.matmul(psum[1][:, 0:3], padb2[0:1, :], stat[0:1, 48:51],
                                start=True, stop=True)
            ins.then_inc(sems["tE"], 1)
        emit("tensor", f2_gbc)
        gbc_done = bump("tE")

        def f2_vth(vector, gb=gbc_done):
            vector.wait_ge(sems["tE"], gb)
            vector.tensor_copy(wsc[:, 0:3], psum[1][:, 0:3])
            vector.tensor_copy(padb2[:, :], ident[:, :])
            vector.tensor_scalar(wsc[:, 3:6], wsc[:, 0:3], 0.5, None, ALU.mult)
            vector.tensor_copy(padb[:, :], ident[:, :])
            ins = vector.tensor_scalar(wsc[:, 6:9], wsc[:, 3:6], -1.0, None, ALU.mult)
            ins.then_inc(sems["vE"], 1)
        emit("vector", f2_vth)
        vth_done = bump("vE")

        def f2_ldb(sync, vth=vth_done, base2=cnt["ldw"]):
            sync.wait_ge(sems["vE"], vth)
            for idx, (nm, i, w) in enumerate(wchunks):
                buf = AW[:, (idx % 2) * 4096:(idx % 2) * 4096 + w]
                if idx >= 2:
                    sync.wait_ge(sems["vE"], vth + idx - 1)
                sync.dma_start(out=buf, in_=ap[nm][i * 128:(i + 1) * 128, :]).then_inc(sems["ldw"], 16)
        emit("sync", f2_ldb)
        ldw_passB0 = cnt["ldw"]
        bump("ldw", 16 * len(wchunks))

        def f2_vb(vector, base=ldw_passB0, vth=vth_done):
            for idx, (nm, i, w) in enumerate(wchunks):
                vector.wait_ge(sems["ldw"], base + 16 * (idx + 1))
                if idx >= 2:
                    vector.wait_ge(sems["spl"], cnt_spl_b0 + 16 * (idx - 1))
                buf = AW[:, (idx % 2) * 4096:(idx % 2) * 4096 + w]
                m = ("wgT", "wuT", "wdT").index(nm)
                c1 = AW[:, 8192:8192 + w]
                c2 = AW[:, 12288:12288 + w]
                vector.tensor_scalar(c1, buf, wsc[:, 3 + m:4 + m], None, ALU.is_ge)
                vector.tensor_scalar(c2, buf, wsc[:, 6 + m:7 + m], None, ALU.is_le)
                qb = ABF[:, (idx % 2) * 4096:(idx % 2) * 4096 + w]
                ins = vector.tensor_tensor(qb, c1, c2, ALU.subtract)
                ins.then_inc(sems["vE"], 1)
        emit("vector", f2_vb)
        vb0 = cnt["vE"]
        bump("vE", len(wchunks))

        cnt_spl_b0 = cnt["spl"]

        def f2_gb(gpsimd, vb0=vb0):
            for idx, (nm, i, w) in enumerate(wchunks):
                gpsimd.wait_ge(sems["vE"], vb0 + idx + 1)
                qb = ABF[:, (idx % 2) * 4096:(idx % 2) * 4096 + w]
                m = ("wgT", "wuT", "wdT").index(nm)
                tgt = (wgq_sp, wuq_sp, wdq_sp)[m]
                gpsimd.dma_start(out=tgt[i if m < 2 else i, :, :], in_=qb).then_inc(sems["spl"], 16)
        emit("gpsimd", f2_gb)
        bump("spl", 16 * len(wchunks))
        wq_spl_all = cnt["spl"]

        if DEBUG:
            def f2_dbgw(gpsimd, need=wq_spl_all):
                gpsimd.wait_ge(sems["spl"], need)
                gpsimd.dma_start(out=ap["dbg_wgq_c2"], in_=wgq_sp[2, :, 1024:1536]).then_inc(sems["spl"], 16)
            emit("gpsimd", f2_dbgw)
            bump("spl", 16)

        prev2 = {"vE": cnt["vE"], "spl": cnt["spl"], "sE": cnt["sE"], "tE": cnt["tE"]}
        for gg in range(4):
            def f2_rs(vector, gg=gg, pv=prev2["vE"], wq=wq_spl_all):
                vector.wait_ge(sems["spl"], wq)
                vector.wait_ge(sems["vE"], pv)
                vector.tensor_tensor(stat[:, 32:36], rsx[:, gg * 4:gg * 4 + 4],
                                     wsc[:, 0:1].broadcast_to([128, 4]), ALU.mult)
                vector.tensor_copy(padb[:, :], ident[:, :])
                ins = vector.tensor_tensor(stat[:, 36:40], rsx[:, gg * 4:gg * 4 + 4],
                                           wsc[:, 1:2].broadcast_to([128, 4]), ALU.mult)
                ins.then_inc(sems["vE"], 1)
            emit("vector", f2_rs)
            rs_done = bump("vE")

            def f2_ldnx(sync, gg=gg, pv=prev2["vE"], ptE=prev2["tE"]):
                sync.wait_ge(sems["tE"], ptE)
                sync.dma_start(
                    out=nxtg[:, :].rearrange("p (kc f) -> p kc f", f=512),
                    in_=nxt_sp[:, :, gg * 512:(gg + 1) * 512]
                    .rearrange("kc p f -> p kc f"),
                ).then_inc(sems["ldw"], 16)
            emit("sync", f2_ldnx)
            nx_ld = bump("ldw", 16)

            sil_dones = []
            prev_sil = rs_done
            for icb in range(8):
                def f2_ldw(sync, gg=gg, icb=icb, ps=prev_sil, ptE=prev2["tE"]):
                    sync.wait_ge(sems["vE"], ps)
                    if icb == 0:
                        sync.wait_ge(sems["tE"], ptE)
                    sync.dma_start(out=ABF[:, 0:4096].rearrange("p (kc w) -> p kc w", w=512),
                                   in_=wgq_sp[:, :, icb * 512:(icb + 1) * 512]
                                   .rearrange("kc p w -> p kc w")).then_inc(sems["ldw"], 16)
                    sync.dma_start(out=ABF[:, 4096:8192].rearrange("p (kc w) -> p kc w", w=512),
                                   in_=wuq_sp[:, :, icb * 512:(icb + 1) * 512]
                                   .rearrange("kc p w -> p kc w")).then_inc(sems["ldw"], 16)
                emit("sync", f2_ldw)
                ld_need = bump("ldw", 32)

                def f2_mm(tensor, gg=gg, icb=icb, ld=ld_need, ps=prev_sil,
                          nxl=nx_ld):
                    tensor.wait_ge(sems["ldw"], ld)
                    tensor.wait_ge(sems["ldw"], nxl)
                    tensor.wait_ge(sems["vE"], ps)
                    ins = None
                    for t4 in range(4):
                        for kc in range(8):
                            lh = nxtg[:, kc * 512 + t4 * 128: kc * 512 + t4 * 128 + 128]
                            ins = tensor.matmul(psum[t4][:, :], lh,
                                                ABF[:, kc * 512:(kc + 1) * 512],
                                                start=(kc == 0), stop=(kc == 7))
                        for kc in range(8):
                            lh = nxtg[:, kc * 512 + t4 * 128: kc * 512 + t4 * 128 + 128]
                            ins = tensor.matmul(psum[4 + t4][:, :], lh,
                                                ABF[:, 4096 + kc * 512: 4096 + (kc + 1) * 512],
                                                start=(kc == 0), stop=(kc == 7))
                    ins.then_inc(sems["tE"], 1)
                emit("tensor", f2_mm)
                mm_done = bump("tE")

                def f2_sil(scalar, gg=gg, icb=icb, mm=mm_done, pspl=prev2["spl"]):
                    scalar.wait_ge(sems["tE"], mm)
                    if icb == 0:
                        scalar.wait_ge(sems["spl"], pspl)
                    ins = None
                    for t4 in range(4):
                        ins = scalar.activation(AS[:, t4 * 512:(t4 + 1) * 512],
                                                psum[t4][:, :], AF.Silu,
                                                scale=stat[:, 32 + t4:33 + t4])
                    ins.then_inc(sems["sE"], 1)
                emit("scalar", f2_sil)
                sil_done = bump("sE")

                def f2_h(vector, gg=gg, icb=icb, sd=sil_done):
                    vector.wait_ge(sems["sE"], sd)
                    ins = None
                    for t4 in range(4):
                        ins = vector.scalar_tensor_tensor(
                            AW[:, t4 * 4096 + icb * 512: t4 * 4096 + (icb + 1) * 512],
                            psum[4 + t4][:, :], stat[:, 36 + t4:37 + t4],
                            AS[:, t4 * 512:(t4 + 1) * 512], ALU.mult, ALU.mult)
                    ins.then_inc(sems["vE"], 1)
                emit("vector", f2_h)
                prev_sil = bump("vE")
                sil_dones.append(prev_sil)
            h_all = prev_sil

            dn_prev_sE = None
            prev_trev = None
            for t4 in range(4):
                def f2_hq1(vector, gg=gg, t4=t4, ha=h_all):
                    if t4 == 0:
                        vector.wait_ge(sems["vE"], ha)
                    ht = AW[:, t4 * 4096:(t4 + 1) * 4096]
                    vector.tensor_reduce(stat[:, 44:45], ht, AX.X, ALU.max,
                                         apply_absolute_value=True)
                    vector.tensor_copy(padb[:, :], ident[:, :])
                    ins = vector.tensor_scalar(stat[:, 45:46], stat[:, 44:45], 1e-5,
                                               None, ALU.max)
                    ins.then_inc(sems["vE"], 1)
                emit("vector", f2_hq1)
                hq1 = bump("vE")

                def f2_hq2(scalar, v1=hq1):
                    scalar.wait_ge(sems["vE"], v1)
                    scalar.activation(stat[:, 46:47], stat[:, 45:46], AF.Ln)
                    scalar.copy(padb2[:, :], ident[:, :])
                    ins = scalar.activation(stat[:, 47:48], stat[:, 46:47], AF.Exp,
                                            scale=-1.0)
                    ins.then_inc(sems["sE"], 1)
                emit("scalar", f2_hq2)
                hq2 = bump("sE")

                def f2_hq3(vector, gg=gg, t4=t4, s2=hq2):
                    vector.wait_ge(sems["sE"], s2)
                    vector.tensor_scalar(stat[:, 48:49], stat[:, 47:48], 127.0, None, ALU.mult)
                    vector.tensor_copy(padb[:, :], ident[:, :])
                    vector.tensor_scalar(rsh[:, gg * 4 + t4:gg * 4 + t4 + 1],
                                         stat[:, 45:46], 1.0 / 127.0, None, ALU.mult)
                    ht = AW[:, t4 * 4096:(t4 + 1) * 4096]
                    ins = None
                    for q in range(4):
                        hq = ht[:, q * 1024:(q + 1) * 1024]
                        tbuf = AACT[:, 0:1024]
                        kf = AACT[:, 1024:2048]
                        dl = AACT[:, 2048 + q * 1024: 3072 + q * 1024]
                        c1 = AACT[:, 6144:7168]
                        c2 = AACT[:, 7168:8192]
                        vector.tensor_scalar(tbuf, hq, stat[:, 48:49], None, ALU.mult)
                        vector.tensor_copy(AI[:, 0:1024], tbuf)
                        vector.tensor_copy(kf, AI[:, 0:1024])
                        vector.tensor_tensor(dl, tbuf, kf, ALU.subtract)
                        vector.tensor_scalar(c1, dl, 0.5, None, ALU.is_gt)
                        vector.tensor_scalar(c2, dl, -0.5, None, ALU.is_lt)
                        vector.tensor_tensor(c1, c1, c2, ALU.subtract)
                        ins = vector.tensor_tensor(dl, kf, c1, ALU.add)
                    ins.then_inc(sems["vE"], 1)
                emit("vector", f2_hq3)
                hq3 = bump("vE")

                def f2_tr(tensor, t4=t4, hq=hq3, psE=dn_prev_sE, ptv=prev_trev):
                    tensor.wait_ge(sems["vE"], hq)
                    if psE is not None:
                        tensor.wait_ge(sems["sE"], psE)
                    if ptv is not None:
                        tensor.wait_ge(sems["vE"], ptv)
                    ins = None
                    for kc in range(32):
                        ins = tensor.transpose(
                            psum[kc // 4][:, (kc % 4) * 128:(kc % 4) * 128 + 128],
                            AACT[:, 2048 + kc * 128: 2048 + kc * 128 + 128],
                            ident[:, :])
                    ins.then_inc(sems["tE"], 1)
                emit("tensor", f2_tr)
                tr_done = bump("tE")

                def f2_trev(vector, t4=t4, tr=tr_done):
                    vector.wait_ge(sems["tE"], tr)
                    ins = None
                    for bank in range(8):
                        ins = vector.tensor_copy(
                            ABF[:, (t4 % 2) * 4096 + bank * 512:
                                (t4 % 2) * 4096 + (bank + 1) * 512],
                            psum[bank][:, :])
                    ins.then_inc(sems["vE"], 1)
                emit("vector", f2_trev)
                trev_done = bump("vE")
                prev_trev = trev_done

                if t4 % 2 == 1:
                    pr = t4 // 2
                    dn_base = cnt["tE"]
                    for kc in range(32):
                        def f2_lddn(sync, kc=kc, tv=trev_done, base=dn_base):
                            if kc == 0:
                                sync.wait_ge(sems["vE"], tv)
                            if kc >= 2:
                                sync.wait_ge(sems["tE"], base + kc - 1)
                            sync.dma_start(
                                out=ABF[:, 8192 + (kc % 2) * 1024:
                                        8192 + (kc % 2) * 1024 + 1024],
                                in_=wdq_sp[kc, :, :]).then_inc(sems["ldw"], 16)
                        emit("sync", f2_lddn)
                        ld_dn = bump("ldw", 16)

                        def f2_dnmm(tensor, kc=kc, ld=ld_dn, tv=trev_done,
                                    psE=dn_prev_sE):
                            tensor.wait_ge(sems["ldw"], ld)
                            if kc == 0:
                                tensor.wait_ge(sems["vE"], tv)
                                if psE is not None:
                                    tensor.wait_ge(sems["sE"], psE)
                            ins = None
                            for tt in range(2):
                                lh = ABF[:, tt * 4096 + kc * 128: tt * 4096 + kc * 128 + 128]
                                for dh in range(2):
                                    ins = tensor.matmul(
                                        psum[tt * 2 + dh][:, :], lh,
                                        ABF[:, 8192 + (kc % 2) * 1024 + dh * 512:
                                            8192 + (kc % 2) * 1024 + (dh + 1) * 512],
                                        start=(kc == 0), stop=(kc == 31))
                            ins.then_inc(sems["tE"], 1)
                        emit("tensor", f2_dnmm)
                        bump("tE")
                    dn_done = cnt["tE"]

                    def f2_rsd(vector, gg=gg, pr=pr, psE=dn_prev_sE):
                        if psE is not None:
                            vector.wait_ge(sems["sE"], psE)
                        vector.tensor_tensor(stat[:, 40:42],
                                             rsh[:, gg * 4 + pr * 2: gg * 4 + pr * 2 + 2],
                                             wsc[:, 2:3].broadcast_to([128, 2]), ALU.mult)
                        ins = vector.tensor_copy(padb[:, :], ident[:, :])
                        ins.then_inc(sems["vE"], 1)
                    emit("vector", f2_rsd)
                    rsd_done = bump("vE")

                    def f2_gsil(scalar, dn=dn_done, rsd=rsd_done):
                        scalar.wait_ge(sems["tE"], dn)
                        scalar.wait_ge(sems["vE"], rsd)
                        ins = None
                        for tt in range(2):
                            for dh in range(2):
                                ins = scalar.activation(
                                    AS[:, (tt * 2 + dh) * 512:(tt * 2 + dh) * 512 + 512],
                                    psum[tt * 2 + dh][:, :], AF.Silu,
                                    scale=stat[:, 40 + tt:41 + tt])
                        ins.then_inc(sems["sE"], 1)
                    emit("scalar", f2_gsil)
                    gsil_done = bump("sE")
                    dn_prev_sE = gsil_done

                    def f2_gsp(gpsimd, gg=gg, pr=pr, gs=gsil_done):
                        gpsimd.wait_ge(sems["sE"], gs)
                        for tt in range(2):
                            t_abs = gg * 4 + pr * 2 + tt
                            gpsimd.dma_start(out=g_sp[t_abs, :, :],
                                             in_=AS[:, tt * 1024: tt * 1024 + 1024]
                                             ).then_inc(sems["spl"], 16)
                    emit("gpsimd", f2_gsp)
                    bump("spl", 32)

            prev2 = {"vE": cnt["vE"], "spl": cnt["spl"], "sE": cnt["sE"],
                     "tE": cnt["tE"]}

        ph2_spl = cnt["spl"]
        if DEBUG:
            def f2_dbgg(gpsimd, need=ph2_spl):
                gpsimd.wait_ge(sems["spl"], need)
                gpsimd.dma_start(out=ap["dbg_g_t1"], in_=g_sp[1, :, :]).then_inc(sems["spl"], 16)
            emit("gpsimd", f2_dbgg)
            bump("spl", 16)



        def emit_fp4_quant(src_ap, nblk, Wd, xq_ap, sq_wait=None, name="",
                           accb=None, ub=None, sgb=None, tmb=None):
            accb = accb or (AACT, 0)
            ub = ub or (AACT, 4096)
            sgb = sgb or (AACT, 8192)
            tmb = tmb or (AW, 12288)
            """emits vector+scalar ops; returns vE value when hi/lo written."""
            def q_s1(scalar, src=src_ap, w=sq_wait):
                if w is not None:
                    scalar.wait_ge(sems[w[0]], w[1])
                ins = scalar.activation(accb[0][:, accb[1]:accb[1] + Wd], src, AF.Abs)
                ins.then_inc(sems["sE"], 1)
            emit("scalar", q_s1)
            abs_done = bump("sE")

            def q_v1(vector, nblk=nblk, Wd=Wd, ad=abs_done, accb=accb):
                vector.wait_ge(sems["sE"], ad)
                v3 = accb[0][:, accb[1]:accb[1] + Wd].rearrange("p (a b) -> p a b", b=16)
                vector.tensor_reduce(blkq[:, 0:nblk], v3, AX.X, ALU.max)
                vector.tensor_copy(padb[:, :], ident[:, :])
                ins = vector.tensor_scalar(blkq[:, 0:nblk], blkq[:, 0:nblk],
                                           1.0 / 6.0, 1e-12, ALU.mult, ALU.max)
                ins.then_inc(sems["vE"], 1)
            emit("vector", q_v1)
            bm_done = bump("vE")

            def q_s2(scalar, nblk=nblk, bd=bm_done):
                scalar.wait_ge(sems["vE"], bd)
                scalar.activation(blkq[:, 192:192 + nblk], blkq[:, 0:nblk], AF.Ln)
                scalar.copy(padb2[:, :], ident[:, :])
                ins = scalar.activation(blkq[:, 384:384 + nblk], blkq[:, 192:192 + nblk],
                                        AF.Exp, scale=-1.0)
                ins.then_inc(sems["sE"], 1)
            emit("scalar", q_s2)
            inv_done = bump("sE")

            def q_s3(scalar, src=src_ap, Wd=Wd, sgb=sgb):
                ins = scalar.activation(sgb[0][:, sgb[1]:sgb[1] + Wd], src, AF.Sign)
                ins.then_inc(sems["sE"], 1)
            emit("scalar", q_s3)
            sgn_done = bump("sE")

            def q_v2(vector, nblk=nblk, Wd=Wd, iv=inv_done, sg=sgn_done,
                     xq=xq_ap, accb=accb, ub=ub, sgb=sgb, tmb=tmb):
                vector.wait_ge(sems["sE"], iv)
                a3 = accb[0][:, accb[1]:accb[1] + Wd].rearrange("p (a b) -> p a b", b=16)
                u3 = ub[0][:, ub[1]:ub[1] + Wd].rearrange("p (a b) -> p a b", b=16)
                iv3 = blkq[:, 384:384 + nblk].rearrange("p (a b) -> p a b", b=1)
                vector.tensor_tensor(u3, a3, iv3.broadcast_to([128, nblk, 16]), ALU.mult)
                uu = ub[0][:, ub[1]:ub[1] + Wd]
                acc = accb[0][:, accb[1]:accb[1] + Wd]
                tmp = tmb[0][:, tmb[1]:tmb[1] + Wd]
                vector.tensor_scalar(acc, uu, 0.25, None, ALU.is_gt)
                vector.tensor_scalar(tmp, uu, 0.75, None, ALU.is_gt)
                vector.tensor_tensor(acc, acc, tmp, ALU.add)
                vector.tensor_scalar(tmp, uu, 1.25, None, ALU.is_gt)
                vector.tensor_tensor(acc, acc, tmp, ALU.add)
                vector.tensor_scalar(tmp, uu, 1.75, None, ALU.is_gt)
                vector.tensor_tensor(acc, acc, tmp, ALU.add)
                vector.tensor_scalar(acc, acc, 0.5, None, ALU.mult)
                vector.tensor_scalar(tmp, uu, 2.5, None, ALU.is_gt)
                vector.tensor_tensor(acc, acc, tmp, ALU.add)
                vector.tensor_scalar(tmp, uu, 3.5, None, ALU.is_gt)
                vector.tensor_tensor(acc, acc, tmp, ALU.add)
                vector.tensor_scalar(tmp, uu, 5.0, 2.0, ALU.is_gt, ALU.mult)
                vector.tensor_tensor(acc, acc, tmp, ALU.add)
                sc3 = blkq[:, 0:nblk].rearrange("p (a b) -> p a b", b=1)
                a3b = acc.rearrange("p (a b) -> p a b", b=16)
                vector.tensor_tensor(a3b, a3b, sc3.broadcast_to([128, nblk, 16]), ALU.mult)
                vector.wait_ge(sems["sE"], sg)
                ins = vector.tensor_tensor(xq, acc, sgb[0][:, sgb[1]:sgb[1] + Wd], ALU.mult)
                ins.then_inc(sems["vE"], 1)
            emit("vector", q_v2)
            return bump("vE")


        ph2_end = {"vE": cnt["vE"], "spl": cnt["spl"], "sE": cnt["sE"],
                   "tE": cnt["tE"]}

        def f3_ldb(sync):
            sync.dma_start(out=btab[:, :], in_=ap["attn_bias"]).then_inc(sems["ldw"], 16)
            sync.dma_start(out=mtab[:, :], in_=ap["attn_mask"]).then_inc(sems["ldw"], 16)
            sync.dma_start(out=ften[:, :], in_=ap["attn_fden"]).then_inc(sems["ldw"], 16)
        emit("sync", f3_ldb)
        btab_ld = bump("ldw", 48)

        def pbuf(slot):
            return AS[:, 4096 + slot * 512: 4096 + slot * 512 + 512]

        def mtab_slice(W, d):
            if W == 512:
                c0 = d * 512
            elif W == 256:
                c0 = 2048 + d * 256
            else:
                c0 = 2560 + d * 128
            return mtab[:, c0:c0 + W]

        W1JOB = [(0, 0, 1024, 0), (1, 0, 1024, 1), (2, 0, 1024, 2),
                 (3, 0, 1024, 3), (4, 0, 1024, 4), (5, 0, 1024, 5),
                 (6, 0, 1024, 6), (6, 1024, 1024, 6), (6, 2048, 768, 6)]
        w1_prev = {"qv": 0, "spl": 0}

        def emit_w1_job(jidx):
            ci, c0, Wd, slot = W1JOB[jidx]
            nblk = Wd // 16

            def w1_ldc(sync, ci=ci, c0=c0, Wd=Wd, pv=w1_prev["qv"], ps=w1_prev["spl"],
                       p2t=ph2_end["tE"], p2s=ph2_end["sE"], p2v=ph2_end["vE"],
                       first=(jidx == 0)):
                if first:
                    sync.wait_ge(sems["tE"], p2t)
                    sync.wait_ge(sems["sE"], p2s)
                    sync.wait_ge(sems["vE"], p2v)
                if pv:
                    sync.wait_ge(sems["vE"], pv)
                if ps:
                    sync.wait_ge(sems["spl"], ps)
                sync.dma_start(out=AW[:, 8192:8192 + Wd],
                               in_=ap["w1shard"][ci, :, c0:c0 + Wd]).then_inc(sems["ldw"], 16)
            emit("sync", w1_ldc)
            ldc = bump("ldw", 16)

            qv = emit_fp4_quant(AW[:, 8192:8192 + Wd], nblk, Wd, AW[:, 9216:9216 + Wd],
                                sq_wait=("ldw", ldc),
                                accb=(AW, 10240), ub=(AW, 11264),
                                sgb=(AW, 13312), tmb=(AW, 14336))
            w1_prev["qv"] = qv

            def w1_spc(gpsimd, c0=c0, Wd=Wd, slot=slot, qv=qv):
                gpsimd.wait_ge(sems["vE"], qv)
                base = slot * 128 * 1024
                sw = 2816 if slot == 6 else 1024
                dst = w1agi[base:base + 128 * sw].rearrange("(p f) -> p f", f=sw)[:, c0:c0 + Wd]
                gpsimd.dma_start(out=dst, in_=AW[:, 9216:9216 + Wd]).then_inc(sems["spl"], 16)
            emit("gpsimd", w1_spc)
            w1_prev["spl"] = bump("spl", 16)

        bank_free = {0: 0, 1: 0, 2: 0, 3: 0}
        pslot_free = {0: 0, 1: 0}
        obank_free = 0
        rot4 = [0]
        rot2 = [0]
        prev_hg_ldw = 0

        for hg in range(4):
            def f3_ldk(sync, hg=hg, p2t=ph2_end["tE"], p2s=ph2_end["sE"],
                       ptE=cnt["tE"]):
                if hg == 0:
                    sync.wait_ge(sems["tE"], p2t)
                    sync.wait_ge(sems["sE"], p2s)
                else:
                    sync.wait_ge(sems["tE"], ptE)
                for pair in range(2):
                    for sub in range(2):
                        h = hg * 4 + pair * 2 + sub
                        sync.dma_start(
                            out=AW[sub * 64:(sub + 1) * 64,
                                   pair * 4096: pair * 4096 + 4096]
                            .rearrange("p (t f) -> p t f", f=128),
                            in_=kT_sp[h, V0:NV, :, :].rearrange("t q f -> q t f"),
                        ).then_inc(sems["ldw"], 16)
                        sync.dma_start(
                            out=AS[sub * 64:(sub + 1) * 64,
                                   pair * 2048: pair * 2048 + 2048]
                            .rearrange("p (t f) -> p t f", f=128),
                            in_=qT_sp[h, :, :, :].rearrange("t q f -> q t f"),
                        ).then_inc(sems["ldw"], 16)
                for hs in range(4):
                    sync.dma_start(
                        out=AACT[:, 0:32 * 260].rearrange("p (vt c) -> p vt c", c=260)
                        [:, :, hs * 65: hs * 65 + 64],
                        in_=v_sp[V0:NV, :, hg * 256 + hs * 64: hg * 256 + hs * 64 + 64]
                        .rearrange("vt p f -> p vt f"),
                    ).then_inc(sems["ldw"], 16)
            emit("sync", f3_ldk)
            ld_k = bump("ldw", 16 * 12)

            def f3_ones(vector, ld=ld_k):
                vector.wait_ge(sems["ldw"], ld)
                ins = vector.memset(
                    AACT[:, 0:32 * 260].rearrange("p (vt c) -> p vt c", c=260)
                    .rearrange("p vt (hs f) -> p vt hs f", f=65)[:, :, :, 64:65],
                    1.0)
                ins.then_inc(sems["vE"], 1)
            emit("vector", f3_ones)
            ones_done = bump("vE")

            for (h, qc, vjs) in [t for t in ATT_SCHED if t[0] // 4 == hg]:
                W = CHUNKW[h]
                nts = W // 128
                pair, sub = (h % 4) // 2, h % 2
                hs = h % 4
                qs_lo = qc * nts
                av_last = 0
                my_obank_free = obank_free

                for j, vj in enumerate(vjs):
                    bank = rot4[0] % 4
                    rot4[0] += 1
                    pslot = rot2[0] % 2
                    rot2[0] += 1

                    def f3_st(tensor, h=h, vj=vj, bank=bank, pair=pair, sub=sub,
                              W=W, qs_lo=qs_lo, ld=ld_k, bf=bank_free[bank], j=j,
                              od=ones_done):
                        if j == 0:
                            tensor.wait_ge(sems["ldw"], ld)
                            tensor.wait_ge(sems["vE"], od)
                        if bf:
                            tensor.wait_ge(sems["sE"], bf)
                        lh = AW[sub * 64:(sub + 1) * 64,
                                pair * 4096 + (vj - V0) * 128:
                                pair * 4096 + (vj - V0) * 128 + 128]
                        rh = AS[sub * 64:(sub + 1) * 64,
                                pair * 2048 + qs_lo * 128:
                                pair * 2048 + qs_lo * 128 + W]
                        ins = tensor.matmul(psum[bank][:, 0:W], lh, rh,
                                            start=True, stop=True)
                        ins.then_inc(sems["tE"], 1)
                    emit("tensor", f3_st)
                    st_done = bump("tE")

                    def f3_exp(scalar, bank=bank, pslot=pslot, W=W, st=st_done,
                               col=ATT_BIDX[(h, qc, vj)], pf=pslot_free[pslot]):
                        scalar.wait_ge(sems["tE"], st)
                        if pf:
                            scalar.wait_ge(sems["tE"], pf)
                        ins = scalar.activation(pbuf(pslot)[:, 0:W], psum[bank][:, 0:W],
                                                AF.Exp, bias=btab[:, col:col + 1],
                                                scale=0.125)
                        ins.then_inc(sems["sE"], 1)
                    emit("scalar", f3_exp)
                    exp_done = bump("sE")
                    bank_free[bank] = exp_done

                    d = (vj - 32) - qs_lo
                    av_wait_kind = "sE"
                    av_wait_val = exp_done
                    if d >= 0:
                        def f3_mask(vector, pslot=pslot, W=W, d=d, ed=exp_done):
                            vector.wait_ge(sems["sE"], ed)
                            ins = vector.tensor_tensor(pbuf(pslot)[:, 0:W],
                                                       pbuf(pslot)[:, 0:W],
                                                       mtab_slice(W, d), ALU.mult)
                            ins.then_inc(sems["vE"], 1)
                        emit("vector", f3_mask)
                        av_wait_kind = "vE"
                        av_wait_val = bump("vE")

                    def f3_av(tensor, h=h, vj=vj, pslot=pslot, W=W, nts=nts, hs=hs,
                              j=j, last=(j == len(vjs) - 1), wk=av_wait_kind,
                              wv=av_wait_val, ob=my_obank_free):
                        tensor.wait_ge(sems[wk], wv)
                        if j == 0 and ob:
                            tensor.wait_ge(sems["vE"], ob)
                        ins = None
                        for tt in range(nts):
                            lh = pbuf(pslot)[:, tt * 128:(tt + 1) * 128]
                            vt = vj - V0
                            rhv = AACT[:, vt * 260 + hs * 65: vt * 260 + hs * 65 + 65]
                            ins = tensor.matmul(psum[4 + tt][:, 0:65], lh, rhv,
                                                start=(j == 0), stop=last)
                        ins.then_inc(sems["tE"], 1)
                    emit("tensor", f3_av)
                    av_last = bump("tE")
                    pslot_free[pslot] = av_last

                def f3_den_v(vector, nts=nts, av=av_last, h=h, qc=qc):
                    vector.wait_ge(sems["tE"], av)
                    for tt in range(nts):
                        vector.tensor_copy(stat[:, 52 + tt:53 + tt],
                                           psum[4 + tt][:, 64:65])
                    vector.tensor_copy(padb[:, :], ident[:, :])
                    fc = h * 16 + qc * nts
                    ins = vector.tensor_tensor(stat[:, 52:52 + nts],
                                               stat[:, 52:52 + nts],
                                               ften[:, fc:fc + nts], ALU.mult)
                    ins.then_inc(sems["vE"], 1)
                emit("vector", f3_den_v)
                den_v = bump("vE")

                def f3_den_s(scalar, nts=nts, dv=den_v):
                    scalar.wait_ge(sems["vE"], dv)
                    scalar.activation(stat[:, 56:56 + nts], stat[:, 52:52 + nts], AF.Ln)
                    scalar.copy(padb2[:, :], ident[:, :])
                    ins = scalar.activation(stat[:, 60:60 + nts], stat[:, 56:56 + nts],
                                            AF.Exp, scale=-1.0)
                    ins.then_inc(sems["sE"], 1)
                emit("scalar", f3_den_s)
                den_s = bump("sE")

                def f3_den_n(vector, nts=nts, ds=den_s, h=h, qc=qc):
                    vector.wait_ge(sems["sE"], ds)
                    vector.tensor_tensor(stat[:, 44:44 + nts], stat[:, 52:52 + nts],
                                         stat[:, 60:60 + nts], ALU.mult)
                    vector.tensor_copy(padb[:, :], ident[:, :])
                    vector.tensor_scalar(stat[:, 48:48 + nts], stat[:, 44:44 + nts],
                                         -1.0, 2.0, ALU.mult, ALU.add)
                    vector.tensor_copy(padb[:, :], ident[:, :])
                    vector.tensor_tensor(stat[:, 60:60 + nts], stat[:, 60:60 + nts],
                                          stat[:, 48:48 + nts], ALU.mult)
                    vector.tensor_copy(padb[:, :], ident[:, :])
                    fc = h * 16 + qc * nts
                    ins = vector.tensor_tensor(stat[:, 60:60 + nts],
                                               stat[:, 60:60 + nts],
                                               ften[:, fc:fc + nts], ALU.mult)
                    ins.then_inc(sems["vE"], 1)
                emit("vector", f3_den_n)
                den_n = bump("vE")

                def f3_oev(vector, nts=nts, ds=den_n):
                    vector.wait_ge(sems["vE"], ds)
                    ins = None
                    for tt in range(nts):
                        ins = vector.tensor_scalar(aev[:, tt * 64:(tt + 1) * 64],
                                                   psum[4 + tt][:, 0:64],
                                                   stat[:, 60 + tt:61 + tt], None,
                                                   ALU.mult)
                    ins.then_inc(sems["vE"], 1)
                emit("vector", f3_oev)
                obank_free = bump("vE")

                def f3_osp(gpsimd, h=h, qs_lo=qs_lo, nts=nts, oe=obank_free):
                    gpsimd.wait_ge(sems["vE"], oe)
                    gpsimd.dma_start(
                        out=ao_sp[qs_lo:qs_lo + nts, :, h * 64:(h + 1) * 64]
                        .rearrange("t p f -> p t f"),
                        in_=aev[:, 0:nts * 64].rearrange("p (t f) -> p t f", f=64),
                    ).then_inc(sems["spl"], 16)
                emit("gpsimd", f3_osp)
                bump("spl", 16)

            for jidx in range(3 * hg, 3 * hg + 3):
                if jidx < len(W1JOB):
                    emit_w1_job(jidx)
            if hg == 3:
                def w1_ag(gpsimd, need=w1_prev["spl"]):
                    gpsimd.wait_ge(sems["spl"], need)
                    ins = gpsimd.collective_compute(
                        "AllGather", ALU.bypass,
                        replica_groups=[list(range(NCORES))],
                        ins=[w1agi.opt()],
                        outs=[w1ago.opt()],
                    )
                    ins.then_inc(sems["spl"], 1)
                emit("gpsimd", w1_ag)
                ag_done = bump("spl", 1)

        ph3_spl = cnt["spl"]
        if DEBUG:
            def f3_dbg(gpsimd, need=ph3_spl):
                gpsimd.wait_ge(sems["spl"], need)
                gpsimd.dma_start(out=ap["dbg_ao_t2"], in_=ao_sp[2, :, :]).then_inc(sems["spl"], 16)
            emit("gpsimd", f3_dbg)
            bump("spl", 16)

        def f4_ldw(sync, p3=cnt["tE"]):
            sync.wait_ge(sems["tE"], p3)
            for kc in range(8):
                sync.dma_start(out=AW[:, kc * 1024:(kc + 1) * 1024],
                               in_=ap["wout"][kc * 128:(kc + 1) * 128, :]).then_inc(sems["ldw"], 16)
        emit("sync", f4_ldw)
        wout_ld = bump("ldw", 16 * 8)

        prev4 = {"vE": cnt["vE"], "spl": cnt["spl"]}
        x1_prev = 0
        for t in range(NT_OWN):
            def f4_lda(sync, t=t, p=dict(prev4), spl_need=cnt["spl"], xp=x1_prev):
                sync.wait_ge(sems["spl"], spl_need)
                if t > 0:
                    sync.wait_ge(sems["vE"], xp)
                sync.dma_start(out=AS[:, 0:1024], in_=ao_sp[t, :, :]).then_inc(sems["ldx"], 16)
                sync.dma_start(out=AS[:, 2048:3072],
                               in_=ap["xv"][(OWN0 + t) * 128:(OWN0 + t + 1) * 128, :]
                               ).then_inc(sems["ldx"], 16)
                sync.dma_start(out=AS[:, 3072:4096], in_=g_sp[t, :, :]).then_inc(sems["ldx"], 16)
            emit("sync", f4_lda)
            lda = bump("ldx", 48)

            def f4_tr(tensor, t=t, ld=lda, wl=wout_ld, pv=prev4["vE"]):
                tensor.wait_ge(sems["ldx"], ld)
                if t == 0:
                    tensor.wait_ge(sems["ldw"], wl)
                    tensor.wait_ge(sems["vE"], pv)
                ins = None
                for kc in range(8):
                    ins = tensor.transpose(psum[kc // 4][:, (kc % 4) * 128:(kc % 4) * 128 + 128],
                                           AS[:, kc * 128:kc * 128 + 128], ident[:, :])
                ins.then_inc(sems["tE"], 1)
            emit("tensor", f4_tr)
            tr4 = bump("tE")

            def f4_trev(vector, t=t, tr=tr4):
                vector.wait_ge(sems["tE"], tr)
                ins = None
                for half in range(2):
                    ins = vector.tensor_copy(AS[:, 1024 + half * 512: 1024 + half * 512 + 512],
                                             psum[half][:, :])
                ins.then_inc(sems["vE"], 1)
            emit("vector", f4_trev)
            trev4 = bump("vE")

            def f4_mm(tensor, t=t, tv=trev4):
                tensor.wait_ge(sems["vE"], tv)
                ins = None
                for oc in range(2):
                    for kc in range(8):
                        ins = tensor.matmul(psum[2 + oc][:, :],
                                            AS[:, 1024 + kc * 128:1024 + kc * 128 + 128],
                                            AW[:, kc * 1024 + oc * 512: kc * 1024 + oc * 512 + 512],
                                            start=(kc == 0), stop=(kc == 7))
                ins.then_inc(sems["tE"], 1)
            emit("tensor", f4_mm)
            smm = bump("tE")

            def f4_gate(vector, t=t, sm=smm):
                vector.wait_ge(sems["tE"], sm)
                for oc in range(2):
                    vector.tensor_tensor(AS[:, 4096 + oc * 512: 4096 + oc * 512 + 512],
                                         psum[2 + oc][:, :],
                                         AS[:, 3072 + oc * 512: 3072 + oc * 512 + 512],
                                         ALU.mult)
                ins = None
                for oc in range(2):
                    ins = vector.tensor_tensor(AS[:, 5120 + oc * 512: 5120 + oc * 512 + 512],
                                               AS[:, 4096 + oc * 512: 4096 + oc * 512 + 512],
                                               AS[:, 2048 + oc * 512: 2048 + oc * 512 + 512],
                                               ALU.add)
                ins.then_inc(sems["vE"], 1)
            emit("vector", f4_gate)
            x1_done = bump("vE")
            x1_prev = x1_done

            def f4_sp(gpsimd, t=t, xd=x1_done):
                gpsimd.wait_ge(sems["vE"], xd)
                gpsimd.dma_start(out=x1_sp[t, :, :], in_=AS[:, 5120:6144]).then_inc(sems["spl"], 16)
            emit("gpsimd", f4_sp)
            bump("spl", 16)

        ph4_spl = cnt["spl"]
        if DEBUG:
            def f4_dbg(gpsimd, need=ph4_spl):
                gpsimd.wait_ge(sems["spl"], need)
                gpsimd.dma_start(out=ap["dbg_x1_t3"], in_=x1_sp[3, :, :]).then_inc(sems["spl"], 16)
            emit("gpsimd", f4_dbg)
            bump("spl", 16)


        ph4_end = {"vE": cnt["vE"], "spl": cnt["spl"], "sE": cnt["sE"],
                   "tE": cnt["tE"]}
        uln_prev_v = 0
        for t in range(NT_OWN):
            def f5_ld(sync, t=t, p4=ph4_end["spl"], spl_need=cnt["spl"], pv=uln_prev_v):
                sync.wait_ge(sems["spl"], spl_need)
                if pv:
                    sync.wait_ge(sems["vE"], pv)
                sync.dma_start(out=AS[:, 0:1024], in_=x1_sp[t, :, :]).then_inc(sems["ldx"], 16)
            emit("sync", f5_ld)
            ld5 = bump("ldx", 16)

            def f5_v1(vector, t=t, ld=ld5):
                vector.wait_ge(sems["ldx"], ld)
                vector.tensor_reduce(stat[:, 0:1], AS[:, 0:1024], AX.X, ALU.add)
            emit("vector", f5_v1)

            def f5_s1(scalar, t=t, ld=ld5):
                scalar.wait_ge(sems["ldx"], ld)
                ins = scalar.activation(AS[:, 1024:2048], AS[:, 0:1024], AF.Square)
                ins.then_inc(sems["sE"], 1)
            emit("scalar", f5_s1)
            sq5 = bump("sE")

            def f5_v2(vector, t=t, sq=sq5):
                vector.wait_ge(sems["sE"], sq)
                vector.tensor_reduce(stat[:, 1:2], AS[:, 1024:2048], AX.X, ALU.add)
                vector.tensor_copy(padb[:, :], ident[:, :])
                vector.tensor_scalar(stat[:, 4:6], stat[:, 0:2], 1.0 / D, None, ALU.mult)
                vector.tensor_copy(padb[:, :], ident[:, :])
                vector.tensor_tensor(stat[:, 8:9], stat[:, 4:5], stat[:, 4:5], ALU.mult)
                vector.tensor_copy(padb[:, :], ident[:, :])
                vector.tensor_tensor(stat[:, 9:10], stat[:, 5:6], stat[:, 8:9], ALU.subtract)
                vector.tensor_copy(padb[:, :], ident[:, :])
                ins = vector.tensor_scalar(stat[:, 10:11], stat[:, 9:10], 1e-5, None, ALU.add)
                ins.then_inc(sems["vE"], 1)
            emit("vector", f5_v2)
            var5 = bump("vE")

            def f5_s2(scalar, t=t, vd=var5):
                scalar.wait_ge(sems["vE"], vd)
                scalar.activation(stat[:, 11:12], stat[:, 10:11], AF.Ln)
                scalar.copy(padb2[:, :], ident[:, :])
                ins = scalar.activation(stat[:, 12:13], stat[:, 11:12], AF.Exp, scale=-0.5)
                ins.then_inc(sems["sE"], 1)
            emit("scalar", f5_s2)
            rstd5 = bump("sE")

            def f5_v3(vector, t=t, rs=rstd5):
                vector.wait_ge(sems["sE"], rs)
                vector.tensor_tensor(stat[:, 13:14], stat[:, 4:5], stat[:, 12:13], ALU.mult)
                vector.tensor_copy(padb[:, :], ident[:, :])
                vector.tensor_scalar(stat[:, 14:15], stat[:, 13:14], -1.0, None, ALU.mult)
                vector.tensor_copy(padb[:, :], ident[:, :])
                ins = vector.tensor_scalar(AS[:, 1024:2048], AS[:, 0:1024],
                                           stat[:, 12:13], stat[:, 14:15],
                                           ALU.mult, ALU.add)
                ins.then_inc(sems["vE"], 1)
            emit("vector", f5_v3)
            uln_prev_v = bump("vE")

            def f5_sp(gpsimd, t=t, ud=uln_prev_v):
                gpsimd.wait_ge(sems["vE"], ud)
                gpsimd.dma_start(out=u_sp[t, :, :], in_=AS[:, 1024:2048]).then_inc(sems["spl"], 16)
            emit("gpsimd", f5_sp)
            bump("spl", 16)

        ph5_spl = cnt["spl"]
        if DEBUG:
            def f5_dbg(gpsimd, need=ph5_spl):
                gpsimd.wait_ge(sems["spl"], need)
                gpsimd.dma_start(out=ap["dbg_u_t2"], in_=u_sp[2, :, :]).then_inc(sems["spl"], 16)
            emit("gpsimd", f5_dbg)
            bump("spl", 16)

        RANKB = 6 * 128 * 1024 + 128 * INNER1

        def w1_src(rank, slot, Wd):
            off = rank * RANKB
            for s2 in range(slot):
                off += 128 * (1024 if s2 < 6 else INNER1)
            return w1ago[off:off + 128 * Wd].rearrange("(p f) -> p f", p=128)

        tr_jobs = []
        for m, tgt in ((0, wg1q_sp), (1, wu1q_sp)):
            for c in range(22):
                rank, sl = c // 3, (c % 3) + m * 3
                tr_jobs.append((tgt, c, rank, sl, 1024))
        for r in range(8):
            tr_jobs.append((wd1q_sp, r, r, 6, INNER1))

        prev_trv = 0
        prev_tsp = 0
        for (tgt, c, rank, sl, Wd) in tr_jobs:
            nj = Wd // 128
            def f6_ldt(sync, rank=rank, sl=sl, Wd=Wd, agd=ag_done, ptv=prev_trv):
                sync.wait_ge(sems["spl"], agd)
                if ptv:
                    sync.wait_ge(sems["vE"], ptv)
                sync.dma_start(out=AW[:, 0:Wd], in_=w1_src(rank, sl, Wd)
                               ).then_inc(sems["ldw"], 16)
            emit("sync", f6_ldt)
            ldt = bump("ldw", 16)

            def f6_trw(tensor, Wd=Wd, nj=nj, ld=ldt, ptv=prev_trv):
                tensor.wait_ge(sems["ldw"], ld)
                if ptv:
                    tensor.wait_ge(sems["vE"], ptv)
                ins = None
                for j in range(nj):
                    ins = tensor.transpose(
                        psum[j % 6][:, (j // 6 % 4) * 128:(j // 6 % 4) * 128 + 128],
                        AW[:, j * 128:(j + 1) * 128], ident[:, :])
                ins.then_inc(sems["tE"], 1)
            emit("tensor", f6_trw)
            trw = bump("tE")

            def f6_tev(vector, Wd=Wd, nj=nj, tr=trw, psp=prev_tsp):
                vector.wait_ge(sems["tE"], tr)
                if psp:
                    vector.wait_ge(sems["spl"], psp)
                for j in range(nj):
                    vector.tensor_copy(AW[:, 4096 + j * 128: 4096 + (j + 1) * 128],
                                       psum[j % 6][:, (j // 6 % 4) * 128:(j // 6 % 4) * 128 + 128])
                vector.tensor_copy(ABF[:, 0:Wd], AW[:, 4096:4096 + Wd])
                vector.tensor_copy(padb[:, :], ident[:, :])
                vector.tensor_tensor(AW[:, 8192:8192 + Wd], AW[:, 4096:4096 + Wd],
                                     ABF[:, 0:Wd], ALU.subtract)
                vector.tensor_copy(padb[:, :], ident[:, :])
                ins = vector.tensor_copy(ABF[:, 4096:4096 + Wd], AW[:, 8192:8192 + Wd])
                ins.then_inc(sems["vE"], 1)
            emit("vector", f6_tev)
            prev_trv = bump("vE")

            def f6_tsp(gpsimd, tgt=tgt, c=c, Wd=Wd, nj=nj, tv=prev_trv):
                gpsimd.wait_ge(sems["vE"], tv)
                for hilo in range(2):
                    gpsimd.dma_start(
                        out=tgt[hilo, 0:nj, :, c * 128:(c + 1) * 128]
                        .rearrange("j p f -> p j f"),
                        in_=ABF[:, hilo * 4096: hilo * 4096 + Wd]
                        .rearrange("p (j f) -> p j f", f=128),
                    ).then_inc(sems["spl"], 16)
            emit("gpsimd", f6_tsp)
            prev_tsp = bump("spl", 32)

        w1t_spl = cnt["spl"]
        prev_w_use = 0
        prev_dn_use = 0
        prev_dn_v = 0
        prev_hsp = 0

        prev6 = {"vE": cnt["vE"], "spl": cnt["spl"], "sE": cnt["sE"], "tE": cnt["tE"]}
        ICB = [(i * 512, 512) for i in range(5)] + [(2560, 256)]
        for gg in range(4):
            xq_evs = []
            prev_xtr = prev6["vE"] if gg == 0 else prev_dn_v
            for t4 in range(4):
                t = gg * 4 + t4

                def f6_ldu(sync, t=t, pv=prev_xtr, sneed=cnt["spl"]):
                    sync.wait_ge(sems["spl"], sneed)
                    if pv:
                        sync.wait_ge(sems["vE"], pv)
                    sync.dma_start(out=AS[:, 0:1024], in_=u_sp[t, :, :]).then_inc(sems["ldx"], 16)
                emit("sync", f6_ldu)
                ldu = bump("ldx", 16)

                qv = emit_fp4_quant(AS[:, 0:1024], 64, 1024, AS[:, 1024:2048],
                                    sq_wait=("ldx", ldu), name=f"u{t}")

                def f6_xtr(tensor, qv=qv, pv=prev_xtr):
                    tensor.wait_ge(sems["vE"], qv)
                    if pv:
                        tensor.wait_ge(sems["vE"], pv)
                    ins = None
                    for j in range(8):
                        ins = tensor.transpose(psum[j // 4][:, (j % 4) * 128:(j % 4) * 128 + 128],
                                               AS[:, 1024 + j * 128:1024 + (j + 1) * 128],
                                               ident[:, :])
                    ins.then_inc(sems["tE"], 1)
                emit("tensor", f6_xtr)
                xtr = bump("tE")

                def f6_xev(vector, t4=t4, tr=xtr):
                    vector.wait_ge(sems["tE"], tr)
                    for half in range(2):
                        vector.tensor_copy(AS[:, 2048 + half * 512:2048 + half * 512 + 512],
                                           psum[half][:, :])
                    vector.tensor_copy(ABF[:, t4 * 2048: t4 * 2048 + 1024], AS[:, 2048:3072])
                    vector.tensor_copy(padb[:, :], ident[:, :])
                    vector.tensor_tensor(AS[:, 0:1024], AS[:, 2048:3072],
                                         ABF[:, t4 * 2048: t4 * 2048 + 1024], ALU.subtract)
                    vector.tensor_copy(padb[:, :], ident[:, :])
                    ins = vector.tensor_copy(ABF[:, t4 * 2048 + 1024: t4 * 2048 + 2048],
                                             AS[:, 0:1024])
                    ins.then_inc(sems["vE"], 1)
                emit("vector", f6_xev)
                prev_xtr = bump("vE")
                xq_evs.append(prev_xtr)

                if DEBUG and t == 2:
                    def f6_dbgx(gpsimd, xe=qv):
                        gpsimd.wait_ge(sems["vE"], xe)
                        gpsimd.dma_start(out=ap["dbg_xq_t2"],
                                         in_=AS[:, 1024:2048]).then_inc(sems["spl"], 16)
                    emit("gpsimd", f6_dbgx)
                    bump("spl", 16)

            prev_hev = 0
            for (ic0, icw) in ICB:
                for mat, tgt_sp in ((0, wg1q_sp), (1, wu1q_sp)):
                    for hilo_pass in range(2):
                        def f6_ldw1(sync, tgt_sp=tgt_sp, hl=hilo_pass, ic0=ic0,
                                    icw=icw, pw=prev_w_use, xe=xq_evs[-1]):
                            sync.wait_ge(sems["vE"], xe)
                            if pw:
                                sync.wait_ge(sems["tE"], pw)
                            sync.dma_start(
                                out=nxtg[:, 0:8 * icw].rearrange("p (kc w) -> p kc w", w=icw),
                                in_=tgt_sp[hl, :, :, ic0:ic0 + icw]
                                .rearrange("kc p w -> p kc w"),
                            ).then_inc(sems["ldw"], 16)
                        emit("sync", f6_ldw1)
                        ldw1 = bump("ldw", 16)

                        def f6_gmm(tensor, mat=mat, hl=hilo_pass, ic0=ic0, icw=icw,
                                   ld=ldw1, phe=prev_hev, first=(hilo_pass == 0)):
                            tensor.wait_ge(sems["ldw"], ld)
                            if first and phe:
                                tensor.wait_ge(sems["vE"], phe)
                            ins = None
                            for t4 in range(4):
                                pb = psum[mat * 4 + t4]
                                if hl == 0:
                                    for xs_ in range(2):
                                        for kc in range(8):
                                            lh = ABF[:, t4 * 2048 + xs_ * 1024 + kc * 128:
                                                     t4 * 2048 + xs_ * 1024 + kc * 128 + 128]
                                            ins = tensor.matmul(
                                                pb[:, 0:icw], lh,
                                                nxtg[:, kc * icw:(kc + 1) * icw],
                                                start=(xs_ == 0 and kc == 0), stop=False)
                                else:
                                    for kc in range(8):
                                        lh = ABF[:, t4 * 2048 + kc * 128:
                                                 t4 * 2048 + kc * 128 + 128]
                                        ins = tensor.matmul(
                                            pb[:, 0:icw], lh,
                                            nxtg[:, kc * icw:(kc + 1) * icw],
                                            start=False, stop=(kc == 7))
                            ins.then_inc(sems["tE"], 1)
                        emit("tensor", f6_gmm)
                        prev_w_use = bump("tE")
                gu_done = prev_w_use

                def f6_sil1(scalar, ic0=ic0, icw=icw, gd=gu_done):
                    scalar.wait_ge(sems["tE"], gd)
                    ins = None
                    for t4 in range(4):
                        ins = scalar.activation(AACT[:, t4 * 512:t4 * 512 + icw],
                                                psum[t4][:, 0:icw], AF.Silu)
                    ins.then_inc(sems["sE"], 1)
                emit("scalar", f6_sil1)
                sil1 = bump("sE")

                def f6_h2(vector, ic0=ic0, icw=icw, sd=sil1):
                    vector.wait_ge(sems["sE"], sd)
                    ins = None
                    for t4 in range(4):
                        ins = vector.tensor_tensor(
                            AW[:, t4 * 2816 + ic0: t4 * 2816 + ic0 + icw],
                            psum[4 + t4][:, 0:icw],
                            AACT[:, t4 * 512:t4 * 512 + icw], ALU.mult)
                    ins.then_inc(sems["vE"], 1)
                emit("vector", f6_h2)
                prev_hev = bump("vE")

            prev_htr = prev_hev
            for t4 in range(4):
                t = gg * 4 + t4
                qv2 = emit_fp4_quant(AW[:, t4 * 2816:(t4 + 1) * 2816], 176, INNER1,
                                     AS[:, 0:2816],
                                     sq_wait=("vE", prev_htr), name=f"h2q{t}")

                def f6_htr(tensor, qv=qv2, pv=prev_htr):
                    tensor.wait_ge(sems["vE"], qv)
                    ins = None
                    for j in range(22):
                        ins = tensor.transpose(
                            psum[j % 6][:, (j // 6 % 4) * 128:(j // 6 % 4) * 128 + 128],
                            AS[:, j * 128:(j + 1) * 128], ident[:, :])
                    ins.then_inc(sems["tE"], 1)
                emit("tensor", f6_htr)
                htr = bump("tE")

                def f6_hev(vector, tr=htr, psp=prev_hsp):
                    vector.wait_ge(sems["tE"], tr)
                    if psp:
                        vector.wait_ge(sems["spl"], psp)
                    for j in range(22):
                        vector.tensor_copy(AS[:, j * 128:(j + 1) * 128],
                                           psum[j % 6][:, (j // 6 % 4) * 128:(j // 6 % 4) * 128 + 128])
                    vector.tensor_copy(ABF[:, 0:2816], AS[:, 0:2816])
                    vector.tensor_copy(padb[:, :], ident[:, :])
                    vector.tensor_tensor(AS[:, 2816:5632], AS[:, 0:2816],
                                         ABF[:, 0:2816], ALU.subtract)
                    vector.tensor_copy(padb[:, :], ident[:, :])
                    ins = vector.tensor_copy(ABF[:, 2816:5632], AS[:, 2816:5632])
                    ins.then_inc(sems["vE"], 1)
                emit("vector", f6_hev)
                hev = bump("vE")
                prev_htr = hev

                def f6_hsp(gpsimd, t=t, he=hev):
                    gpsimd.wait_ge(sems["vE"], he)
                    for hilo in range(2):
                        gpsimd.dma_start(
                            out=hq_sp[hilo, 0:22, :, t * 128:(t + 1) * 128]
                            .rearrange("j p f -> p j f"),
                            in_=ABF[:, hilo * 2816: hilo * 2816 + 2816]
                            .rearrange("p (j f) -> p j f", f=128),
                        ).then_inc(sems["spl"], 16)
                emit("gpsimd", f6_hsp)
                prev_hsp = bump("spl", 32)

            def f6_lddn(sync, gg=gg, hs=prev_hsp, pdu=prev_dn_use):
                sync.wait_ge(sems["spl"], hs)
                if pdu:
                    sync.wait_ge(sems["tE"], pdu)
                pass
            dn_base = cnt["tE"]
            for kc in range(22):
                def f6_ldr(sync, gg=gg, kc=kc, hs=prev_hsp, base=dn_base,
                           pdu=prev_dn_use):
                    if kc == 0:
                        sync.wait_ge(sems["spl"], hs)
                        if pdu:
                            sync.wait_ge(sems["tE"], pdu)
                    if kc >= 2:
                        sync.wait_ge(sems["tE"], base + kc - 1)
                    for hilo in range(2):
                        sync.dma_start(
                            out=ABF[:, (kc % 2) * 5120 + hilo * 512:
                                    (kc % 2) * 5120 + hilo * 512 + 512],
                            in_=hq_sp[hilo, kc, :, gg * 512:(gg + 1) * 512],
                        ).then_inc(sems["ldw"], 16)
                    for hilo in range(2):
                        sync.dma_start(
                            out=ABF[:, (kc % 2) * 5120 + 1024 + hilo * 1024:
                                    (kc % 2) * 5120 + 1024 + hilo * 1024 + 1024],
                            in_=wd1q_sp[hilo, kc, :, :],
                        ).then_inc(sems["ldw"], 16)
                emit("sync", f6_ldr)
                ldr = bump("ldw", 64)

                def f6_dmm(tensor, kc=kc, ld=ldr, phe=prev_htr):
                    tensor.wait_ge(sems["ldw"], ld)
                    if kc == 0:
                        tensor.wait_ge(sems["vE"], phe)
                    b0 = (kc % 2) * 5120
                    ins = None
                    for oc in range(8):
                        pb = psum[oc]
                        ins = tensor.matmul(pb[:, :],
                                            ABF[:, b0 + 1024 + oc * 128: b0 + 1024 + oc * 128 + 128],
                                            ABF[:, b0 + 0: b0 + 512],
                                            start=(kc == 0), stop=False)
                        ins = tensor.matmul(pb[:, :],
                                            ABF[:, b0 + 1024 + oc * 128: b0 + 1024 + oc * 128 + 128],
                                            ABF[:, b0 + 512: b0 + 1024],
                                            start=False, stop=False)
                        ins = tensor.matmul(pb[:, :],
                                            ABF[:, b0 + 2048 + oc * 128: b0 + 2048 + oc * 128 + 128],
                                            ABF[:, b0 + 0: b0 + 512],
                                            start=False, stop=(kc == 21))
                    ins.then_inc(sems["tE"], 1)
                emit("tensor", f6_dmm)
                bump("tE")
            dn_done = cnt["tE"]
            prev_dn_use = dn_done

            def f6_sil2(scalar, dn=dn_done):
                scalar.wait_ge(sems["tE"], dn)
                ins = None
                for oc in range(8):
                    ins = scalar.activation(AACT[:, oc * 512:(oc + 1) * 512],
                                            psum[oc][:, :], AF.Silu)
                ins.then_inc(sems["sE"], 1)
            emit("scalar", f6_sil2)
            sil2 = bump("sE")

            prev_gev = 0
            for oc in range(2):
                def f6_gtr(tensor, oc=oc, sd=sil2, pg=prev_gev):
                    tensor.wait_ge(sems["sE"], sd)
                    if pg:
                        tensor.wait_ge(sems["vE"], pg)
                    ins = None
                    for o4 in range(4):
                        occ = oc * 4 + o4
                        for t4 in range(4):
                            ins = tensor.transpose(
                                psum[o4][:, t4 * 128: t4 * 128 + 128],
                                AACT[:, occ * 512 + t4 * 128: occ * 512 + t4 * 128 + 128],
                                ident[:, :])
                    ins.then_inc(sems["tE"], 1)
                emit("tensor", f6_gtr)
                gtr = bump("tE")

                def f6_gev(vector, oc=oc, tr=gtr):
                    vector.wait_ge(sems["tE"], tr)
                    ins = None
                    for o4 in range(4):
                        occ = oc * 4 + o4
                        for t4 in range(4):
                            ins = vector.tensor_copy(
                                AS[:, t4 * 1024 + occ * 128: t4 * 1024 + occ * 128 + 128],
                                psum[o4][:, t4 * 128: t4 * 128 + 128])
                    ins.then_inc(sems["vE"], 1)
                emit("vector", f6_gev)
                prev_gev = bump("vE")
            prev_dn_v = prev_gev

            def f6_gsp(gpsimd, gg=gg, gv=prev_dn_v):
                gpsimd.wait_ge(sems["vE"], gv)
                for t4 in range(4):
                    gpsimd.dma_start(out=g2_sp[gg * 4 + t4, :, :],
                                     in_=AS[:, t4 * 1024:(t4 + 1) * 1024]
                                     ).then_inc(sems["spl"], 16)
            emit("gpsimd", f6_gsp)
            prev_g2sp = bump("spl", 64)

        ph6_spl = cnt["spl"]
        if DEBUG:
            def f6_dbgg(gpsimd, need=ph6_spl):
                gpsimd.wait_ge(sems["spl"], need)
                gpsimd.dma_start(out=ap["dbg_g2_t1"], in_=g2_sp[1, :, :]).then_inc(sems["spl"], 16)
            emit("gpsimd", f6_dbgg)
            bump("spl", 16)

        bar = {k: cnt[k] for k in ("tE", "vE", "sE", "spl", "ldx", "ldw")}
        PI = math.pi

        def PS(i, n=1):
            return AACT[:, 4096 + 256 * i:4096 + 256 * (i + n)]

        PRK_LQR, PRK_LQI, PRK_CLR, PRK_CLI = 12032, 12288, 12544, 12800

        def p7_ld(sync, b=bar):
            sync.wait_ge(sems["tE"], b["tE"])
            sync.wait_ge(sems["vE"], b["vE"])
            sync.wait_ge(sems["sE"], b["sE"])
            sync.wait_ge(sems["spl"], b["spl"])
            sync.dma_start(out=PS(0), in_=ap["ssm_AR2"]).then_inc(sems["ldw"], 16)
            sync.dma_start(out=PS(1), in_=ap["ssm_AI2"]).then_inc(sems["ldw"], 16)
            sync.dma_start(out=PS(2), in_=ap["ssm_CR2"]).then_inc(sems["ldw"], 16)
            sync.dma_start(out=PS(3), in_=ap["ssm_CI2"]).then_inc(sems["ldw"], 16)
            sync.dma_start(out=PS(4), in_=ap["ssm_LD2"]).then_inc(sems["ldw"], 16)
            sync.dma_start(out=mtab[0:1, 0:1024], in_=ap["ssm_aim1"]).then_inc(sems["ldw"], 16)
            sync.dma_start(out=mtab[0:1, 1024:1536],
                           in_=ap["ssm_lrows"].rearrange("a b -> (a b)")
                           .rearrange("(p f) -> p f", p=1)).then_inc(sems["ldw"], 16)
            sync.dma_start(out=mtab[0:1, 1536:1537], in_=ap["ssm_D"]).then_inc(sems["ldw"], 16)
            sync.dma_start(out=mtab[0:1, 1537:2561], in_=ap["ssm_logdt"]).then_inc(sems["ldw"], 16)
            sync.dma_start(out=AW[0:1, 3072:4096], in_=ap["ssm_are0"]).then_inc(sems["ldw"], 16)
            sync.dma_start(out=ften[:, 0:4], in_=ap["ssm_h4mask"]).then_inc(sems["ldw"], 16)
            sync.dma_start(out=blkq[:, 0:128], in_=ap["F_re"][:, 0:128]).then_inc(sems["ldw"], 16)
            sync.dma_start(out=blkq[:, 128:256], in_=ap["F_im"][:, 0:128]).then_inc(sems["ldw"], 16)
            sync.dma_start(out=blkq[:, 256:257], in_=ap["ssm_flast"]).then_inc(sems["ldw"], 16)
            sync.dma_start(out=blkq[:, 258:386], in_=ap["G_re"][0:128, :]).then_inc(sems["ldw"], 16)
            sync.dma_start(out=blkq[:, 386:514], in_=ap["G_im"][0:128, :]).then_inc(sems["ldw"], 16)
            sync.dma_start(out=aev[0:1, 0:128], in_=ap["G_re"][128:129, :]).then_inc(sems["ldw"], 16)
        emit("sync", p7_ld)
        p7_ld_done = bump("ldw", 16 * 17)

        def p7_sc1(scalar, ld=p7_ld_done, b=bar):
            scalar.wait_ge(sems["ldw"], ld)
            scalar.wait_ge(sems["vE"], b["vE"])
            scalar.activation(btab[0:1, 0:1024], mtab[0:1, 1537:2561], AF.Exp)
            scalar.activation(AW[0:1, 2048:3072], AW[0:1, 3072:4096], AF.Exp)
            ins = scalar.activation(PS(5), PS(4), AF.Exp)
            ins.then_inc(sems["sE"], 1)
        emit("scalar", p7_sc1)
        p7s1 = bump("sE")

        def p7_v1(vector, s1=p7s1, b=bar):
            vector.wait_ge(sems["sE"], s1)
            vector.wait_ge(sems["tE"], b["tE"])
            vector.tensor_tensor(AW[0:1, 0:1024], btab[0:1, 0:1024], mtab[0:1, 0:1024], ALU.mult)
            ins = vector.scalar_tensor_tensor(AW[0:1, 1024:2048], btab[0:1, 0:1024], -1.0,
                                              AW[0:1, 2048:3072], ALU.mult, ALU.mult)
            ins.then_inc(sems["vE"], 1)
        emit("vector", p7_v1)
        p7v1 = bump("vE")

        def p7_t1(tensor, v1=p7v1):
            tensor.wait_ge(sems["vE"], v1)
            ins = tensor.matmul(psum[2][:, 0:1], mtab[0:1, 1280:1408], mtab[0:1, 1536:1537],
                                start=True, stop=True)
            ins.then_inc(sems["tE"], 1)
        emit("tensor", p7_t1)
        p7t1 = bump("tE")

        def p7_v2(vector, t1=p7t1):
            vector.wait_ge(sems["tE"], t1)
            ins = vector.tensor_copy(rsx[:, 0:1], psum[2][:, 0:1])
            ins.then_inc(sems["vE"], 1)
        emit("vector", p7_v2)
        p7v2 = bump("vE")

        def p7_sc2(scalar, v2=p7v2):
            scalar.wait_ge(sems["vE"], v2)
            ins = scalar.activation(PS(6), PS(0), AF.Exp)
            ins.then_inc(sems["sE"], 1)
        emit("scalar", p7_sc2)
        p7s2 = bump("sE")

        def p7_v3(vector, s2=p7s2):
            vector.wait_ge(sems["sE"], s2)
            vector.scalar_tensor_tensor(PS(0), PS(5), -1.0, PS(6), ALU.mult, ALU.mult)
            ins = vector.tensor_tensor(PS(7), PS(5), PS(1), ALU.mult)
            ins.then_inc(sems["vE"], 1)
        emit("vector", p7_v3)
        p7v3 = bump("vE")

        def p7_sc3(scalar, v3=p7v3):
            scalar.wait_ge(sems["vE"], v3)
            ins = scalar.activation(PS(8), PS(0), AF.Exp)
            ins.then_inc(sems["sE"], 1)
        emit("scalar", p7_sc3)
        p7s3 = bump("sE")

        def p7_v4(vector, s3=p7s3):
            vector.wait_ge(sems["sE"], s3)
            vector.tensor_scalar(PS(9), PS(7), PI, None, ALU.is_ge)
            vector.tensor_scalar(PS(10), PS(7), 3 * PI, None, ALU.is_ge)
            vector.tensor_scalar(PS(11), PS(7), 5 * PI, None, ALU.is_ge)
            vector.tensor_tensor(PS(12), PS(9), PS(10), ALU.add)
            vector.tensor_tensor(PS(9), PS(12), PS(11), ALU.add)
            ins = vector.scalar_tensor_tensor(PS(10), PS(9), -2 * PI, PS(7),
                                              ALU.mult, ALU.add)
            ins.then_inc(sems["vE"], 1)
        emit("vector", p7_v4)
        p7v4 = bump("vE")

        def p7_sc4(scalar, v4=p7v4):
            scalar.wait_ge(sems["vE"], v4)
            scalar.activation(PS(7), PS(10), AF.Sin)
            scalar.activation(PS(11), PS(10), AF.Sin, scale=0.5)
            ins = scalar.activation(PS(12), PS(11), AF.Square)
            ins.then_inc(sems["sE"], 1)
        emit("scalar", p7_sc4)
        p7s4 = bump("sE")

        def p7_v5(vector, s4=p7s4):
            vector.wait_ge(sems["sE"], s4)
            vector.tensor_scalar(PS(11), PS(12), -2.0, 1.0, ALU.mult, ALU.add)
            vector.tensor_tensor(PS(9), PS(8), PS(11), ALU.mult)
            ins = vector.tensor_tensor(PS(10), PS(8), PS(7), ALU.mult)
            ins.then_inc(sems["vE"], 1)
        emit("vector", p7_v5)
        p7v5 = bump("vE")

        def p7_sc5(scalar, v5=p7v5):
            scalar.wait_ge(sems["vE"], v5)
            scalar.activation(PS(8), PS(6), AF.Square)
            ins = scalar.activation(PS(12), PS(1), AF.Square)
            ins.then_inc(sems["sE"], 1)
        emit("scalar", p7_sc5)
        p7s5 = bump("sE")

        def p7_v6(vector, s5=p7s5):
            vector.wait_ge(sems["sE"], s5)
            ins = vector.tensor_tensor(PS(8), PS(8), PS(12), ALU.add)
            ins.then_inc(sems["vE"], 1)
        emit("vector", p7_v6)
        p7v6 = bump("vE")

        def p7_sc6(scalar, v6=p7v6):
            scalar.wait_ge(sems["vE"], v6)
            scalar.activation(PS(12), PS(8), AF.Ln)
            ins = scalar.activation(PS(8), PS(12), AF.Exp, scale=-1.0)
            ins.then_inc(sems["sE"], 1)
        emit("scalar", p7_sc6)
        p7s6 = bump("sE")

        def p7_v7(vector, s6=p7s6):
            vector.wait_ge(sems["sE"], s6)
            vector.tensor_scalar(PS(12), PS(9), 1.0, None, ALU.subtract)
            vector.scalar_tensor_tensor(PS(13), PS(12), -1.0, PS(6), ALU.mult, ALU.mult)
            vector.tensor_tensor(PS(14), PS(10), PS(1), ALU.mult)
            vector.tensor_tensor(PS(13), PS(13), PS(14), ALU.add)
            vector.scalar_tensor_tensor(PS(14), PS(12), -1.0, PS(1), ALU.mult, ALU.mult)
            vector.scalar_tensor_tensor(PS(15), PS(10), -1.0, PS(6), ALU.mult, ALU.mult)
            vector.tensor_tensor(PS(14), PS(14), PS(15), ALU.add)
            vector.tensor_tensor(PS(13), PS(13), PS(8), ALU.mult)
            vector.tensor_tensor(PS(14), PS(14), PS(8), ALU.mult)
            vector.tensor_tensor(PS(6), PS(2), PS(13), ALU.mult)
            vector.tensor_tensor(PS(8), PS(3), PS(14), ALU.mult)
            vector.tensor_tensor(PS(6), PS(6), PS(8), ALU.subtract)
            vector.tensor_tensor(PS(8), PS(2), PS(14), ALU.mult)
            vector.tensor_tensor(PS(12), PS(3), PS(13), ALU.mult)
            vector.tensor_tensor(PS(8), PS(8), PS(12), ALU.add)
            ins = vector.tensor_scalar(PS(8), PS(8), -1.0, None, ALU.mult)
            ins.then_inc(sems["vE"], 1)
        emit("vector", p7_v7)
        p7v7 = bump("vE")

        for it in range(7):
            si_c, si_s = (11, 7) if it % 2 == 0 else (2, 3)
            di_c, di_s = (2, 3) if it % 2 == 0 else (11, 7)

            def p7_dsc(scalar, sc=si_c, ss=si_s, pv=cnt["vE"], it=it, v7=p7v7):
                scalar.wait_ge(sems["vE"], pv if it else v7)
                scalar.activation(PS(13), PS(sc), AF.Square)
                ins = scalar.activation(PS(14), PS(ss), AF.Square)
                ins.then_inc(sems["sE"], 1)
            emit("scalar", p7_dsc)
            dsq = bump("sE")

            def p7_dv(vector, sc=si_c, ss=si_s, dc=di_c, ds=di_s, sq=dsq):
                vector.wait_ge(sems["sE"], sq)
                vector.tensor_tensor(PS(dc), PS(13), PS(14), ALU.subtract)
                vector.tensor_tensor(PS(13), PS(sc), PS(ss), ALU.mult)
                ins = vector.tensor_scalar(PS(ds), PS(13), 2.0, None, ALU.mult)
                ins.then_inc(sems["vE"], 1)
            emit("vector", p7_dv)
            bump("vE")
        lq_v = cnt["vE"]

        def p7_sc7(scalar, lv=lq_v):
            scalar.wait_ge(sems["vE"], lv)
            ins = scalar.activation(PS(12), PS(0), AF.Exp, scale=128.0)
            ins.then_inc(sems["sE"], 1)
        emit("scalar", p7_sc7)
        p7s7 = bump("sE")

        def p7_v8(vector, s7=p7s7):
            vector.wait_ge(sems["sE"], s7)
            vector.tensor_tensor(PS(4), PS(12), PS(2), ALU.mult)
            vector.tensor_tensor(PS(5), PS(12), PS(3), ALU.mult)
            vector.tensor_tensor(PS(13), PS(6), PS(9), ALU.mult)
            vector.tensor_tensor(PS(14), PS(8), PS(10), ALU.mult)
            vector.tensor_tensor(PS(13), PS(13), PS(14), ALU.add)
            vector.tensor_tensor(PS(14), PS(8), PS(9), ALU.mult)
            vector.tensor_tensor(PS(15), PS(6), PS(10), ALU.mult)
            vector.tensor_tensor(PS(14), PS(14), PS(15), ALU.subtract)
            vector.tensor_copy(AACT[:, PRK_LQR:PRK_LQR + 256], PS(4))
            vector.tensor_copy(AACT[:, PRK_LQI:PRK_LQI + 256], PS(5))
            vector.tensor_copy(AACT[:, PRK_CLR:PRK_CLR + 256], PS(13))
            ins = vector.tensor_copy(AACT[:, PRK_CLI:PRK_CLI + 256], PS(14))
            ins.then_inc(sems["vE"], 1)
        emit("vector", p7_v8)
        p7v8 = bump("vE")

        CBD = 8960

        def p7_bd(vector, v8=p7v8):
            vector.wait_ge(sems["vE"], v8)
            msk = ften[:, 0:4].rearrange("p (a f) -> p a f", a=1).broadcast_to([128, 256, 4])
            in0r = PS(6).rearrange("p (q x) -> p q x", x=1).broadcast_to([128, 256, 4])
            in0i = PS(8).rearrange("p (q x) -> p q x", x=1).broadcast_to([128, 256, 4])
            vector.tensor_tensor(AACT[:, CBD:CBD + 1024].rearrange("p (q f) -> p q f", f=4),
                                 in0r, msk, ALU.mult)
            ins = vector.tensor_tensor(AACT[:, CBD + 1024:CBD + 2048]
                                       .rearrange("p (q f) -> p q f", f=4),
                                       in0i, msk, ALU.mult)
            ins.then_inc(sems["vE"], 1)
        emit("vector", p7_bd)
        p7bd = bump("vE")

        PL = {"pf": 0, "pr": 1024, "lf": 2048, "lr": 3072}
        plane_jobs = [("pf", 1024, 0), ("pr", 1024 + 128, 0),
                      ("lf", 1024, 1024), ("lr", 1024 + 128, 1024)]
        pl_prev = p7v2
        for pn, lrow0, rrow0 in plane_jobs:
            def p7_plm(tensor, l0=lrow0, r0=rrow0, pe=pl_prev):
                tensor.wait_ge(sems["vE"], pe)
                tensor.matmul(psum[0][:, 0:512], mtab[0:1, l0:l0 + 128],
                              AW[0:1, r0:r0 + 512], start=True, stop=True)
                ins = tensor.matmul(psum[1][:, 0:512], mtab[0:1, l0:l0 + 128],
                                    AW[0:1, r0 + 512:r0 + 1024], start=True, stop=True)
                ins.then_inc(sems["tE"], 1)
            emit("tensor", p7_plm)
            plm = bump("tE")

            def p7_ple(vector, pn=pn, t=plm):
                vector.wait_ge(sems["tE"], t)
                vector.tensor_copy(AACT[:, PL[pn]:PL[pn] + 512], psum[0][:, 0:512])
                ins = vector.tensor_copy(AACT[:, PL[pn] + 512:PL[pn] + 1024], psum[1][:, 0:512])
                ins.then_inc(sems["vE"], 1)
            emit("vector", p7_ple)
            pl_prev = bump("vE")

        def p7_wrap(vector, pe=pl_prev):
            vector.wait_ge(sems["vE"], pe)
            wA = AACT[:, 6144:7168]
            wB = AACT[:, 7168:8192]
            for pn in ("pf", "pr"):
                src = AACT[:, PL[pn]:PL[pn] + 1024]
                vector.tensor_scalar(wA, src, PI, None, ALU.is_ge)
                for m in range(1, 7):
                    vector.tensor_scalar(wB, src, (2 * m + 1) * PI, None, ALU.is_ge)
                    vector.tensor_tensor(wA, wA, wB, ALU.add)
                vector.scalar_tensor_tensor(src, wA, -2 * PI, src, ALU.mult, ALU.add)
            ins = vector.tensor_copy(padb[:, :], ident[:, :])
            ins.then_inc(sems["vE"], 1)
        emit("vector", p7_wrap)
        planes_done = bump("vE")

        MPL, TBF, TRG, UGB, EBF = 4096, 4608, 5120, 5888, 7936
        T_COSF, T_SINF, T_COSR, T_SINR, T_RF, T_RR = (TRG, TRG + 128, TRG + 256,
                                                      TRG + 384, TRG + 512, TRG + 640)
        KPL = 11008

        def slab_n(tc_lo, tc_n, n):
            v = AW[:, :].rearrange("p (tc h n) -> p tc h n", tc=4, n=32)
            return v[:, tc_lo:tc_lo + tc_n, :, n]

        prev7 = {"init_v": 0, "tr_last": 0, "tc_last": 0, "tcv_last": 0}
        bank_hist = {i: 0 for i in range(6)}
        bank_tag = {i: "sE" for i in range(6)}
        ebuf_hist = [(0, 0), (0, 0)]

        for g in range(8):
            def s2_u(sync, g=g, ptc=prev7["tc_last"], pd=planes_done):
                if ptc:
                    sync.wait_ge(sems["tE"], ptc)
                else:
                    sync.wait_ge(sems["vE"], pd)
                for c in range(16):
                    sync.dma_start(out=AACT[:, UGB + 128 * c:UGB + 128 * c + 128],
                                   in_=u_sp[c, :, 128 * g:128 * g + 128]).then_inc(sems["ldx"], 16)
            emit("sync", s2_u)
            ug_ld = bump("ldx", 256)

            def s2_scA(scalar, g=g, pd=planes_done, pv=prev7["init_v"]):
                scalar.wait_ge(sems["vE"], pv if pv else pd)
                c0 = 128 * g
                scalar.activation(AACT[:, T_COSF:T_COSF + 128],
                                  AACT[:, PL["pf"] + c0:PL["pf"] + c0 + 128], AF.Sin, scale=0.5)
                scalar.activation(AACT[:, T_RF:T_RF + 128], AACT[:, T_COSF:T_COSF + 128], AF.Square)
                scalar.activation(AACT[:, T_SINF:T_SINF + 128],
                                  AACT[:, PL["pf"] + c0:PL["pf"] + c0 + 128], AF.Sin)
                scalar.activation(AACT[:, T_COSR:T_COSR + 128],
                                  AACT[:, PL["pr"] + c0:PL["pr"] + c0 + 128], AF.Sin, scale=0.5)
                scalar.activation(AACT[:, T_RR:T_RR + 128], AACT[:, T_COSR:T_COSR + 128], AF.Square)
                ins = scalar.activation(AACT[:, T_SINR:T_SINR + 128],
                                        AACT[:, PL["pr"] + c0:PL["pr"] + c0 + 128], AF.Sin)
                ins.then_inc(sems["sE"], 1)
            emit("scalar", s2_scA)
            trigA = bump("sE")

            def s2_vA(vector, sa=trigA):
                vector.wait_ge(sems["sE"], sa)
                vector.tensor_scalar(AACT[:, T_COSF:T_COSF + 128], AACT[:, T_RF:T_RF + 128],
                                     -2.0, 1.0, ALU.mult, ALU.add)
                ins = vector.tensor_scalar(AACT[:, T_COSR:T_COSR + 128], AACT[:, T_RR:T_RR + 128],
                                           -2.0, 1.0, ALU.mult, ALU.add)
                ins.then_inc(sems["vE"], 1)
            emit("vector", s2_vA)
            vA = bump("vE")

            def s2_scB(scalar, g=g, va=vA):
                scalar.wait_ge(sems["vE"], va)
                c0 = 128 * g
                scalar.activation(AACT[:, T_RF:T_RF + 128],
                                  AACT[:, PL["lf"] + c0:PL["lf"] + c0 + 128], AF.Exp)
                ins = scalar.activation(AACT[:, T_RR:T_RR + 128],
                                        AACT[:, PL["lr"] + c0:PL["lr"] + c0 + 128], AF.Exp)
                ins.then_inc(sems["sE"], 1)
            emit("scalar", s2_scB)
            scB = bump("sE")

            def s2_vB(vector, g=g, sb=scB, ptr=prev7["tr_last"], ptc=prev7["tc_last"]):
                vector.wait_ge(sems["sE"], sb)
                if ptc:
                    vector.wait_ge(sems["tE"], ptc)
                vector.tensor_scalar(AACT[:, MPL + 0:MPL + 128], AACT[:, T_COSF:T_COSF + 128],
                                     2.0, None, ALU.mult)
                vector.tensor_copy(AACT[:, MPL + 128:MPL + 256], AACT[:, MPL + 0:MPL + 128])
                vector.tensor_scalar(AACT[:, MPL + 256:MPL + 384], AACT[:, T_COSR:T_COSR + 128],
                                     2.0, None, ALU.mult)
                vector.tensor_copy(AACT[:, MPL + 384:MPL + 512], AACT[:, MPL + 256:MPL + 384])
                vector.tensor_copy(slab_n(0, 1, 0)[:, 0, :], AACT[:, T_RF:T_RF + 128])
                vector.memset(slab_n(1, 1, 0)[:, 0, :], 0.0)
                vector.tensor_tensor(slab_n(0, 1, 1)[:, 0, :], AACT[:, T_RF:T_RF + 128],
                                     AACT[:, T_COSF:T_COSF + 128], ALU.mult)
                vector.tensor_tensor(slab_n(1, 1, 1)[:, 0, :], AACT[:, T_RF:T_RF + 128],
                                     AACT[:, T_SINF:T_SINF + 128], ALU.mult)
                mpl_f = AACT[:, MPL:MPL + 256].rearrange("p (tc h) -> p tc h", tc=2)
                tb_f = AACT[:, TBF:TBF + 256].rearrange("p (tc h) -> p tc h", tc=2)
                ins = None
                for n in range(2, 32):
                    ins = vector.tensor_tensor(tb_f, mpl_f, slab_n(0, 2, n - 1), ALU.mult)
                    ins = vector.scalar_tensor_tensor(slab_n(0, 2, n), tb_f, 1.0,
                                                      slab_n(0, 2, n - 2), ALU.mult, ALU.subtract)
                ins.then_inc(sems["vE"], 1)
            emit("vector", s2_vB)
            fwd_done = bump("vE")

            def s2_vC(vector, g=g):
                vector.tensor_copy(slab_n(2, 1, 0)[:, 0, :], AACT[:, T_RR:T_RR + 128])
                vector.memset(slab_n(3, 1, 0)[:, 0, :], 0.0)
                vector.tensor_tensor(slab_n(2, 1, 1)[:, 0, :], AACT[:, T_RR:T_RR + 128],
                                     AACT[:, T_COSR:T_COSR + 128], ALU.mult)
                vector.tensor_tensor(slab_n(3, 1, 1)[:, 0, :], AACT[:, T_RR:T_RR + 128],
                                     AACT[:, T_SINR:T_SINR + 128], ALU.mult)
                mpl_r = AACT[:, MPL + 256:MPL + 512].rearrange("p (tc h) -> p tc h", tc=2)
                tb_r = AACT[:, TBF + 256:TBF + 512].rearrange("p (tc h) -> p tc h", tc=2)
                ins = None
                for n in range(2, 32):
                    ins = vector.tensor_tensor(tb_r, mpl_r, slab_n(2, 2, n - 1), ALU.mult)
                    ins = vector.scalar_tensor_tensor(slab_n(2, 2, n), tb_r, 1.0,
                                                      slab_n(2, 2, n - 2), ALU.mult, ALU.subtract)
                ins.then_inc(sems["vE"], 1)
            emit("vector", s2_vC)
            rev_done = bump("vE")
            prev7["init_v"] = rev_done

            for b in range(16):
                bank = b % 6
                half = b % 2

                def s2_tr(tensor, g=g, b=b, bank=bank, fd=fwd_done,
                          bh=bank_hist[bank], bt=bank_tag[bank]):
                    tensor.wait_ge(sems["vE"], fd)
                    if bh:
                        tensor.wait_ge(sems[bt], bh)
                    ins = None
                    for qq in range(2):
                        q = 2 * b + qq
                        for comp in range(2):
                            ins = tensor.transpose(
                                psum[bank][:, (2 * qq + comp) * 128:(2 * qq + comp) * 128 + 128],
                                AW[:, comp * 4096 + q * 128:comp * 4096 + q * 128 + 128],
                                ident[:, :])
                    ins.then_inc(sems["tE"], 1)
                emit("tensor", s2_tr)
                trb = bump("tE")

                def s2_ev(scalar, bank=bank, half=half, t=trb, eh=tuple(ebuf_hist[half])):
                    scalar.wait_ge(sems["tE"], max(t, eh[1]))
                    if eh[0]:
                        scalar.wait_ge(sems["spl"], eh[0])
                    ins = scalar.activation(AACT[:, EBF + half * 512:EBF + half * 512 + 512],
                                            psum[bank][:, :], AF.Copy)
                    ins.then_inc(sems["sE"], 1)
                emit("scalar", s2_ev)
                evb = bump("sE")
                bank_hist[bank] = evb
                bank_tag[bank] = "sE"

                def s2_km(tensor, g=g, b=b, half=half, ev=evb, bd=p7bd):
                    tensor.wait_ge(sems["sE"], ev)
                    if g == 0 and b == 0:
                        tensor.wait_ge(sems["vE"], bd)
                    ins = None
                    kb = psum[6 + g // 4]
                    for qq in range(2):
                        q = 2 * b + qq
                        qg = 32 * g + q
                        col = (g % 4) * 128 + q * 4
                        e0 = EBF + half * 512 + qq * 256
                        ins = tensor.matmul(kb[:, col:col + 4],
                                            AACT[:, e0:e0 + 128],
                                            AACT[:, CBD + qg * 4:CBD + qg * 4 + 4],
                                            start=True, stop=False)
                        ins = tensor.matmul(kb[:, col:col + 4],
                                            AACT[:, e0 + 128:e0 + 256],
                                            AACT[:, CBD + 1024 + qg * 4:CBD + 1024 + qg * 4 + 4],
                                            start=False, stop=True)
                    ins.then_inc(sems["tE"], 1)
                emit("tensor", s2_km)
                kmb = bump("tE")

                def s2_sp(gpsimd, g=g, b=b, half=half, ev=evb):
                    gpsimd.wait_ge(sems["sE"], ev)
                    gpsimd.dma_start(out=fwdT_sp[g, b, :, :],
                                     in_=AACT[:, EBF + half * 512:EBF + half * 512 + 512]
                                     ).then_inc(sems["spl"], 16)
                emit("gpsimd", s2_sp)
                spb = bump("spl", 16)
                ebuf_hist[half] = (spb, kmb)
            prev7["tr_last"] = cnt["tE"]

            for tb in range(4):
                bA = (2 * tb) % 6
                bB = (2 * tb + 1) % 6

                def s2_tc(tensor, g=g, tb=tb, bA=bA, bB=bB, rd=rev_done, ug=ug_ld,
                          hA=bank_hist[bA], tA=bank_tag[bA], hB=bank_hist[bB], tB=bank_tag[bB]):
                    tensor.wait_ge(sems["vE"], rd)
                    tensor.wait_ge(sems["ldx"], ug)
                    if hA:
                        tensor.wait_ge(sems[tA], hA)
                    if hB:
                        tensor.wait_ge(sems[tB], hB)
                    ins = None
                    ugv = AACT[:, UGB:UGB + 2048].rearrange("p (c f) -> p c f", f=128)
                    for qq in range(8):
                        q = 8 * tb + qq
                        rhs = ugv[:, :, 4 * q:4 * q + 4]
                        ins = tensor.matmul(psum[bA][:, qq * 64:qq * 64 + 64],
                                            AW[:, 2 * 4096 + q * 128:2 * 4096 + q * 128 + 128],
                                            rhs, start=True, stop=True)
                        ins = tensor.matmul(psum[bB][:, qq * 64:qq * 64 + 64],
                                            AW[:, 3 * 4096 + q * 128:3 * 4096 + q * 128 + 128],
                                            rhs, start=True, stop=True)
                    ins.then_inc(sems["tE"], 1)
                emit("tensor", s2_tc)
                tcb = bump("tE")

                def s2_tcv(vector, g=g, tb=tb, bA=bA, bB=bB, t=tcb):
                    vector.wait_ge(sems["tE"], t)
                    ins = None
                    for comp, bk in ((0, bA), (1, bB)):
                        src = psum[bk][:, :].rearrange("p (q c f) -> p q c f", q=8, c=16, f=4)
                        dst0 = AS[:, comp * 4096 + (32 * g + 8 * tb) * 16:
                                  comp * 4096 + (32 * g + 8 * tb) * 16 + 128]
                        dst = dst0.rearrange("p (q c) -> p q c", c=16)
                        for h4 in range(4):
                            ins = vector.tensor_copy(dst[32 * h4:32 * h4 + 32, :, :],
                                                     src[32 * h4:32 * h4 + 32, :, :, h4])
                    ins.then_inc(sems["vE"], 1)
                emit("vector", s2_tcv)
                tcv = bump("vE")
                bank_hist[bA] = tcv
                bank_tag[bA] = "vE"
                bank_hist[bB] = tcv
                bank_tag[bB] = "vE"
            prev7["tc_last"] = cnt["tE"]
            prev7["tcv_last"] = cnt["vE"]

        s2_tE_end = cnt["tE"]
        s2_vE_end = cnt["vE"]
        s2_tcv_end = prev7["tcv_last"]

        if DEBUG:
            def s2_dbg(gpsimd, te=s2_tE_end, ve=s2_vE_end):
                gpsimd.wait_ge(sems["tE"], te)
                gpsimd.wait_ge(sems["vE"], ve)
                gpsimd.dma_start(out=ap["dbg_ssm_slabf"], in_=AW[:, 0:512]).then_inc(sems["spl"], 16)
                gpsimd.dma_start(out=ap["dbg_ssm_slabr"], in_=AW[:, 8192:8704]).then_inc(sems["spl"], 16)
                gpsimd.dma_start(out=ap["dbg_ssm_ug"],
                                 in_=AACT[:, 5888:5888 + 512]).then_inc(sems["spl"], 16)
            emit("gpsimd", s2_dbg)
            s2dbg = bump("spl", 48)
        else:
            s2dbg = 0

        def s3_kev(vector, te=s2_tE_end):
            vector.wait_ge(sems["tE"], te)
            vector.tensor_scalar(AACT[:, KPL:KPL + 512], psum[6][:, :], 2.0, None, ALU.mult)
            ins = vector.tensor_scalar(AACT[:, KPL + 512:KPL + 1024], psum[7][:, :],
                                       2.0, None, ALU.mult)
            ins.then_inc(sems["vE"], 1)
        emit("vector", s3_kev)
        kev = bump("vE")

        KFRE, KFIM = 11008, 9216

        def s3_kf(tensor, kv=kev, tcv=s2_tcv_end):
            tensor.wait_ge(sems["vE"], max(kv, tcv))
            ins = None
            for hh in range(2):
                rhs = AACT[:, KPL + 512 * hh:KPL + 512 * hh + 512]
                ins = tensor.matmul(psum[0 + hh][:, :], blkq[:, 0:128], rhs, start=True, stop=True)
                ins = tensor.matmul(psum[2 + hh][:, :], blkq[:, 128:256], rhs, start=True, stop=True)
                ins = tensor.matmul(psum[4 + hh][0:1, :], blkq[:, 256:257], rhs, start=True, stop=True)
            ins.then_inc(sems["tE"], 1)
        emit("tensor", s3_kf)
        kf_mm = bump("tE")

        def s3_kfe(scalar, t=kf_mm):
            scalar.wait_ge(sems["tE"], t)
            scalar.activation(AACT[:, KFRE:KFRE + 512], psum[0][:, :], AF.Copy)
            scalar.activation(AACT[:, KFRE + 512:KFRE + 1024], psum[1][:, :], AF.Copy)
            scalar.activation(AACT[:, KFIM:KFIM + 512], psum[2][:, :], AF.Copy)
            scalar.activation(AACT[:, KFIM + 512:KFIM + 1024], psum[3][:, :], AF.Copy)
            scalar.activation(mtab[0:1, 0:512], psum[4][0:1, :], AF.Copy)
            ins = scalar.activation(mtab[0:1, 512:1024], psum[5][0:1, :], AF.Copy)
            ins.then_inc(sems["sE"], 1)
        emit("scalar", s3_kfe)
        kf_done = bump("sE")

        LQR, LQI, CLR, CLI = PRK_LQR, PRK_LQI, PRK_CLR, PRK_CLI
        SRE, SIM = 0, 256
        T1, T2, T3, T4 = 512, 768, 1024, 1280
        AGS, BKA, BKB = 1536, 2048, 2304
        AGB0, AGB1 = 2560, 3072

        def s3_rst(sync, ve=s2_vE_end):
            sync.wait_ge(sems["vE"], ve)
            sync.dma_start(out=AACT[:, BKA:BKA + 256], in_=ap["ssm_blkA"]).then_inc(sems["ldw"], 16)
            sync.dma_start(out=AACT[:, BKB:BKB + 256], in_=ap["ssm_blkB"]).then_inc(sems["ldw"], 16)
        emit("sync", s3_rst)
        rst_ld = bump("ldw", 16 * 2)

        def tre_slot(comp, c):
            return AS[:, comp * 4096:(comp + 1) * 4096]\
                .rearrange("p (q s) -> p q s", s=16)[:, :, c]

        def aw_slot(comp, s):
            return AW[:, comp * 4096:(comp + 1) * 4096]\
                .rearrange("p (q s) -> p q s", s=16)[:, :, s]

        def s3_scan1(vector, ld=rst_ld, tcv=s2_tcv_end):
            vector.wait_ge(sems["ldw"], ld)
            vector.wait_ge(sems["vE"], tcv)
            vector.tensor_copy(AACT[:, SRE:SRE + 256], tre_slot(0, 0))
            vector.tensor_copy(AACT[:, SIM:SIM + 256], tre_slot(1, 0))
            for c in range(1, 16):
                vector.tensor_tensor(AACT[:, T1:T1 + 256], AACT[:, LQR:LQR + 256],
                                     AACT[:, SRE:SRE + 256], ALU.mult)
                vector.tensor_tensor(AACT[:, T2:T2 + 256], AACT[:, LQI:LQI + 256],
                                     AACT[:, SIM:SIM + 256], ALU.mult)
                vector.tensor_tensor(AACT[:, T3:T3 + 256], AACT[:, LQI:LQI + 256],
                                     AACT[:, SRE:SRE + 256], ALU.mult)
                vector.tensor_tensor(AACT[:, T4:T4 + 256], AACT[:, LQR:LQR + 256],
                                     AACT[:, SIM:SIM + 256], ALU.mult)
                vector.tensor_tensor(AACT[:, T1:T1 + 256], AACT[:, T1:T1 + 256],
                                     AACT[:, T2:T2 + 256], ALU.subtract)
                vector.tensor_tensor(AACT[:, T3:T3 + 256], AACT[:, T3:T3 + 256],
                                     AACT[:, T4:T4 + 256], ALU.add)
                vector.tensor_tensor(AACT[:, SRE:SRE + 256], AACT[:, T1:T1 + 256],
                                     tre_slot(0, c), ALU.add)
                vector.tensor_tensor(AACT[:, SIM:SIM + 256], AACT[:, T3:T3 + 256],
                                     tre_slot(1, c), ALU.add)
            vector.tensor_copy(AACT[:, AGS:AGS + 256], AACT[:, SRE:SRE + 256])
            ins = vector.tensor_copy(AACT[:, AGS + 256:AGS + 512], AACT[:, SIM:SIM + 256])
            ins.then_inc(sems["vE"], 1)
        emit("vector", s3_scan1)
        scan1 = bump("vE")

        def s3_ag(gpsimd, sc=scan1, base=cnt["spl"]):
            gpsimd.wait_ge(sems["vE"], sc)
            gpsimd.dma_start(out=sag_i.rearrange("(p f) -> p f", p=128),
                             in_=AACT[:, AGS:AGS + 512]).then_inc(sems["spl"], 16)
            gpsimd.wait_ge(sems["spl"], base + 16)
            ins = gpsimd.collective_compute(
                "AllGather", ALU.bypass,
                replica_groups=[[0, 1], [2, 3], [4, 5], [6, 7]],
                ins=[sag_i.opt()],
                outs=[sag_o.opt()],
            )
            ins.then_inc(sems["spl"], 1)
            gpsimd.wait_ge(sems["spl"], base + 17)
            gpsimd.dma_start(out=AACT[:, AGB0:AGB0 + 512],
                             in_=sag_o[0:65536].rearrange("(p f) -> p f", p=128)
                             ).then_inc(sems["spl"], 16)
            gpsimd.dma_start(out=AACT[:, AGB1:AGB1 + 512],
                             in_=sag_o[65536:131072].rearrange("(p f) -> p f", p=128)
                             ).then_inc(sems["spl"], 16)
        emit("gpsimd", s3_ag)
        ag_done7 = bump("spl", 49)

        def s3_sinit(vector, agd=ag_done7, te=s2_tE_end, sd=s2dbg):
            vector.wait_ge(sems["spl"], agd)
            if sd:
                vector.wait_ge(sems["spl"], sd)
            vector.wait_ge(sems["tE"], te)
            for comp in range(2):
                vector.tensor_tensor(AACT[:, T1:T1 + 256],
                                     AACT[:, AGB0 + comp * 256:AGB0 + comp * 256 + 256],
                                     AACT[:, BKA:BKA + 256], ALU.mult)
                vector.tensor_tensor(AACT[:, T2:T2 + 256],
                                     AACT[:, AGB1 + comp * 256:AGB1 + comp * 256 + 256],
                                     AACT[:, BKB:BKB + 256], ALU.mult)
                vector.tensor_tensor(aw_slot(comp, 0), AACT[:, T1:T1 + 256],
                                     AACT[:, T2:T2 + 256], ALU.add)
            ins = vector.tensor_copy(padb[:, :], ident[:, :])
            ins.then_inc(sems["vE"], 1)
        emit("vector", s3_sinit)
        sinit = bump("vE")

        def s3_scan2(vector, si=sinit):
            vector.wait_ge(sems["vE"], si)
            ins = None
            for c in range(15):
                vector.tensor_tensor(AACT[:, T1:T1 + 256], AACT[:, LQR:LQR + 256],
                                     aw_slot(0, c), ALU.mult)
                vector.tensor_tensor(AACT[:, T2:T2 + 256], AACT[:, LQI:LQI + 256],
                                     aw_slot(1, c), ALU.mult)
                vector.tensor_tensor(AACT[:, T3:T3 + 256], AACT[:, LQI:LQI + 256],
                                     aw_slot(0, c), ALU.mult)
                vector.tensor_tensor(AACT[:, T4:T4 + 256], AACT[:, LQR:LQR + 256],
                                     aw_slot(1, c), ALU.mult)
                vector.tensor_tensor(AACT[:, T1:T1 + 256], AACT[:, T1:T1 + 256],
                                     AACT[:, T2:T2 + 256], ALU.subtract)
                vector.tensor_tensor(AACT[:, T3:T3 + 256], AACT[:, T3:T3 + 256],
                                     AACT[:, T4:T4 + 256], ALU.add)
                vector.tensor_tensor(aw_slot(0, c + 1), AACT[:, T1:T1 + 256],
                                     tre_slot(0, c), ALU.add)
                ins = vector.tensor_tensor(aw_slot(1, c + 1), AACT[:, T3:T3 + 256],
                                           tre_slot(1, c), ALU.add)
            ins.then_inc(sems["vE"], 1)
        emit("vector", s3_scan2)
        scan2 = bump("vE")

        if DEBUG:
            def s3_dbg(gpsimd, sc=scan2):
                gpsimd.wait_ge(sems["vE"], sc)
                gpsimd.dma_start(out=ap["dbg_ssm_tre"], in_=AS[:, 0:512]).then_inc(sems["spl"], 16)
            emit("gpsimd", s3_dbg)
            dbg_tre = bump("spl", 16)
        else:
            dbg_tre = 0

        def s3_fold(vector, sc=scan2, dt=dbg_tre):
            vector.wait_ge(sems["vE"], sc)
            if dt:
                vector.wait_ge(sems["spl"], dt)
            awr = AW[:, 0:4096].rearrange("p (q s) -> p q s", s=16)
            awi = AW[:, 4096:8192].rearrange("p (q s) -> p q s", s=16)
            asr = AS[:, 0:4096].rearrange("p (q s) -> p q s", s=16)
            asi = AS[:, 4096:8192].rearrange("p (q s) -> p q s", s=16)
            clr = AACT[:, CLR:CLR + 256].rearrange("p (q x) -> p q x", x=1)\
                .broadcast_to([128, 256, 16])
            cli = AACT[:, CLI:CLI + 256].rearrange("p (q x) -> p q x", x=1)\
                .broadcast_to([128, 256, 16])
            vector.tensor_tensor(asr, awr, clr, ALU.mult)
            vector.tensor_tensor(asi, awi, cli, ALU.mult)
            vector.tensor_tensor(asr, asr, asi, ALU.add)
            vector.tensor_tensor(asi, awr, cli, ALU.mult)
            vector.tensor_tensor(awr, awi, clr, ALU.mult)
            vector.tensor_tensor(awi, asi, awr, ALU.subtract)
            ins = vector.tensor_copy(awr, asr)
            ins.then_inc(sems["vE"], 1)
        emit("vector", s3_fold)
        fold_done = bump("vE")

        ybank_hist = {i: 0 for i in range(8)}
        GBUF, BDR, BDI = 0, 8192, 8704
        prev_s4mm = 0
        fillc = 0
        for g in range(8):
            def s4_ld(sync, g=g, fd=fold_done, pm=prev_s4mm, kfd=kf_done):
                sync.wait_ge(sems["vE"], fd)
                sync.wait_ge(sems["sE"], kfd)
                if pm:
                    sync.wait_ge(sems["tE"], pm)
                for b in range(16):
                    sync.dma_start(out=AACT[:, GBUF + 512 * b:GBUF + 512 * b + 512],
                                   in_=fwdT_sp[g, b, :, :]).then_inc(sems["ldw"], 16)
            emit("sync", s4_ld)
            gld = bump("ldw", 256)

            for bb in range(4):
                bank = fillc % 8
                fillc += 1

                def s4_bd(vector, g=g, bb=bb, fd=fold_done, pm=prev_s4mm):
                    if pm:
                        vector.wait_ge(sems["tE"], pm)
                    vector.wait_ge(sems["vE"], fd)
                    q0 = 32 * g + 8 * bb
                    msk = ften[:, 0:4].rearrange("p (a f c) -> p a f c", a=1, c=1)\
                        .broadcast_to([128, 8, 4, 16])
                    ins = None
                    for comp, dst in ((0, BDR), (1, BDI)):
                        in0 = AW[:, comp * 4096 + q0 * 16:comp * 4096 + q0 * 16 + 128]\
                            .rearrange("p (q x s) -> p q x s", x=1, s=16)\
                            .broadcast_to([128, 8, 4, 16])
                        ins = vector.tensor_tensor(
                            AACT[:, dst:dst + 512].rearrange("p (q f c) -> p q f c", f=4, c=16),
                            in0, msk, ALU.mult)
                    ins.then_inc(sems["vE"], 1)
                emit("vector", s4_bd)
                bdv = bump("vE")

                def s4_mm(tensor, g=g, bb=bb, bank=bank, bv=bdv, gl=gld,
                          bh=ybank_hist[bank]):
                    tensor.wait_ge(sems["vE"], bv)
                    tensor.wait_ge(sems["ldw"], gl)
                    if bh:
                        tensor.wait_ge(sems["vE"], bh)
                    ins = None
                    for qq in range(8):
                        q = 8 * bb + qq
                        eb = (q // 2) * 512 + (q % 2) * 256
                        ins = tensor.matmul(psum[bank][:, qq * 64:qq * 64 + 64],
                                            AACT[:, GBUF + eb:GBUF + eb + 128],
                                            AACT[:, BDR + qq * 64:BDR + qq * 64 + 64],
                                            start=True, stop=False)
                        ins = tensor.matmul(psum[bank][:, qq * 64:qq * 64 + 64],
                                            AACT[:, GBUF + eb + 128:GBUF + eb + 256],
                                            AACT[:, BDI + qq * 64:BDI + qq * 64 + 64],
                                            start=False, stop=True)
                    ins.then_inc(sems["tE"], 1)
                emit("tensor", s4_mm)
                mmv = bump("tE")
                prev_s4mm = mmv

                def s4_ev(vector, g=g, bb=bb, bank=bank, t=mmv):
                    vector.wait_ge(sems["tE"], t)
                    h0 = 128 * g + 32 * bb
                    if h0 < 512:
                        dst = AW[:, 8192 + h0 * 16:8192 + h0 * 16 + 512]
                    else:
                        dst = AS[:, (h0 - 512) * 16:(h0 - 512) * 16 + 512]
                    ins = vector.tensor_scalar(dst, psum[bank][:, :], 2.0, None, ALU.mult)
                    ins.then_inc(sems["vE"], 1)
                emit("vector", s4_ev)
                yev = bump("vE")
                ybank_hist[bank] = yev
        s4_vE_end = cnt["vE"]
        s4_tE_end = cnt["tE"]

        if DEBUG:
            def s4_dbg(gpsimd, ve=s4_vE_end, kfd=kf_done):
                gpsimd.wait_ge(sems["vE"], ve)
                gpsimd.wait_ge(sems["sE"], kfd)
                gpsimd.dma_start(out=ap["dbg_ssm_sp"], in_=AW[:, 0:512]).then_inc(sems["spl"], 16)
                gpsimd.dma_start(out=ap["dbg_ssm_y0"], in_=AW[:, 8192:9216]).then_inc(sems["spl"], 16)
                gpsimd.dma_start(out=ap["dbg_ssm_kf"],
                                 in_=AACT[:, 11008:11008 + 1024]).then_inc(sems["spl"], 16)
            emit("gpsimd", s4_dbg)
            bump("spl", 48)

        VR0, VR1, VI0, VI1 = 0, 512, 1024, 1536
        UXR, VXR = 2048, 3072
        TVa, TVb = 4096, 4608
        X1T, G2T, UT, YT = 5120, 6144, 7168, 8192
        DUB = 4096

        s5_prev_v = s4_vE_end
        s5_prev_stf = cnt["stf"]
        for c in range(NT_OWN):
            def s5_ld(sync, c=c, pv=s5_prev_v, kfd=kf_done, pstf=s5_prev_stf):
                sync.wait_ge(sems["vE"], pv)
                sync.wait_ge(sems["sE"], kfd)
                if pstf:
                    sync.wait_ge(sems["stf"], pstf)
                sync.dma_start(out=AACT[:, X1T:X1T + 1024], in_=x1_sp[c, :, :]).then_inc(sems["ldx"], 16)
                sync.dma_start(out=AACT[:, G2T:G2T + 1024], in_=g2_sp[c, :, :]).then_inc(sems["ldx"], 16)
                sync.dma_start(out=AACT[:, UT:UT + 1024], in_=u_sp[c, :, :]).then_inc(sems["ldx"], 16)
            emit("sync", s5_ld)
            ld5 = bump("ldx", 48)

            def s5_mm1(tensor, c=c, ld=ld5, pv=s5_prev_v):
                tensor.wait_ge(sems["ldx"], ld)
                tensor.wait_ge(sems["vE"], pv)
                tensor.matmul(psum[4][0:1, :], blkq[:, 256:257], AACT[:, UT:UT + 512],
                              start=True, stop=True)
                tensor.matmul(psum[5][0:1, :], blkq[:, 256:257], AACT[:, UT + 512:UT + 1024],
                              start=True, stop=True)
                tensor.matmul(psum[0][:, :], blkq[:, 0:128], AACT[:, UT:UT + 512],
                              start=True, stop=True)
                tensor.matmul(psum[1][:, :], blkq[:, 0:128], AACT[:, UT + 512:UT + 1024],
                              start=True, stop=True)
                tensor.matmul(psum[2][:, :], blkq[:, 128:256], AACT[:, UT:UT + 512],
                              start=True, stop=True)
                ins = tensor.matmul(psum[3][:, :], blkq[:, 128:256], AACT[:, UT + 512:UT + 1024],
                                    start=True, stop=True)
                ins.then_inc(sems["tE"], 1)
            emit("tensor", s5_mm1)
            mm1 = bump("tE")

            def s5_pw(vector, c=c, t=mm1):
                vector.wait_ge(sems["tE"], t)
                vector.tensor_copy(AACT[0:1, UXR:UXR + 512], psum[4][0:1, :])
                vector.tensor_copy(AACT[0:1, UXR + 512:UXR + 1024], psum[5][0:1, :])
                for hh, (bre, bim, vre, vim) in enumerate(((0, 2, VR0, VI0), (1, 3, VR1, VI1))):
                    kfr = AACT[:, KFRE + 512 * hh:KFRE + 512 * hh + 512]
                    kfi = AACT[:, KFIM + 512 * hh:KFIM + 512 * hh + 512]
                    vector.tensor_tensor(AACT[:, TVa:TVa + 512], psum[bre][:, :], kfr, ALU.mult)
                    vector.tensor_tensor(AACT[:, TVb:TVb + 512], psum[bim][:, :], kfi, ALU.mult)
                    vector.tensor_tensor(AACT[:, vre:vre + 512], AACT[:, TVa:TVa + 512],
                                         AACT[:, TVb:TVb + 512], ALU.subtract)
                    vector.tensor_tensor(AACT[:, TVa:TVa + 512], psum[bre][:, :], kfi, ALU.mult)
                    vector.tensor_tensor(AACT[:, TVb:TVb + 512], psum[bim][:, :], kfr, ALU.mult)
                    vector.tensor_tensor(AACT[:, vim:vim + 512], AACT[:, TVa:TVa + 512],
                                         AACT[:, TVb:TVb + 512], ALU.add)
                ins = vector.tensor_tensor(AACT[0:1, VXR:VXR + 1024], AACT[0:1, UXR:UXR + 1024],
                                           mtab[0:1, 0:1024], ALU.mult)
                ins.then_inc(sems["vE"], 1)
            emit("vector", s5_pw)
            pw = bump("vE")

            def s5_mm2(tensor, c=c, v=pw):
                tensor.wait_ge(sems["vE"], v)
                ins = None
                for hh, (vre, vim, yb) in enumerate(((VR0, VI0, 6), (VR1, VI1, 7))):
                    ins = tensor.matmul(psum[yb][:, :], blkq[:, 258:386],
                                        AACT[:, vre:vre + 512], start=True, stop=False)
                    ins = tensor.matmul(psum[yb][:, :], blkq[:, 386:514],
                                        AACT[:, vim:vim + 512], start=False, stop=False)
                    ins = tensor.matmul(psum[yb][:, :], aev[0:1, 0:128],
                                        AACT[0:1, VXR + 512 * hh:VXR + 512 * hh + 512],
                                        start=False, stop=True)
                ins.then_inc(sems["tE"], 1)
            emit("tensor", s5_mm2)
            mm2 = bump("tE")

            def s5_fin_v(vector, c=c, t=mm2, ld=ld5):
                vector.wait_ge(sems["tE"], t)
                vector.wait_ge(sems["ldx"], ld)
                ya = AW[:, 8192:16384].rearrange("p (h s) -> p h s", s=16)[:, :, c]
                yb = AS[:, 0:8192].rearrange("p (h s) -> p h s", s=16)[:, :, c]
                vector.tensor_tensor(AACT[:, YT:YT + 512], psum[6][:, :], ya, ALU.add)
                vector.tensor_tensor(AACT[:, YT + 512:YT + 1024], psum[7][:, :], yb, ALU.add)
                ins = vector.scalar_tensor_tensor(AACT[:, DUB:DUB + 1024], AACT[:, UT:UT + 1024],
                                                  rsx[:, 0:1], AACT[:, YT:YT + 1024],
                                                  ALU.mult, ALU.add)
                ins.then_inc(sems["vE"], 1)
            emit("vector", s5_fin_v)
            duv = bump("vE")

            def s5_gelu(scalar, v=duv):
                scalar.wait_ge(sems["vE"], v)
                ins = scalar.activation(AACT[:, UT:UT + 1024], AACT[:, DUB:DUB + 1024], AF.Gelu)
                ins.then_inc(sems["sE"], 1)
            emit("scalar", s5_gelu)
            gel = bump("sE")

            def s5_gate(vector, s=gel):
                vector.wait_ge(sems["sE"], s)
                vector.tensor_tensor(AACT[:, DUB:DUB + 1024], AACT[:, UT:UT + 1024],
                                     AACT[:, G2T:G2T + 1024], ALU.mult)
                ins = vector.tensor_tensor(AACT[:, G2T:G2T + 1024], AACT[:, DUB:DUB + 1024],
                                           AACT[:, X1T:X1T + 1024], ALU.add)
                ins.then_inc(sems["vE"], 1)
            emit("vector", s5_gate)
            gate = bump("vE")
            s5_prev_v = gate

            def s5_out(gpsimd, c=c, v=gate):
                gpsimd.wait_ge(sems["vE"], v)
                gpsimd.dma_start(out=ap["out_x"][c * 128:(c + 1) * 128, :],
                                 in_=AACT[:, G2T:G2T + 1024]).then_inc(sems["stf"], 16)
            emit("gpsimd", s5_out)
            s5_prev_stf = bump("stf", 16)

        def s5_wait(gpsimd, fin=cnt["stf"]):
            gpsimd.wait_ge(sems["stf"], fin)
        emit("gpsimd", s5_wait)

        with nc.Block() as block:
            @block.sync
            def _(sync):
                for fn in ops["sync"]:
                    fn(sync)

            @block.tensor
            def _(tensor):
                for fn in ops["tensor"]:
                    fn(tensor)

            @block.vector
            def _(vector):
                for fn in ops["vector"]:
                    fn(vector)

            @block.scalar
            def _(scalar):
                for fn in ops["scalar"]:
                    fn(scalar)

            @block.gpsimd
            def _(gpsimd):
                for fn in ops["gpsimd"]:
                    fn(gpsimd)

    return nc, dbgs


_CACHE = {}


def kernel(**inputs):
    if "nc" not in _CACHE:
        _CACHE["nc"], _CACHE["dbgs"] = build_program()
    nc, dbgs = _CACHE["nc"], _CACHE["dbgs"]
    consts = _host_consts()
    x = np.ascontiguousarray(inputs["L0_attn_x"], dtype=np.float32)
    wqkv = np.ascontiguousarray(inputs["L0_attn_Wqkv"], dtype=np.float32)
    wgT = np.ascontiguousarray(inputs["L0_ffn_w_gate"].T, dtype=np.float32)
    wuT = np.ascontiguousarray(inputs["L0_ffn_w_up"].T, dtype=np.float32)
    wdT = np.ascontiguousarray(inputs["L0_ffn_w_down"].T, dtype=np.float32)
    wout = np.ascontiguousarray(inputs["L0_attn_Wout"], dtype=np.float32)
    wg1 = np.asarray(inputs["L1_ffn_w_gate"], dtype=np.float32)
    wu1 = np.asarray(inputs["L1_ffn_w_up"], dtype=np.float32)
    wd1 = np.asarray(inputs["L1_ffn_w_down"], dtype=np.float32)
    wg1p = np.zeros((3072, D), dtype=np.float32); wg1p[0:INNER1] = wg1
    wu1p = np.zeros((3072, D), dtype=np.float32); wu1p[0:INNER1] = wu1

    def traj2(a):
        return np.ascontiguousarray(
            np.asarray(a, np.float32).reshape(256, 4, 32).transpose(1, 2, 0).reshape(128, 256))

    log_dt = np.asarray(inputs["L1_attn_log_dt"], np.float32)
    ssm_logdt = np.ascontiguousarray(log_dt.reshape(1, 1024))
    ssm_aim1 = np.ascontiguousarray(
        np.asarray(inputs["L1_attn_A_imag"], np.float32)[:, 1].reshape(1, 1024))
    ssm_are0 = np.ascontiguousarray(
        np.asarray(inputs["L1_attn_A_real"], np.float32)[:, 0].reshape(1, 1024))
    ssm_AR2 = traj2(inputs["L1_attn_A_real"])
    ssm_AI2 = traj2(inputs["L1_attn_A_imag"])
    ssm_CR2 = traj2(inputs["L1_attn_C_real"])
    ssm_CI2 = traj2(inputs["L1_attn_C_imag"])
    ssm_LD2 = np.ascontiguousarray(
        np.repeat(log_dt.reshape(256, 4).T, 32, axis=0))
    ssm_D = np.asarray(inputs["L1_attn_D"], np.float32).reshape(1, 1)
    jj = np.arange(128, dtype=np.float32)
    ssm_lrows = np.stack([jj, 127.0 - jj, np.ones(128, np.float32),
                          np.zeros(128, np.float32)]).astype(np.float32)
    h4m = np.zeros((128, 4), dtype=np.float32)
    for h4 in range(4):
        h4m[32 * h4:32 * h4 + 32, h4] = 1.0
    fcol = np.ascontiguousarray(np.cos(
        2.0 * np.pi * np.arange(128) * 128.0 / 256.0).astype(np.float32).reshape(128, 1))

    in_maps = []
    for c in range(NCORES):
        b, half = c // 2, c % 2
        xv = np.zeros((NV * 128, D), dtype=np.float32)
        if half == 0:
            xv[32 * 128:, :] = x[b, 0:2048, :]
        else:
            xv[16 * 128:, :] = x[b, :, :]
        blkA = np.full((128, 256), 1.0 if half == 1 else 0.0, dtype=np.float32)
        blkB = np.zeros((128, 256), dtype=np.float32)
        in_maps.append({"xv": xv, "wqkv": wqkv, "identity": consts["identity"],
                        "attn_mask": consts["attn_mask"],
                        "attn_bias": _attn_bias_host(half),
                        "attn_fden": _attn_fden_host(half),
                        "wout": wout,
                        "wgT": wgT, "wuT": wuT, "wdT": wdT,
                        "w1shard": _w1shard(wg1p, wu1p, wd1, c),
                        "ssm_logdt": ssm_logdt, "ssm_aim1": ssm_aim1,
                        "ssm_are0": ssm_are0, "ssm_AR2": ssm_AR2,
                        "ssm_AI2": ssm_AI2, "ssm_CR2": ssm_CR2,
                        "ssm_CI2": ssm_CI2, "ssm_LD2": ssm_LD2,
                        "ssm_flast": fcol, "ssm_D": ssm_D,
                        "ssm_lrows": ssm_lrows, "ssm_h4mask": h4m,
                        "ssm_blkA": blkA, "ssm_blkB": blkB,
                        "F_re": consts["F_re"], "F_im": consts["F_im"],
                        "G_re": consts["G_re"], "G_im": consts["G_im"]})

    res = run_bass_kernel_spmd(nc, in_maps, core_ids=list(range(NCORES)))
    _CACHE["results"] = res.results
    outs = [res.results[c]["out_x"] for c in range(NCORES)]
    full = np.zeros((B, T, D), dtype=np.float32)
    for c in range(NCORES):
        b, half = c // 2, c % 2
        full[b, half * 2048:(half + 1) * 2048, :] = outs[c]
    return full


# version 04
# speedup vs baseline: 1.0055x; 1.0055x over previous
import math
import contextlib
import numpy as np
import concourse.bass as bass
import concourse.mybir as mybir
from concourse.bass_utils import run_bass_kernel_spmd

F32 = mybir.dt.float32
BF16 = mybir.dt.bfloat16
I32 = mybir.dt.int32
AF = mybir.ActivationFunctionType
ALU = mybir.AluOpType
AX = mybir.AxisListType

NCORES = 8
B, T, D = 4, 4096, 1024
NH, HD = 16, 64
T_OWN = 2048
NT_OWN = 16
NV = 48
V0 = 16
OWN0 = 32
INNER0 = 4096
INNER1 = 2816

SLOPES = [2.0 ** (-0.5 * (h + 1)) for h in range(NH)]
BAND = [min(33, int(math.ceil(45.0 / SLOPES[h] / 128.0)) + 2) for h in range(NH)]

DEBUG = False


def _w1shard(wg1p, wu1p, wd1, c):
    sh = np.zeros((7, 128, INNER1), dtype=np.float32)
    for i in range(3):
        sh[i, :, 0:1024] = wg1p[(3 * c + i) * 128:(3 * c + i + 1) * 128, :]
        sh[3 + i, :, 0:1024] = wu1p[(3 * c + i) * 128:(3 * c + i + 1) * 128, :]
    sh[6, :, :] = wd1[c * 128:(c + 1) * 128, :]
    return sh


def _host_consts():
    c = {"identity": np.eye(128, dtype=np.float32)}
    j = np.arange(128)
    f = np.arange(129)
    ang = 2.0 * np.pi * np.outer(j, f) / 256.0
    c["F_re"] = np.cos(ang).astype(np.float32)
    c["F_im"] = (-np.sin(ang)).astype(np.float32)
    w = np.where((f == 0) | (f == 128), 1.0, 2.0)
    angT = 2.0 * np.pi * np.outer(f, j) / 256.0
    c["G_re"] = (w[:, None] * np.cos(angT) / 256.0).astype(np.float32)
    c["G_im"] = (-w[:, None] * np.sin(angT) / 256.0).astype(np.float32)
    m = np.zeros((128, 2688), dtype=np.float32)
    kj = np.arange(128)[:, None]
    off = 0
    for W, nd in ((512, 4), (256, 2), (128, 1)):
        for d in range(nd):
            ti = np.arange(W)[None, :]
            m[:, off:off + W] = (ti - d * 128 - kj >= 0).astype(np.float32)
            off += W
    c["attn_mask"] = m
    return c


CHUNKW = [128 if h < 2 else (256 if h < 4 else 512) for h in range(NH)]


def _attn_schedule():
    sched = []
    bias_idx = {}
    nb = 0
    for h in range(NH):
        W = CHUNKW[h]
        nqc = T_OWN // W
        nts = W // 128
        for qc in range(nqc):
            qs_lo = qc * nts
            qs_hi = (qc + 1) * nts - 1
            vj_lo = max(V0, qs_lo + 32 + 1 - BAND[h])
            vj_hi = qs_hi + 32
            vjs = list(range(vj_lo, vj_hi + 1))
            for vj in vjs:
                bias_idx[(h, qc, vj)] = nb
                nb += 1
            sched.append((h, qc, vjs))
    return sched, bias_idx, nb


ATT_SCHED, ATT_BIDX, ATT_NB = _attn_schedule()


def _attn_bias_host(half):
    tab = np.zeros((128, ATT_NB), dtype=np.float32)
    for (h, qc, vj), col in ATT_BIDX.items():
        W = CHUNKW[h]
        q0g = half * 2048 + qc * W
        ti_center = q0g + (W - 1) - 30.0 / SLOPES[h]
        kj0g = (vj - 32) * 128 + half * 2048
        kjg = kj0g + np.arange(128)
        bias = SLOPES[h] * (kjg - ti_center)
        bias = np.where((kjg < 0) | (kjg > q0g + W - 1), -1e30, bias)
        tab[:, col] = bias.astype(np.float32)
    return tab


def _attn_fden_host(half):
    tab = np.zeros((128, 256), dtype=np.float32)
    for h in range(NH):
        W = CHUNKW[h]
        nts = W // 128
        nqc = T_OWN // W
        for qc in range(nqc):
            q0g = half * 2048 + qc * W
            ti_center = q0g + (W - 1) - 30.0 / SLOPES[h]
            for tt in range(nts):
                tig = q0g + tt * 128 + np.arange(128)
                tab[:, h * 16 + qc * nts + tt] = np.exp(
                    -SLOPES[h] * (tig - ti_center)).astype(np.float32)
    return tab


def build_program():
    nc = bass.Bass("TRN2", target_bir_lowering=False)
    ops = {e: [] for e in ("tensor", "vector", "scalar", "gpsimd", "sync")}

    def emit(eng, fn):
        ops[eng].append(fn)

    ap = {}

    def din(name, shape, dtype=F32):
        ap[name] = nc.dram_tensor(name, list(shape), dtype, kind="ExternalInput").ap()
        return ap[name]

    def dout(name, shape, dtype=F32):
        ap[name] = nc.dram_tensor(name, list(shape), dtype, kind="ExternalOutput").ap()
        return ap[name]

    def dint(name, shape, dtype=F32):
        ap[name] = nc.dram_tensor(name, list(shape), dtype, kind="Internal").ap()
        return ap[name]

    din("xv", (NV * 128, D))
    din("wqkv", (D, 3 * D))
    din("wgT", (D, INNER0))
    din("wuT", (D, INNER0))
    din("wdT", (INNER0, D))
    din("w1shard", (7, 128, INNER1))
    din("ssm_logdt", (1, 1024))
    din("ssm_aim1", (1, 1024))
    din("ssm_are0", (1, 1024))
    din("ssm_AR2", (128, 256))
    din("ssm_AI2", (128, 256))
    din("ssm_CR2", (128, 256))
    din("ssm_CI2", (128, 256))
    din("ssm_LD2", (128, 256))
    din("ssm_flast", (128, 1))
    din("ssm_D", (1, 1))
    din("ssm_lrows", (4, 128))
    din("ssm_h4mask", (128, 4))
    din("ssm_blkA", (128, 256))
    din("ssm_blkB", (128, 256))
    din("F_re", (128, 129))
    din("F_im", (128, 129))
    din("G_re", (129, 128))
    din("G_im", (129, 128))
    din("identity", (128, 128))
    din("attn_bias", (128, ATT_NB))
    din("attn_fden", (128, 256))
    din("attn_mask", (128, 2688))
    din("wout", (D, D))
    dout("out_x", (T_OWN, D))

    dbgs = []

    def dbg(name, shape, dtype=F32):
        dbgs.append(name)
        return dout("dbg_" + name, shape, dtype)

    if DEBUG:
        dbg("xn0", (512, D))
        dbg("stat", (128, 64))
        dbg("xg0", (128, D))
        dbg("nx_t0", (128, D), BF16)
        dbg("g_t1", (128, D))
        dbg("wgq_c2", (128, 512))
        dbg("ao_t2", (128, D))
        dbg("x1_t3", (128, D))
        dbg("u_t2", (128, D))
        dbg("xq_t2", (128, D))
        dbg("g2_t1", (128, D))
        dbg("qT_h3", (64, 128))
        dbg("kT_h5", (64, 128))
        dbg("v_t40", (128, 256))
        dbg("ssm_tre", (128, 512))
        dbg("ssm_sp", (128, 512))
        dbg("ssm_y0", (128, 1024))
        dbg("ssm_kf", (128, 1024))
        dbg("ssm_slabf", (128, 512))
        dbg("ssm_slabr", (128, 512))
        dbg("ssm_ug", (128, 512))

    kT_sp = dint("kT_sp", (NH, NV, 64, 128))
    wgq_sp = dint("wgq_sp", (8, 128, INNER0), BF16)
    wuq_sp = dint("wuq_sp", (8, 128, INNER0), BF16)
    wdq_sp = dint("wdq_sp", (32, 128, D), BF16)
    g_sp = dint("g_sp", (NT_OWN, 128, D))
    nxt_sp = dint("nxt_sp", (8, 128, T_OWN), BF16)
    u_sp = dint("u_sp", (NT_OWN, 128, D))
    W1RANK = 6 * 128 * 1024 + 128 * INNER1
    w1agi = dint("w1agi", (W1RANK,))
    w1ago = dint("w1ago", (NCORES * W1RANK,))
    wg1q_sp = dint("wg1q_sp", (2, 8, 128, INNER1), BF16)
    wu1q_sp = dint("wu1q_sp", (2, 8, 128, INNER1), BF16)
    wd1q_sp = dint("wd1q_sp", (2, 22, 128, D), BF16)
    g2_sp = dint("g2_sp", (NT_OWN, 128, D))
    hq_sp = dint("hq_sp", (2, 22, 128, T_OWN), BF16)
    fwdT_sp = dint("fwdT_sp", (8, 16, 128, 512))
    sag_i = dint("sag_i", (2 * 128 * 256,))
    sag_o = dint("sag_o", (2 * 2 * 128 * 256,))
    v_sp = dint("v_sp", (NV, 128, D))
    qT_sp = dint("qT_sp", (NH, NT_OWN, 64, 128))
    xn_sp = dint("xn_sp", (NT_OWN, 128, D))
    ao_sp = dint("ao_sp", (NT_OWN, 128, D))
    x1_sp = dint("x1_sp", (NT_OWN, 128, D))

    with contextlib.ExitStack() as stack:
        def sb(name, cols, dtype=F32):
            return stack.enter_context(nc.sbuf_tensor(name, [128, cols], dtype))

        ident = sb("ident", 128)
        identb = sb("identb", 128, BF16)
        AW = sb("AW", 16384)
        AACT = sb("AACT", 13440)
        AS = sb("AS", 8192)
        ABF = sb("ABF", 10240, BF16)
        nxtg = sb("nxtg", 4096, BF16)
        AI = sb("AI", 1024, I32)
        blkq = sb("blkq", 576)
        btab = sb("btab", ATT_NB)
        ften = sb("ften", 256)
        mtab = sb("mtab", 2688)
        aev = sb("aev", 256)
        rsx = sb("rsx", 16)
        rsh = sb("rsh", 16)
        wsc = sb("wsc", 16)
        stat = sb("stat", 64)
        padb = sb("padb", 128)
        padb2 = sb("padb2", 128)

        psum = [stack.enter_context(nc.psum_tensor(f"ps{i}", [128, 512], F32))
                for i in range(8)]

        sems, cnt = {}, {}
        for s in ("ldx", "ldw", "tE", "vE", "sE", "spl", "stf"):
            sems[s] = stack.enter_context(nc.semaphore(s))
            cnt[s] = 0

        def bump(s, by=1):
            cnt[s] += by
            return cnt[s]

        def wqkv_slice(kc, col0, w):
            if kc < 5:
                return AW[:, kc * 3072 + col0: kc * 3072 + col0 + w]
            return AACT[:, (kc - 5) * 3072 + col0: (kc - 5) * 3072 + col0 + w]

        xg_buf = AS[:, 0:4096]
        xn_buf = AS[:, 4096:8192]
        xnT_buf = AACT[:, 9216:13312]

        def ph1_loads(sync):
            sync.dma_start(out=ident[:, :], in_=ap["identity"]).then_inc(sems["ldw"], 16)
            for kc in range(8):
                sync.dma_start(out=wqkv_slice(kc, 0, 3072),
                               in_=ap["wqkv"][kc * 128:(kc + 1) * 128, :]).then_inc(sems["ldw"], 16)
        emit("sync", ph1_loads)
        ldw_wqkv = bump("ldw", 16 * 9)

        def f_once_v(vector, need=cnt["ldw"] + 16 * 9 if False else 16):
            vector.wait_ge(sems["ldw"], 16)
            vector.tensor_copy(identb[:, :], ident[:, :])
            vector.memset(wsc[:, 15:16], 1.0)
        emit("vector", f_once_v)

        NGRP = 8
        prev = {"vE": 0, "spl": 0, "tr1": 0, "xnspl": 0}

        for g in range(NGRP):
            own = g >= 4
            vbase = V0 + g * 4

            def f_sync(sync, g=g, vbase=vbase, pv=prev["vE"], ps=prev["spl"]):
                if g > 0:
                    sync.wait_ge(sems["vE"], pv)
                    sync.wait_ge(sems["spl"], ps)
                sync.dma_start(out=xg_buf.rearrange("p (t d) -> p t d", d=D),
                               in_=ap["xv"][vbase * 128:(vbase + 4) * 128, :]
                               .rearrange("(t p) d -> p t d", p=128)).then_inc(sems["ldx"], 16)
            emit("sync", f_sync)
            ldx_need = bump("ldx", 16)

            def f_vec1(vector, ldx_need=ldx_need):
                vector.wait_ge(sems["ldx"], ldx_need)
                for i in range(4):
                    vector.tensor_reduce(stat[:, i:i + 1], xg_buf[:, i * D:(i + 1) * D],
                                         AX.X, ALU.add)
            emit("vector", f_vec1)

            def f_sc1(scalar, ldx_need=ldx_need, ptr=prev["tr1"], pxs=prev["xnspl"]):
                scalar.wait_ge(sems["ldx"], ldx_need)
                if ptr:
                    scalar.wait_ge(sems["tE"], ptr)
                if pxs:
                    scalar.wait_ge(sems["spl"], pxs)
                ins = None
                for i in range(4):
                    ins = scalar.activation(xn_buf[:, i * D:(i + 1) * D],
                                            xg_buf[:, i * D:(i + 1) * D], AF.Square)
                ins.then_inc(sems["sE"], 1)
            emit("scalar", f_sc1)
            sq_done = bump("sE")

            def f_vec2(vector, sq_done=sq_done):
                vector.wait_ge(sems["sE"], sq_done)
                for i in range(4):
                    vector.tensor_reduce(stat[:, 4 + i:5 + i],
                                         xn_buf[:, i * D:(i + 1) * D], AX.X, ALU.add)
                vector.tensor_scalar(stat[:, 8:12], stat[:, 0:4], 1.0 / D, None, ALU.mult)
                vector.tensor_scalar(stat[:, 12:16], stat[:, 4:8], 1.0 / D, None, ALU.mult)
                vector.tensor_copy(padb[:, :], ident[:, :])
                vector.tensor_tensor(stat[:, 16:20], stat[:, 8:12], stat[:, 8:12], ALU.mult)
                vector.tensor_copy(padb[:, :], ident[:, :])
                vector.tensor_tensor(stat[:, 20:24], stat[:, 12:16], stat[:, 16:20], ALU.subtract)
                vector.tensor_copy(padb[:, :], ident[:, :])
                ins = vector.tensor_scalar(stat[:, 16:20], stat[:, 20:24], 1e-5, None, ALU.add)
                ins.then_inc(sems["vE"], 1)
            emit("vector", f_vec2)
            var_done = bump("vE")

            def f_sc2(scalar, var_done=var_done):
                scalar.wait_ge(sems["vE"], var_done)
                scalar.activation(stat[:, 36:40], stat[:, 16:20], AF.Ln)
                scalar.copy(padb2[:, :], ident[:, :])
                ins = scalar.activation(stat[:, 24:28], stat[:, 36:40], AF.Exp, scale=-0.5)
                ins.then_inc(sems["sE"], 1)
            emit("scalar", f_sc2)
            rstd_done = bump("sE")

            def f_vec3(vector, rstd_done=rstd_done):
                vector.wait_ge(sems["sE"], rstd_done)
                vector.tensor_tensor(stat[:, 32:36], stat[:, 8:12], stat[:, 24:28], ALU.mult)
                vector.tensor_copy(padb[:, :], ident[:, :])
                ins = vector.tensor_scalar(stat[:, 28:32], stat[:, 32:36], -1.0, None, ALU.mult)
                ins.then_inc(sems["vE"], 1)
            emit("vector", f_vec3)
            negmr_done = bump("vE")

            def f_vec3b(vector, ptr=prev["tr1"], pxs=prev["xnspl"]):
                if ptr:
                    vector.wait_ge(sems["tE"], ptr)
                if pxs:
                    vector.wait_ge(sems["spl"], pxs)
                vector.tensor_copy(padb[:, :], ident[:, :])
                ins = None
                for i in range(4):
                    ins = vector.tensor_scalar(xn_buf[:, i * D:(i + 1) * D],
                                               xg_buf[:, i * D:(i + 1) * D],
                                               stat[:, 24 + i:25 + i],
                                               stat[:, 28 + i:29 + i],
                                               ALU.mult, ALU.add)
                ins.then_inc(sems["vE"], 1)
            emit("vector", f_vec3b)
            xn_done = bump("vE")
            XNSEM = "vE"

            xnspl_done = prev["xnspl"]
            if own:
                def f_g1(gpsimd, g=g, xn_done=xn_done):
                    gpsimd.wait_ge(sems["vE"], xn_done)
                    t0 = (g - 4) * 4
                    gpsimd.dma_start(
                        out=xn_sp[t0:t0 + 4, :, :].rearrange("t p d -> p t d"),
                        in_=xn_buf.rearrange("p (t d) -> p t d", d=D),
                    ).then_inc(sems["spl"], 16)
                emit("gpsimd", f_g1)
                xnspl_done = bump("spl", 16)
            if DEBUG and g == 4:
                def f_gdbg(gpsimd, xn_done=xn_done):
                    gpsimd.wait_ge(sems["vE"], xn_done)
                    gpsimd.dma_start(
                        out=ap["dbg_xn0"].rearrange("(t p) d -> p t d", p=128),
                        in_=xn_buf.rearrange("p (t d) -> p t d", d=D),
                    ).then_inc(sems["spl"], 16)
                    gpsimd.dma_start(out=ap["dbg_stat"], in_=stat[:, :]).then_inc(sems["spl"], 16)
                    gpsimd.dma_start(out=ap["dbg_xg0"], in_=xg_buf[:, 0:D]).then_inc(sems["spl"], 16)
                emit("gpsimd", f_gdbg)
                xnspl_done = bump("spl", 48)

            def f_t1a(tensor, xn_done=xn_done, pv=prev["vE"]):
                tensor.wait_ge(sems["vE"], xn_done)
                if pv:
                    tensor.wait_ge(sems["vE"], pv)
                ins = None
                for kc in range(4):
                    for i in range(4):
                        ins = tensor.transpose(
                            psum[kc][:, i * 128:(i + 1) * 128],
                            xn_buf[:, i * D + kc * 128: i * D + kc * 128 + 128],
                            ident[:, :])
                ins.then_inc(sems["tE"], 1)
            emit("tensor", f_t1a)
            tr0_done = bump("tE")

            def f_vec4a(vector, tr0=tr0_done):
                vector.wait_ge(sems["tE"], tr0)
                ins = None
                for kc in range(4):
                    ins = vector.tensor_copy(xnT_buf[:, kc * 512:(kc + 1) * 512],
                                             psum[kc][:, :])
                ins.then_inc(sems["vE"], 1)
            emit("vector", f_vec4a)
            ev0_done = bump("vE")

            def f_t1b(tensor, ev0=ev0_done):
                tensor.wait_ge(sems["vE"], ev0)
                ins = None
                for kc4 in range(4):
                    kc = 4 + kc4
                    for i in range(4):
                        ins = tensor.transpose(
                            psum[kc4][:, i * 128:(i + 1) * 128],
                            xn_buf[:, i * D + kc * 128: i * D + kc * 128 + 128],
                            ident[:, :])
                ins.then_inc(sems["tE"], 1)
            emit("tensor", f_t1b)
            tr1_done = bump("tE")

            def f_vec4b(vector, tr1=tr1_done):
                vector.wait_ge(sems["tE"], tr1)
                ins = None
                for kc4 in range(4):
                    kc = 4 + kc4
                    ins = vector.tensor_copy(xnT_buf[:, kc * 512:(kc + 1) * 512],
                                             psum[kc4][:, :])
                ins.then_inc(sems["vE"], 1)
            emit("vector", f_vec4b)
            ev1_done = bump("vE")

            if own:
                t_abs0 = (g - 4) * 4

                def f_qv1(vector, xn_done=xn_done, g=g):
                    for i in range(4):
                        vector.tensor_reduce(stat[:, 44 + i:45 + i],
                                             xn_buf[:, i * D:(i + 1) * D], AX.X,
                                             ALU.max, apply_absolute_value=True)
                    vector.tensor_copy(padb[:, :], ident[:, :])
                    ins = vector.tensor_scalar(stat[:, 48:52], stat[:, 44:48], 1e-5,
                                               None, ALU.max)
                    ins.then_inc(sems["vE"], 1)
                emit("vector", f_qv1)
                qv1_done = bump("vE")

                def f_qs1(scalar, qv1=qv1_done):
                    scalar.wait_ge(sems["vE"], qv1)
                    scalar.activation(stat[:, 52:56], stat[:, 48:52], AF.Ln)
                    scalar.copy(padb2[:, :], ident[:, :])
                    ins = scalar.activation(stat[:, 56:60], stat[:, 52:56], AF.Exp,
                                            scale=-1.0)
                    ins.then_inc(sems["sE"], 1)
                emit("scalar", f_qs1)
                qs1_done = bump("sE")

                def f_qv2(vector, qs1=qs1_done, g=g):
                    vector.wait_ge(sems["sE"], qs1)
                    vector.tensor_scalar(stat[:, 60:64], stat[:, 56:60], 127.0, None,
                                         ALU.mult)
                    vector.tensor_copy(padb[:, :], ident[:, :])
                    ins = vector.tensor_scalar(rsx[:, (g - 4) * 4:(g - 4) * 4 + 4],
                                               stat[:, 48:52], 1.0 / 127.0, None,
                                               ALU.mult)
                    ins.then_inc(sems["vE"], 1)
                emit("vector", f_qv2)
                qv2_done = bump("vE")

                nf32_dones = []
                prev_qev = ev1_done
                prev_nxt_spl = 0
                for i in range(4):
                    def f_qv3(vector, i=i, qv2=qv2_done):
                        if i == 0:
                            vector.wait_ge(sems["vE"], qv2)
                        xt = xn_buf[:, i * D:(i + 1) * D]
                        tbuf = AS[:, 0:1024]
                        kf = AS[:, 1024:2048]
                        dl = AS[:, 2048:3072]
                        c1 = AS[:, 3072:4096]
                        vector.tensor_scalar(tbuf, xt, stat[:, 60 + i:61 + i], None,
                                             ALU.mult)
                        vector.tensor_copy(AI[:, 0:1024], tbuf)
                        vector.tensor_copy(kf, AI[:, 0:1024])
                        vector.tensor_tensor(dl, tbuf, kf, ALU.subtract)
                        vector.tensor_scalar(c1, dl, 0.5, None, ALU.is_gt)
                        vector.tensor_scalar(tbuf, dl, -0.5, None, ALU.is_lt)
                        vector.tensor_tensor(c1, c1, tbuf, ALU.subtract)
                        ins = vector.tensor_tensor(dl, kf, c1, ALU.add)
                        ins.then_inc(sems["vE"], 1)
                    emit("vector", f_qv3)
                    nf32_done = bump("vE")
                    nf32_dones.append(nf32_done)

                    def f_qt(tensor, i=i, nf=nf32_done, g=g, pev=prev_qev):
                        tensor.wait_ge(sems["vE"], nf)
                        tensor.wait_ge(sems["vE"], pev)
                        ins = None
                        for kc in range(8):
                            ins = tensor.transpose(
                                psum[kc // 4][:, (kc % 4) * 128:(kc % 4) * 128 + 128],
                                AS[:, 2048 + kc * 128:2048 + kc * 128 + 128],
                                ident[:, :])
                        ins.then_inc(sems["tE"], 1)
                    emit("tensor", f_qt)
                    qt_done = bump("tE")

                    def f_qev(vector, i=i, qt=qt_done, g=g, pspl=prev_nxt_spl):
                        vector.wait_ge(sems["tE"], qt)
                        if pspl:
                            vector.wait_ge(sems["spl"], pspl)
                        ins = None
                        for half in range(2):
                            for kc4 in range(4):
                                kc = half * 4 + kc4
                                ins = vector.tensor_copy(
                                    ABF[:, kc * 128: kc * 128 + 128],
                                    psum[half][:, kc4 * 128:kc4 * 128 + 128])
                        ins.then_inc(sems["vE"], 1)
                    emit("vector", f_qev)
                    prev_qev = bump("vE")

                    def f_qsp(gpsimd, i=i, g=g, qe=prev_qev):
                        gpsimd.wait_ge(sems["vE"], qe)
                        t_abs = (g - 4) * 4 + i
                        gpsimd.dma_start(
                            out=nxt_sp[:, :, t_abs * 128:(t_abs + 1) * 128]
                            .rearrange("kc p f -> p kc f"),
                            in_=ABF[:, 0:1024].rearrange("p (kc f) -> p kc f", f=128),
                        ).then_inc(sems["spl"], 16)
                    emit("gpsimd", f_qsp)
                    prev_nxt_spl = bump("spl", 16)
                nxt_ev = cnt["vE"]

                if DEBUG and g == 4:
                    def f_dbg_nx(gpsimd, ns=prev_nxt_spl):
                        gpsimd.wait_ge(sems["spl"], ns)
                        gpsimd.dma_start(
                            out=ap["dbg_nx_t0"].rearrange("a (kc f) -> a kc f", f=128),
                            in_=nxt_sp[:, :, 0:128].rearrange("kc p f -> p kc f"),
                        ).then_inc(sems["spl"], 16)
                    emit("gpsimd", f_dbg_nx)
                    bump("spl", 16)
                ev1_done = nxt_ev

            def f_t2(tensor, ev1=ev1_done, wq_need=ldw_wqkv):
                tensor.wait_ge(sems["vE"], ev1)
                tensor.wait_ge(sems["ldw"], wq_need)
                ins = None
                for dc in range(8):
                    for kc in range(8):
                        ins = tensor.matmul(psum[dc][:, :],
                                            wqkv_slice(kc, 1024 + dc * 128, 128),
                                            xnT_buf[:, kc * 512:(kc + 1) * 512],
                                            start=(kc == 0), stop=(kc == 7))
                ins.then_inc(sems["tE"], 1)
            emit("tensor", f_t2)
            kmm_done = bump("tE")

            def f_vec5(vector, kmm=kmm_done):
                vector.wait_ge(sems["tE"], kmm)
                ins = None
                for dc in range(8):
                    ins = vector.tensor_copy(AS[:, dc * 512:(dc + 1) * 512], psum[dc][:, :])
                ins.then_inc(sems["vE"], 1)
            emit("vector", f_vec5)
            kev_done = bump("vE")

            def f_g2(gpsimd, kev=kev_done, vbase=vbase):
                gpsimd.wait_ge(sems["vE"], kev)
                for dc in range(8):
                    src = AS[:, dc * 512:(dc + 1) * 512].rearrange("p (t f) -> p t f", f=128)
                    for hh in range(2):
                        gpsimd.dma_start(
                            out=kT_sp[2 * dc + hh, vbase:vbase + 4, :, :]
                            .rearrange("t q f -> q t f"),
                            in_=src[hh * 64:(hh + 1) * 64, :, :],
                        ).then_inc(sems["spl"], 16)
            emit("gpsimd", f_g2)
            kspl_done = bump("spl", 16 * 16)

            def f_t3(tensor, kev=kev_done):
                tensor.wait_ge(sems["vE"], kev)
                ins = None
                for ti in range(4):
                    for dv in range(2):
                        pb = psum[ti * 2 + dv]
                        for kc in range(8):
                            ins = tensor.matmul(
                                pb[:, :],
                                xnT_buf[:, kc * 512 + ti * 128: kc * 512 + ti * 128 + 128],
                                wqkv_slice(kc, 2048 + dv * 512, 512),
                                start=(kc == 0), stop=(kc == 7))
                ins.then_inc(sems["tE"], 1)
            emit("tensor", f_t3)
            vmm_done = bump("tE")

            def f_vec6(vector, vmm=vmm_done, kspl=kspl_done):
                vector.wait_ge(sems["tE"], vmm)
                vector.wait_ge(sems["spl"], kspl)
                ins = None
                for ti in range(4):
                    for dv in range(2):
                        ins = vector.tensor_copy(
                            AS[:, (ti * 2 + dv) * 512:(ti * 2 + dv) * 512 + 512],
                            psum[ti * 2 + dv][:, :])
                ins.then_inc(sems["vE"], 1)
            emit("vector", f_vec6)
            vev_done = bump("vE")

            def f_g3(gpsimd, vev=vev_done, vbase=vbase):
                gpsimd.wait_ge(sems["vE"], vev)
                for ti in range(4):
                    gpsimd.dma_start(
                        out=v_sp[vbase + ti, :, :],
                        in_=AS[:, ti * 1024: ti * 1024 + 1024],
                    ).then_inc(sems["spl"], 16)
            emit("gpsimd", f_g3)
            vspl_done = bump("spl", 16 * 4)

            if own:
                def f_t4(tensor, vev=vev_done):
                    tensor.wait_ge(sems["vE"], vev)
                    ins = None
                    for dc in range(8):
                        for kc in range(8):
                            ins = tensor.matmul(psum[dc][:, :],
                                                wqkv_slice(kc, dc * 128, 128),
                                                xnT_buf[:, kc * 512:(kc + 1) * 512],
                                                start=(kc == 0), stop=(kc == 7))
                    ins.then_inc(sems["tE"], 1)
                emit("tensor", f_t4)
                qmm_done = bump("tE")

                def f_vec7(vector, qmm=qmm_done, vspl=vspl_done):
                    vector.wait_ge(sems["tE"], qmm)
                    vector.wait_ge(sems["spl"], vspl)
                    ins = None
                    for dc in range(8):
                        ins = vector.tensor_copy(AS[:, dc * 512:(dc + 1) * 512],
                                                 psum[dc][:, :])
                    ins.then_inc(sems["vE"], 1)
                emit("vector", f_vec7)
                qev_done = bump("vE")

                def f_g4(gpsimd, g=g, qev=qev_done):
                    gpsimd.wait_ge(sems["vE"], qev)
                    t0 = (g - 4) * 4
                    for dc in range(8):
                        src = AS[:, dc * 512:(dc + 1) * 512].rearrange("p (t f) -> p t f", f=128)
                        for hh in range(2):
                            gpsimd.dma_start(
                                out=qT_sp[2 * dc + hh, t0:t0 + 4, :, :]
                                .rearrange("t q f -> q t f"),
                                in_=src[hh * 64:(hh + 1) * 64, :, :],
                            ).then_inc(sems["spl"], 16)
                emit("gpsimd", f_g4)
                qspl_done = bump("spl", 16 * 16)
                prev = {"vE": qev_done, "spl": qspl_done, "tr1": tr1_done,
                        "xnspl": xnspl_done}
            else:
                prev = {"vE": vev_done, "spl": vspl_done, "tr1": tr1_done,
                        "xnspl": xnspl_done}

        ph1_spl = cnt["spl"]

        if DEBUG:
            def f_dbg2(gpsimd, need=ph1_spl):
                gpsimd.wait_ge(sems["spl"], need)
                gpsimd.dma_start(out=ap["dbg_qT_h3"], in_=qT_sp[3, 2, :, :]).then_inc(sems["stf"], 16)
                gpsimd.dma_start(out=ap["dbg_kT_h5"], in_=kT_sp[5, 33, :, :]).then_inc(sems["stf"], 16)
                gpsimd.dma_start(out=ap["dbg_v_t40"], in_=v_sp[40, :, 256:512]).then_inc(sems["stf"], 16)
            emit("gpsimd", f_dbg2)
            bump("stf", 48)


        ph1_vE = prev["vE"]
        ph1_spl = cnt["spl"]

        wchunks = ([("wgT", i, 4096) for i in range(8)] +
                   [("wuT", i, 4096) for i in range(8)] +
                   [("wdT", i, 1024) for i in range(32)])

        def f2_lda(sync, pv=ph1_vE):
            sync.wait_ge(sems["vE"], pv)
            for idx, (nm, i, w) in enumerate(wchunks):
                buf = AW[:, (idx % 2) * 4096:(idx % 2) * 4096 + w]
                if idx >= 2:
                    sync.wait_ge(sems["vE"], ph1_vE + idx - 1)
                sync.dma_start(out=buf, in_=ap[nm][i * 128:(i + 1) * 128, :]).then_inc(sems["ldw"], 16)
        emit("sync", f2_lda)
        ldw_passA0 = cnt["ldw"]
        bump("ldw", 16 * len(wchunks))

        def f2_va(vector, base=ldw_passA0, pv=ph1_vE):
            vector.memset(wsc[:, 8:11], 0.0)
            vector.tensor_copy(padb[:, :], ident[:, :])
            for idx, (nm, i, w) in enumerate(wchunks):
                vector.wait_ge(sems["ldw"], base + 16 * (idx + 1))
                buf = AW[:, (idx % 2) * 4096:(idx % 2) * 4096 + w]
                col = 8 + ("wgT", "wuT", "wdT").index(nm)
                vector.tensor_reduce(stat[:, 40:41], buf, AX.X, ALU.add,
                                     apply_absolute_value=True)
                vector.tensor_copy(padb[:, :], ident[:, :])
                ins = vector.tensor_tensor(wsc[:, col:col + 1], wsc[:, col:col + 1],
                                           stat[:, 40:41], ALU.add)
                ins.then_inc(sems["vE"], 1)
        emit("vector", f2_va)
        va_dones = [ph1_vE + i + 1 for i in range(len(wchunks))]
        cnt["vE"] = va_dones[-1]
        va_all = cnt["vE"]

        def f2_tsum(tensor, va=va_all):
            tensor.wait_ge(sems["vE"], va)
            ins = tensor.matmul(psum[0][0:1, 0:3], wsc[:, 15:16], wsc[:, 8:11],
                                start=True, stop=True)
            ins.then_inc(sems["tE"], 1)
        emit("tensor", f2_tsum)
        tsum_done = bump("tE")

        def f2_vsc(vector, ts=tsum_done):
            vector.wait_ge(sems["tE"], ts)
            vector.tensor_copy(stat[0:1, 40:43], psum[0][0:1, 0:3])
            vector.tensor_copy(padb[:, :], ident[:, :])
            vector.tensor_scalar(stat[0:1, 44:47], stat[0:1, 40:43],
                                 1.0 / (D * INNER0), None, ALU.mult)
            vector.tensor_copy(padb[:, :], ident[:, :])
            ins = vector.tensor_scalar(stat[0:1, 48:51], stat[0:1, 44:47], 1e-5,
                                       None, ALU.max)
            ins.then_inc(sems["vE"], 1)
        emit("vector", f2_vsc)
        vsc_done = bump("vE")

        def f2_vones(vector, vd=vsc_done):
            ins = vector.memset(padb2[0:1, :], 1.0)
            ins.then_inc(sems["vE"], 1)
        emit("vector", f2_vones)
        vones_done = bump("vE")

        def f2_gbc(tensor, vd=vsc_done, vo=vones_done):
            tensor.wait_ge(sems["vE"], vd)
            tensor.wait_ge(sems["vE"], vo)
            ins = tensor.matmul(psum[1][:, 0:3], padb2[0:1, :], stat[0:1, 48:51],
                                start=True, stop=True)
            ins.then_inc(sems["tE"], 1)
        emit("tensor", f2_gbc)
        gbc_done = bump("tE")

        def f2_vth(vector, gb=gbc_done):
            vector.wait_ge(sems["tE"], gb)
            vector.tensor_copy(wsc[:, 0:3], psum[1][:, 0:3])
            vector.tensor_copy(padb2[:, :], ident[:, :])
            vector.tensor_scalar(wsc[:, 3:6], wsc[:, 0:3], 0.5, None, ALU.mult)
            vector.tensor_copy(padb[:, :], ident[:, :])
            ins = vector.tensor_scalar(wsc[:, 6:9], wsc[:, 3:6], -1.0, None, ALU.mult)
            ins.then_inc(sems["vE"], 1)
        emit("vector", f2_vth)
        vth_done = bump("vE")

        def f2_ldb(sync, vth=vth_done, base2=cnt["ldw"]):
            sync.wait_ge(sems["vE"], vth)
            for idx, (nm, i, w) in enumerate(wchunks):
                buf = AW[:, (idx % 2) * 4096:(idx % 2) * 4096 + w]
                if idx >= 2:
                    sync.wait_ge(sems["vE"], vth + idx - 1)
                sync.dma_start(out=buf, in_=ap[nm][i * 128:(i + 1) * 128, :]).then_inc(sems["ldw"], 16)
        emit("sync", f2_ldb)
        ldw_passB0 = cnt["ldw"]
        bump("ldw", 16 * len(wchunks))

        def f2_vb(vector, base=ldw_passB0, vth=vth_done):
            for idx, (nm, i, w) in enumerate(wchunks):
                vector.wait_ge(sems["ldw"], base + 16 * (idx + 1))
                if idx >= 2:
                    vector.wait_ge(sems["spl"], cnt_spl_b0 + 16 * (idx - 1))
                buf = AW[:, (idx % 2) * 4096:(idx % 2) * 4096 + w]
                m = ("wgT", "wuT", "wdT").index(nm)
                c1 = AW[:, 8192:8192 + w]
                c2 = AW[:, 12288:12288 + w]
                vector.tensor_scalar(c1, buf, wsc[:, 3 + m:4 + m], None, ALU.is_ge)
                vector.tensor_scalar(c2, buf, wsc[:, 6 + m:7 + m], None, ALU.is_le)
                qb = ABF[:, (idx % 2) * 4096:(idx % 2) * 4096 + w]
                ins = vector.tensor_tensor(qb, c1, c2, ALU.subtract)
                ins.then_inc(sems["vE"], 1)
        emit("vector", f2_vb)
        vb0 = cnt["vE"]
        bump("vE", len(wchunks))

        cnt_spl_b0 = cnt["spl"]

        def f2_gb(gpsimd, vb0=vb0):
            for idx, (nm, i, w) in enumerate(wchunks):
                gpsimd.wait_ge(sems["vE"], vb0 + idx + 1)
                qb = ABF[:, (idx % 2) * 4096:(idx % 2) * 4096 + w]
                m = ("wgT", "wuT", "wdT").index(nm)
                tgt = (wgq_sp, wuq_sp, wdq_sp)[m]
                gpsimd.dma_start(out=tgt[i if m < 2 else i, :, :], in_=qb).then_inc(sems["spl"], 16)
        emit("gpsimd", f2_gb)
        bump("spl", 16 * len(wchunks))
        wq_spl_all = cnt["spl"]

        if DEBUG:
            def f2_dbgw(gpsimd, need=wq_spl_all):
                gpsimd.wait_ge(sems["spl"], need)
                gpsimd.dma_start(out=ap["dbg_wgq_c2"], in_=wgq_sp[2, :, 1024:1536]).then_inc(sems["spl"], 16)
            emit("gpsimd", f2_dbgw)
            bump("spl", 16)

        prev2 = {"vE": cnt["vE"], "spl": cnt["spl"], "sE": cnt["sE"], "tE": cnt["tE"]}
        for gg in range(4):
            def f2_rs(vector, gg=gg, pv=prev2["vE"], wq=wq_spl_all):
                vector.wait_ge(sems["spl"], wq)
                vector.wait_ge(sems["vE"], pv)
                vector.tensor_tensor(stat[:, 32:36], rsx[:, gg * 4:gg * 4 + 4],
                                     wsc[:, 0:1].broadcast_to([128, 4]), ALU.mult)
                vector.tensor_copy(padb[:, :], ident[:, :])
                ins = vector.tensor_tensor(stat[:, 36:40], rsx[:, gg * 4:gg * 4 + 4],
                                           wsc[:, 1:2].broadcast_to([128, 4]), ALU.mult)
                ins.then_inc(sems["vE"], 1)
            emit("vector", f2_rs)
            rs_done = bump("vE")

            def f2_ldnx(sync, gg=gg, pv=prev2["vE"], ptE=prev2["tE"]):
                sync.wait_ge(sems["tE"], ptE)
                sync.dma_start(
                    out=nxtg[:, :].rearrange("p (kc f) -> p kc f", f=512),
                    in_=nxt_sp[:, :, gg * 512:(gg + 1) * 512]
                    .rearrange("kc p f -> p kc f"),
                ).then_inc(sems["ldw"], 16)
            emit("sync", f2_ldnx)
            nx_ld = bump("ldw", 16)

            sil_dones = []
            prev_sil = rs_done
            for icb in range(8):
                def f2_ldw(sync, gg=gg, icb=icb, ps=prev_sil, ptE=prev2["tE"]):
                    sync.wait_ge(sems["vE"], ps)
                    if icb == 0:
                        sync.wait_ge(sems["tE"], ptE)
                    sync.dma_start(out=ABF[:, 0:4096].rearrange("p (kc w) -> p kc w", w=512),
                                   in_=wgq_sp[:, :, icb * 512:(icb + 1) * 512]
                                   .rearrange("kc p w -> p kc w")).then_inc(sems["ldw"], 16)
                    sync.dma_start(out=ABF[:, 4096:8192].rearrange("p (kc w) -> p kc w", w=512),
                                   in_=wuq_sp[:, :, icb * 512:(icb + 1) * 512]
                                   .rearrange("kc p w -> p kc w")).then_inc(sems["ldw"], 16)
                emit("sync", f2_ldw)
                ld_need = bump("ldw", 32)

                def f2_mm(tensor, gg=gg, icb=icb, ld=ld_need, ps=prev_sil,
                          nxl=nx_ld):
                    tensor.wait_ge(sems["ldw"], ld)
                    tensor.wait_ge(sems["ldw"], nxl)
                    tensor.wait_ge(sems["vE"], ps)
                    ins = None
                    for t4 in range(4):
                        for kc in range(8):
                            lh = nxtg[:, kc * 512 + t4 * 128: kc * 512 + t4 * 128 + 128]
                            ins = tensor.matmul(psum[t4][:, :], lh,
                                                ABF[:, kc * 512:(kc + 1) * 512],
                                                start=(kc == 0), stop=(kc == 7))
                        for kc in range(8):
                            lh = nxtg[:, kc * 512 + t4 * 128: kc * 512 + t4 * 128 + 128]
                            ins = tensor.matmul(psum[4 + t4][:, :], lh,
                                                ABF[:, 4096 + kc * 512: 4096 + (kc + 1) * 512],
                                                start=(kc == 0), stop=(kc == 7))
                    ins.then_inc(sems["tE"], 1)
                emit("tensor", f2_mm)
                mm_done = bump("tE")

                def f2_sil(scalar, gg=gg, icb=icb, mm=mm_done, pspl=prev2["spl"]):
                    scalar.wait_ge(sems["tE"], mm)
                    if icb == 0:
                        scalar.wait_ge(sems["spl"], pspl)
                    ins = None
                    for t4 in range(4):
                        ins = scalar.activation(AS[:, t4 * 512:(t4 + 1) * 512],
                                                psum[t4][:, :], AF.Silu,
                                                scale=stat[:, 32 + t4:33 + t4])
                    ins.then_inc(sems["sE"], 1)
                emit("scalar", f2_sil)
                sil_done = bump("sE")

                def f2_h(vector, gg=gg, icb=icb, sd=sil_done):
                    vector.wait_ge(sems["sE"], sd)
                    ins = None
                    for t4 in range(4):
                        ins = vector.scalar_tensor_tensor(
                            AW[:, t4 * 4096 + icb * 512: t4 * 4096 + (icb + 1) * 512],
                            psum[4 + t4][:, :], stat[:, 36 + t4:37 + t4],
                            AS[:, t4 * 512:(t4 + 1) * 512], ALU.mult, ALU.mult)
                    ins.then_inc(sems["vE"], 1)
                emit("vector", f2_h)
                prev_sil = bump("vE")
                sil_dones.append(prev_sil)
            h_all = prev_sil

            dn_prev_sE = None
            prev_trev = None
            for t4 in range(4):
                def f2_hq1(vector, gg=gg, t4=t4, ha=h_all):
                    if t4 == 0:
                        vector.wait_ge(sems["vE"], ha)
                    ht = AW[:, t4 * 4096:(t4 + 1) * 4096]
                    vector.tensor_reduce(stat[:, 44:45], ht, AX.X, ALU.max,
                                         apply_absolute_value=True)
                    vector.tensor_copy(padb[:, :], ident[:, :])
                    ins = vector.tensor_scalar(stat[:, 45:46], stat[:, 44:45], 1e-5,
                                               None, ALU.max)
                    ins.then_inc(sems["vE"], 1)
                emit("vector", f2_hq1)
                hq1 = bump("vE")

                def f2_hq2(scalar, v1=hq1):
                    scalar.wait_ge(sems["vE"], v1)
                    scalar.activation(stat[:, 46:47], stat[:, 45:46], AF.Ln)
                    scalar.copy(padb2[:, :], ident[:, :])
                    ins = scalar.activation(stat[:, 47:48], stat[:, 46:47], AF.Exp,
                                            scale=-1.0)
                    ins.then_inc(sems["sE"], 1)
                emit("scalar", f2_hq2)
                hq2 = bump("sE")

                def f2_hq3(vector, gg=gg, t4=t4, s2=hq2):
                    vector.wait_ge(sems["sE"], s2)
                    vector.tensor_scalar(stat[:, 48:49], stat[:, 47:48], 127.0, None, ALU.mult)
                    vector.tensor_copy(padb[:, :], ident[:, :])
                    vector.tensor_scalar(rsh[:, gg * 4 + t4:gg * 4 + t4 + 1],
                                         stat[:, 45:46], 1.0 / 127.0, None, ALU.mult)
                    ht = AW[:, t4 * 4096:(t4 + 1) * 4096]
                    ins = None
                    for q in range(4):
                        hq = ht[:, q * 1024:(q + 1) * 1024]
                        tbuf = AACT[:, 0:1024]
                        kf = AACT[:, 1024:2048]
                        dl = AACT[:, 2048 + q * 1024: 3072 + q * 1024]
                        c1 = AACT[:, 6144:7168]
                        c2 = AACT[:, 7168:8192]
                        vector.tensor_scalar(tbuf, hq, stat[:, 48:49], None, ALU.mult)
                        vector.tensor_copy(AI[:, 0:1024], tbuf)
                        vector.tensor_copy(kf, AI[:, 0:1024])
                        vector.tensor_tensor(dl, tbuf, kf, ALU.subtract)
                        vector.tensor_scalar(c1, dl, 0.5, None, ALU.is_gt)
                        vector.tensor_scalar(c2, dl, -0.5, None, ALU.is_lt)
                        vector.tensor_tensor(c1, c1, c2, ALU.subtract)
                        ins = vector.tensor_tensor(dl, kf, c1, ALU.add)
                    ins.then_inc(sems["vE"], 1)
                emit("vector", f2_hq3)
                hq3 = bump("vE")

                def f2_tr(tensor, t4=t4, hq=hq3, psE=dn_prev_sE, ptv=prev_trev):
                    tensor.wait_ge(sems["vE"], hq)
                    if psE is not None:
                        tensor.wait_ge(sems["sE"], psE)
                    if ptv is not None:
                        tensor.wait_ge(sems["vE"], ptv)
                    ins = None
                    for kc in range(32):
                        ins = tensor.transpose(
                            psum[kc // 4][:, (kc % 4) * 128:(kc % 4) * 128 + 128],
                            AACT[:, 2048 + kc * 128: 2048 + kc * 128 + 128],
                            ident[:, :])
                    ins.then_inc(sems["tE"], 1)
                emit("tensor", f2_tr)
                tr_done = bump("tE")

                def f2_trev(vector, t4=t4, tr=tr_done):
                    vector.wait_ge(sems["tE"], tr)
                    ins = None
                    for bank in range(8):
                        ins = vector.tensor_copy(
                            ABF[:, (t4 % 2) * 4096 + bank * 512:
                                (t4 % 2) * 4096 + (bank + 1) * 512],
                            psum[bank][:, :])
                    ins.then_inc(sems["vE"], 1)
                emit("vector", f2_trev)
                trev_done = bump("vE")
                prev_trev = trev_done

                if t4 % 2 == 1:
                    pr = t4 // 2
                    dn_base = cnt["tE"]
                    for kc in range(32):
                        def f2_lddn(sync, kc=kc, tv=trev_done, base=dn_base):
                            if kc == 0:
                                sync.wait_ge(sems["vE"], tv)
                            if kc >= 2:
                                sync.wait_ge(sems["tE"], base + kc - 1)
                            sync.dma_start(
                                out=ABF[:, 8192 + (kc % 2) * 1024:
                                        8192 + (kc % 2) * 1024 + 1024],
                                in_=wdq_sp[kc, :, :]).then_inc(sems["ldw"], 16)
                        emit("sync", f2_lddn)
                        ld_dn = bump("ldw", 16)

                        def f2_dnmm(tensor, kc=kc, ld=ld_dn, tv=trev_done,
                                    psE=dn_prev_sE):
                            tensor.wait_ge(sems["ldw"], ld)
                            if kc == 0:
                                tensor.wait_ge(sems["vE"], tv)
                                if psE is not None:
                                    tensor.wait_ge(sems["sE"], psE)
                            ins = None
                            for tt in range(2):
                                lh = ABF[:, tt * 4096 + kc * 128: tt * 4096 + kc * 128 + 128]
                                for dh in range(2):
                                    ins = tensor.matmul(
                                        psum[tt * 2 + dh][:, :], lh,
                                        ABF[:, 8192 + (kc % 2) * 1024 + dh * 512:
                                            8192 + (kc % 2) * 1024 + (dh + 1) * 512],
                                        start=(kc == 0), stop=(kc == 31))
                            ins.then_inc(sems["tE"], 1)
                        emit("tensor", f2_dnmm)
                        bump("tE")
                    dn_done = cnt["tE"]

                    def f2_rsd(vector, gg=gg, pr=pr, psE=dn_prev_sE):
                        if psE is not None:
                            vector.wait_ge(sems["sE"], psE)
                        vector.tensor_tensor(stat[:, 40:42],
                                             rsh[:, gg * 4 + pr * 2: gg * 4 + pr * 2 + 2],
                                             wsc[:, 2:3].broadcast_to([128, 2]), ALU.mult)
                        ins = vector.tensor_copy(padb[:, :], ident[:, :])
                        ins.then_inc(sems["vE"], 1)
                    emit("vector", f2_rsd)
                    rsd_done = bump("vE")

                    def f2_gsil(scalar, dn=dn_done, rsd=rsd_done):
                        scalar.wait_ge(sems["tE"], dn)
                        scalar.wait_ge(sems["vE"], rsd)
                        ins = None
                        for tt in range(2):
                            for dh in range(2):
                                ins = scalar.activation(
                                    AS[:, (tt * 2 + dh) * 512:(tt * 2 + dh) * 512 + 512],
                                    psum[tt * 2 + dh][:, :], AF.Silu,
                                    scale=stat[:, 40 + tt:41 + tt])
                        ins.then_inc(sems["sE"], 1)
                    emit("scalar", f2_gsil)
                    gsil_done = bump("sE")
                    dn_prev_sE = gsil_done

                    def f2_gsp(gpsimd, gg=gg, pr=pr, gs=gsil_done):
                        gpsimd.wait_ge(sems["sE"], gs)
                        for tt in range(2):
                            t_abs = gg * 4 + pr * 2 + tt
                            gpsimd.dma_start(out=g_sp[t_abs, :, :],
                                             in_=AS[:, tt * 1024: tt * 1024 + 1024]
                                             ).then_inc(sems["spl"], 16)
                    emit("gpsimd", f2_gsp)
                    bump("spl", 32)

            prev2 = {"vE": cnt["vE"], "spl": cnt["spl"], "sE": cnt["sE"],
                     "tE": cnt["tE"]}

        ph2_spl = cnt["spl"]
        if DEBUG:
            def f2_dbgg(gpsimd, need=ph2_spl):
                gpsimd.wait_ge(sems["spl"], need)
                gpsimd.dma_start(out=ap["dbg_g_t1"], in_=g_sp[1, :, :]).then_inc(sems["spl"], 16)
            emit("gpsimd", f2_dbgg)
            bump("spl", 16)



        ph2_end = {"vE": cnt["vE"], "spl": cnt["spl"], "sE": cnt["sE"],
                   "tE": cnt["tE"]}

        def f3_ldb(sync):
            sync.dma_start(out=btab[:, :], in_=ap["attn_bias"]).then_inc(sems["ldw"], 16)
            sync.dma_start(out=mtab[:, :], in_=ap["attn_mask"]).then_inc(sems["ldw"], 16)
            sync.dma_start(out=ften[:, :], in_=ap["attn_fden"]).then_inc(sems["ldw"], 16)
        emit("sync", f3_ldb)
        btab_ld = bump("ldw", 48)

        def pbuf(slot):
            return AS[:, 4096 + slot * 512: 4096 + slot * 512 + 512]

        def mtab_slice(W, d):
            if W == 512:
                c0 = d * 512
            elif W == 256:
                c0 = 2048 + d * 256
            else:
                c0 = 2560 + d * 128
            return mtab[:, c0:c0 + W]

        bank_free = {0: 0, 1: 0, 2: 0, 3: 0}
        pslot_free = {0: 0, 1: 0}
        obank_free = 0
        rot4 = [0]
        rot2 = [0]
        prev_hg_ldw = 0

        for hg in range(4):
            def f3_ldk(sync, hg=hg, p2t=ph2_end["tE"], p2s=ph2_end["sE"],
                       ptE=cnt["tE"]):
                if hg == 0:
                    sync.wait_ge(sems["tE"], p2t)
                    sync.wait_ge(sems["sE"], p2s)
                else:
                    sync.wait_ge(sems["tE"], ptE)
                for pair in range(2):
                    for sub in range(2):
                        h = hg * 4 + pair * 2 + sub
                        sync.dma_start(
                            out=AW[sub * 64:(sub + 1) * 64,
                                   pair * 4096: pair * 4096 + 4096]
                            .rearrange("p (t f) -> p t f", f=128),
                            in_=kT_sp[h, V0:NV, :, :].rearrange("t q f -> q t f"),
                        ).then_inc(sems["ldw"], 16)
                        sync.dma_start(
                            out=AS[sub * 64:(sub + 1) * 64,
                                   pair * 2048: pair * 2048 + 2048]
                            .rearrange("p (t f) -> p t f", f=128),
                            in_=qT_sp[h, :, :, :].rearrange("t q f -> q t f"),
                        ).then_inc(sems["ldw"], 16)
                for hs in range(4):
                    sync.dma_start(
                        out=AACT[:, 0:32 * 260].rearrange("p (vt c) -> p vt c", c=260)
                        [:, :, hs * 65: hs * 65 + 64],
                        in_=v_sp[V0:NV, :, hg * 256 + hs * 64: hg * 256 + hs * 64 + 64]
                        .rearrange("vt p f -> p vt f"),
                    ).then_inc(sems["ldw"], 16)
            emit("sync", f3_ldk)
            ld_k = bump("ldw", 16 * 12)

            def f3_ones(vector, ld=ld_k):
                vector.wait_ge(sems["ldw"], ld)
                ins = vector.memset(
                    AACT[:, 0:32 * 260].rearrange("p (vt c) -> p vt c", c=260)
                    .rearrange("p vt (hs f) -> p vt hs f", f=65)[:, :, :, 64:65],
                    1.0)
                ins.then_inc(sems["vE"], 1)
            emit("vector", f3_ones)
            ones_done = bump("vE")

            for (h, qc, vjs) in [t for t in ATT_SCHED if t[0] // 4 == hg]:
                W = CHUNKW[h]
                nts = W // 128
                pair, sub = (h % 4) // 2, h % 2
                hs = h % 4
                qs_lo = qc * nts
                av_last = 0
                my_obank_free = obank_free

                for j, vj in enumerate(vjs):
                    bank = rot4[0] % 4
                    rot4[0] += 1
                    pslot = rot2[0] % 2
                    rot2[0] += 1

                    def f3_st(tensor, h=h, vj=vj, bank=bank, pair=pair, sub=sub,
                              W=W, qs_lo=qs_lo, ld=ld_k, bf=bank_free[bank], j=j,
                              od=ones_done):
                        if j == 0:
                            tensor.wait_ge(sems["ldw"], ld)
                            tensor.wait_ge(sems["vE"], od)
                        if bf:
                            tensor.wait_ge(sems["sE"], bf)
                        lh = AW[sub * 64:(sub + 1) * 64,
                                pair * 4096 + (vj - V0) * 128:
                                pair * 4096 + (vj - V0) * 128 + 128]
                        rh = AS[sub * 64:(sub + 1) * 64,
                                pair * 2048 + qs_lo * 128:
                                pair * 2048 + qs_lo * 128 + W]
                        ins = tensor.matmul(psum[bank][:, 0:W], lh, rh,
                                            start=True, stop=True)
                        ins.then_inc(sems["tE"], 1)
                    emit("tensor", f3_st)
                    st_done = bump("tE")

                    def f3_exp(scalar, bank=bank, pslot=pslot, W=W, st=st_done,
                               col=ATT_BIDX[(h, qc, vj)], pf=pslot_free[pslot]):
                        scalar.wait_ge(sems["tE"], st)
                        if pf:
                            scalar.wait_ge(sems["tE"], pf)
                        ins = scalar.activation(pbuf(pslot)[:, 0:W], psum[bank][:, 0:W],
                                                AF.Exp, bias=btab[:, col:col + 1],
                                                scale=0.125)
                        ins.then_inc(sems["sE"], 1)
                    emit("scalar", f3_exp)
                    exp_done = bump("sE")
                    bank_free[bank] = exp_done

                    d = (vj - 32) - qs_lo
                    av_wait_kind = "sE"
                    av_wait_val = exp_done
                    if d >= 0:
                        def f3_mask(vector, pslot=pslot, W=W, d=d, ed=exp_done):
                            vector.wait_ge(sems["sE"], ed)
                            ins = vector.tensor_tensor(pbuf(pslot)[:, 0:W],
                                                       pbuf(pslot)[:, 0:W],
                                                       mtab_slice(W, d), ALU.mult)
                            ins.then_inc(sems["vE"], 1)
                        emit("vector", f3_mask)
                        av_wait_kind = "vE"
                        av_wait_val = bump("vE")

                    def f3_av(tensor, h=h, vj=vj, pslot=pslot, W=W, nts=nts, hs=hs,
                              j=j, last=(j == len(vjs) - 1), wk=av_wait_kind,
                              wv=av_wait_val, ob=my_obank_free):
                        tensor.wait_ge(sems[wk], wv)
                        if j == 0 and ob:
                            tensor.wait_ge(sems["vE"], ob)
                        ins = None
                        for tt in range(nts):
                            lh = pbuf(pslot)[:, tt * 128:(tt + 1) * 128]
                            vt = vj - V0
                            rhv = AACT[:, vt * 260 + hs * 65: vt * 260 + hs * 65 + 65]
                            ins = tensor.matmul(psum[4 + tt][:, 0:65], lh, rhv,
                                                start=(j == 0), stop=last)
                        ins.then_inc(sems["tE"], 1)
                    emit("tensor", f3_av)
                    av_last = bump("tE")
                    pslot_free[pslot] = av_last

                def f3_den_v(vector, nts=nts, av=av_last, h=h, qc=qc):
                    vector.wait_ge(sems["tE"], av)
                    for tt in range(nts):
                        vector.tensor_copy(stat[:, 52 + tt:53 + tt],
                                           psum[4 + tt][:, 64:65])
                    vector.tensor_copy(padb[:, :], ident[:, :])
                    fc = h * 16 + qc * nts
                    ins = vector.tensor_tensor(stat[:, 52:52 + nts],
                                               stat[:, 52:52 + nts],
                                               ften[:, fc:fc + nts], ALU.mult)
                    ins.then_inc(sems["vE"], 1)
                emit("vector", f3_den_v)
                den_v = bump("vE")

                def f3_den_s(scalar, nts=nts, dv=den_v):
                    scalar.wait_ge(sems["vE"], dv)
                    scalar.activation(stat[:, 56:56 + nts], stat[:, 52:52 + nts], AF.Ln)
                    scalar.copy(padb2[:, :], ident[:, :])
                    ins = scalar.activation(stat[:, 60:60 + nts], stat[:, 56:56 + nts],
                                            AF.Exp, scale=-1.0)
                    ins.then_inc(sems["sE"], 1)
                emit("scalar", f3_den_s)
                den_s = bump("sE")

                def f3_den_n(vector, nts=nts, ds=den_s, h=h, qc=qc):
                    vector.wait_ge(sems["sE"], ds)
                    vector.tensor_tensor(stat[:, 44:44 + nts], stat[:, 52:52 + nts],
                                         stat[:, 60:60 + nts], ALU.mult)
                    vector.tensor_copy(padb[:, :], ident[:, :])
                    vector.tensor_scalar(stat[:, 48:48 + nts], stat[:, 44:44 + nts],
                                         -1.0, 2.0, ALU.mult, ALU.add)
                    vector.tensor_copy(padb[:, :], ident[:, :])
                    vector.tensor_tensor(stat[:, 60:60 + nts], stat[:, 60:60 + nts],
                                          stat[:, 48:48 + nts], ALU.mult)
                    vector.tensor_copy(padb[:, :], ident[:, :])
                    fc = h * 16 + qc * nts
                    ins = vector.tensor_tensor(stat[:, 60:60 + nts],
                                               stat[:, 60:60 + nts],
                                               ften[:, fc:fc + nts], ALU.mult)
                    ins.then_inc(sems["vE"], 1)
                emit("vector", f3_den_n)
                den_n = bump("vE")

                def f3_oev(vector, nts=nts, ds=den_n):
                    vector.wait_ge(sems["vE"], ds)
                    ins = None
                    for tt in range(nts):
                        ins = vector.tensor_scalar(aev[:, tt * 64:(tt + 1) * 64],
                                                   psum[4 + tt][:, 0:64],
                                                   stat[:, 60 + tt:61 + tt], None,
                                                   ALU.mult)
                    ins.then_inc(sems["vE"], 1)
                emit("vector", f3_oev)
                obank_free = bump("vE")

                def f3_osp(gpsimd, h=h, qs_lo=qs_lo, nts=nts, oe=obank_free):
                    gpsimd.wait_ge(sems["vE"], oe)
                    gpsimd.dma_start(
                        out=ao_sp[qs_lo:qs_lo + nts, :, h * 64:(h + 1) * 64]
                        .rearrange("t p f -> p t f"),
                        in_=aev[:, 0:nts * 64].rearrange("p (t f) -> p t f", f=64),
                    ).then_inc(sems["spl"], 16)
                emit("gpsimd", f3_osp)
                bump("spl", 16)


        ph3_spl = cnt["spl"]
        if DEBUG:
            def f3_dbg(gpsimd, need=ph3_spl):
                gpsimd.wait_ge(sems["spl"], need)
                gpsimd.dma_start(out=ap["dbg_ao_t2"], in_=ao_sp[2, :, :]).then_inc(sems["spl"], 16)
            emit("gpsimd", f3_dbg)
            bump("spl", 16)

        def f4_ldw(sync, p3=cnt["tE"]):
            sync.wait_ge(sems["tE"], p3)
            for kc in range(8):
                sync.dma_start(out=AW[:, kc * 1024:(kc + 1) * 1024],
                               in_=ap["wout"][kc * 128:(kc + 1) * 128, :]).then_inc(sems["ldw"], 16)
        emit("sync", f4_ldw)
        wout_ld = bump("ldw", 16 * 8)

        prev4 = {"vE": cnt["vE"], "spl": cnt["spl"]}
        x1_prev = 0
        for t in range(NT_OWN):
            def f4_lda(sync, t=t, p=dict(prev4), spl_need=cnt["spl"], xp=x1_prev):
                sync.wait_ge(sems["spl"], spl_need)
                if t > 0:
                    sync.wait_ge(sems["vE"], xp)
                sync.dma_start(out=AS[:, 0:1024], in_=ao_sp[t, :, :]).then_inc(sems["ldx"], 16)
                sync.dma_start(out=AS[:, 2048:3072],
                               in_=ap["xv"][(OWN0 + t) * 128:(OWN0 + t + 1) * 128, :]
                               ).then_inc(sems["ldx"], 16)
                sync.dma_start(out=AS[:, 3072:4096], in_=g_sp[t, :, :]).then_inc(sems["ldx"], 16)
            emit("sync", f4_lda)
            lda = bump("ldx", 48)

            def f4_tr(tensor, t=t, ld=lda, wl=wout_ld, pv=prev4["vE"]):
                tensor.wait_ge(sems["ldx"], ld)
                if t == 0:
                    tensor.wait_ge(sems["ldw"], wl)
                    tensor.wait_ge(sems["vE"], pv)
                ins = None
                for kc in range(8):
                    ins = tensor.transpose(psum[kc // 4][:, (kc % 4) * 128:(kc % 4) * 128 + 128],
                                           AS[:, kc * 128:kc * 128 + 128], ident[:, :])
                ins.then_inc(sems["tE"], 1)
            emit("tensor", f4_tr)
            tr4 = bump("tE")

            def f4_trev(vector, t=t, tr=tr4):
                vector.wait_ge(sems["tE"], tr)
                ins = None
                for half in range(2):
                    ins = vector.tensor_copy(AS[:, 1024 + half * 512: 1024 + half * 512 + 512],
                                             psum[half][:, :])
                ins.then_inc(sems["vE"], 1)
            emit("vector", f4_trev)
            trev4 = bump("vE")

            def f4_mm(tensor, t=t, tv=trev4):
                tensor.wait_ge(sems["vE"], tv)
                ins = None
                for oc in range(2):
                    for kc in range(8):
                        ins = tensor.matmul(psum[2 + oc][:, :],
                                            AS[:, 1024 + kc * 128:1024 + kc * 128 + 128],
                                            AW[:, kc * 1024 + oc * 512: kc * 1024 + oc * 512 + 512],
                                            start=(kc == 0), stop=(kc == 7))
                ins.then_inc(sems["tE"], 1)
            emit("tensor", f4_mm)
            smm = bump("tE")

            def f4_gate(vector, t=t, sm=smm):
                vector.wait_ge(sems["tE"], sm)
                for oc in range(2):
                    vector.tensor_tensor(AS[:, 4096 + oc * 512: 4096 + oc * 512 + 512],
                                         psum[2 + oc][:, :],
                                         AS[:, 3072 + oc * 512: 3072 + oc * 512 + 512],
                                         ALU.mult)
                ins = None
                for oc in range(2):
                    ins = vector.tensor_tensor(AS[:, 5120 + oc * 512: 5120 + oc * 512 + 512],
                                               AS[:, 4096 + oc * 512: 4096 + oc * 512 + 512],
                                               AS[:, 2048 + oc * 512: 2048 + oc * 512 + 512],
                                               ALU.add)
                ins.then_inc(sems["vE"], 1)
            emit("vector", f4_gate)
            x1_done = bump("vE")
            x1_prev = x1_done

            def f4_sp(gpsimd, t=t, xd=x1_done):
                gpsimd.wait_ge(sems["vE"], xd)
                gpsimd.dma_start(out=x1_sp[t, :, :], in_=AS[:, 5120:6144]).then_inc(sems["spl"], 16)
            emit("gpsimd", f4_sp)
            bump("spl", 16)

        ph4_spl = cnt["spl"]
        if DEBUG:
            def f4_dbg(gpsimd, need=ph4_spl):
                gpsimd.wait_ge(sems["spl"], need)
                gpsimd.dma_start(out=ap["dbg_x1_t3"], in_=x1_sp[3, :, :]).then_inc(sems["spl"], 16)
            emit("gpsimd", f4_dbg)
            bump("spl", 16)


        ph4_end = {"vE": cnt["vE"], "spl": cnt["spl"], "sE": cnt["sE"],
                   "tE": cnt["tE"]}
        uln_prev_v = 0
        for t in range(NT_OWN):
            def f5_ld(sync, t=t, p4=ph4_end["spl"], spl_need=cnt["spl"], pv=uln_prev_v):
                sync.wait_ge(sems["spl"], spl_need)
                if pv:
                    sync.wait_ge(sems["vE"], pv)
                sync.dma_start(out=AS[:, 0:1024], in_=x1_sp[t, :, :]).then_inc(sems["ldx"], 16)
            emit("sync", f5_ld)
            ld5 = bump("ldx", 16)

            def f5_v1(vector, t=t, ld=ld5):
                vector.wait_ge(sems["ldx"], ld)
                vector.tensor_reduce(stat[:, 0:1], AS[:, 0:1024], AX.X, ALU.add)
            emit("vector", f5_v1)

            def f5_s1(scalar, t=t, ld=ld5):
                scalar.wait_ge(sems["ldx"], ld)
                ins = scalar.activation(AS[:, 1024:2048], AS[:, 0:1024], AF.Square)
                ins.then_inc(sems["sE"], 1)
            emit("scalar", f5_s1)
            sq5 = bump("sE")

            def f5_v2(vector, t=t, sq=sq5):
                vector.wait_ge(sems["sE"], sq)
                vector.tensor_reduce(stat[:, 1:2], AS[:, 1024:2048], AX.X, ALU.add)
                vector.tensor_copy(padb[:, :], ident[:, :])
                vector.tensor_scalar(stat[:, 4:6], stat[:, 0:2], 1.0 / D, None, ALU.mult)
                vector.tensor_copy(padb[:, :], ident[:, :])
                vector.tensor_tensor(stat[:, 8:9], stat[:, 4:5], stat[:, 4:5], ALU.mult)
                vector.tensor_copy(padb[:, :], ident[:, :])
                vector.tensor_tensor(stat[:, 9:10], stat[:, 5:6], stat[:, 8:9], ALU.subtract)
                vector.tensor_copy(padb[:, :], ident[:, :])
                ins = vector.tensor_scalar(stat[:, 10:11], stat[:, 9:10], 1e-5, None, ALU.add)
                ins.then_inc(sems["vE"], 1)
            emit("vector", f5_v2)
            var5 = bump("vE")

            def f5_s2(scalar, t=t, vd=var5):
                scalar.wait_ge(sems["vE"], vd)
                scalar.activation(stat[:, 11:12], stat[:, 10:11], AF.Ln)
                scalar.copy(padb2[:, :], ident[:, :])
                ins = scalar.activation(stat[:, 12:13], stat[:, 11:12], AF.Exp, scale=-0.5)
                ins.then_inc(sems["sE"], 1)
            emit("scalar", f5_s2)
            rstd5 = bump("sE")

            def f5_v3(vector, t=t, rs=rstd5):
                vector.wait_ge(sems["sE"], rs)
                vector.tensor_tensor(stat[:, 13:14], stat[:, 4:5], stat[:, 12:13], ALU.mult)
                vector.tensor_copy(padb[:, :], ident[:, :])
                vector.tensor_scalar(stat[:, 14:15], stat[:, 13:14], -1.0, None, ALU.mult)
                vector.tensor_copy(padb[:, :], ident[:, :])
                ins = vector.tensor_scalar(AS[:, 1024:2048], AS[:, 0:1024],
                                           stat[:, 12:13], stat[:, 14:15],
                                           ALU.mult, ALU.add)
                ins.then_inc(sems["vE"], 1)
            emit("vector", f5_v3)
            uln_prev_v = bump("vE")

            def f5_sp(gpsimd, t=t, ud=uln_prev_v):
                gpsimd.wait_ge(sems["vE"], ud)
                gpsimd.dma_start(out=u_sp[t, :, :], in_=AS[:, 1024:2048]).then_inc(sems["spl"], 16)
            emit("gpsimd", f5_sp)
            bump("spl", 16)

        ph5_spl = cnt["spl"]
        if DEBUG:
            def f5_dbg(gpsimd, need=ph5_spl):
                gpsimd.wait_ge(sems["spl"], need)
                gpsimd.dma_start(out=ap["dbg_u_t2"], in_=u_sp[2, :, :]).then_inc(sems["spl"], 16)
            emit("gpsimd", f5_dbg)
            bump("spl", 16)

        def emit_fp4_quant(src_ap, nblk, Wd, xq_ap, sq_wait=None, name=""):
            def q_s1(scalar, src=src_ap, w=sq_wait):
                if w is not None:
                    scalar.wait_ge(sems[w[0]], w[1])
                ins = scalar.activation(AACT[:, 0:Wd], src, AF.Abs)
                ins.then_inc(sems["sE"], 1)
            emit("scalar", q_s1)
            abs_done = bump("sE")

            def q_v1(vector, nblk=nblk, Wd=Wd, ad=abs_done):
                vector.wait_ge(sems["sE"], ad)
                v3 = AACT[:, 0:Wd].rearrange("p (a b) -> p a b", b=16)
                vector.tensor_reduce(blkq[:, 0:nblk], v3, AX.X, ALU.max)
                vector.tensor_copy(padb[:, :], ident[:, :])
                ins = vector.tensor_scalar(blkq[:, 0:nblk], blkq[:, 0:nblk],
                                           1.0 / 6.0, 1e-12, ALU.mult, ALU.max)
                ins.then_inc(sems["vE"], 1)
            emit("vector", q_v1)
            bm_done = bump("vE")

            def q_s2(scalar, nblk=nblk, bd=bm_done):
                scalar.wait_ge(sems["vE"], bd)
                scalar.activation(blkq[:, 192:192 + nblk], blkq[:, 0:nblk], AF.Ln)
                scalar.copy(padb2[:, :], ident[:, :])
                ins = scalar.activation(blkq[:, 384:384 + nblk], blkq[:, 192:192 + nblk],
                                        AF.Exp, scale=-1.0)
                ins.then_inc(sems["sE"], 1)
            emit("scalar", q_s2)
            inv_done = bump("sE")

            def q_s3(scalar, src=src_ap, Wd=Wd):
                ins = scalar.activation(AACT[:, 8192:8192 + Wd], src, AF.Sign)
                ins.then_inc(sems["sE"], 1)
            emit("scalar", q_s3)
            sgn_done = bump("sE")

            def q_v2(vector, nblk=nblk, Wd=Wd, iv=inv_done, sg=sgn_done,
                     xq=xq_ap):
                vector.wait_ge(sems["sE"], iv)
                a3 = AACT[:, 0:Wd].rearrange("p (a b) -> p a b", b=16)
                u3 = AACT[:, 4096:4096 + Wd].rearrange("p (a b) -> p a b", b=16)
                iv3 = blkq[:, 384:384 + nblk].rearrange("p (a b) -> p a b", b=1)
                vector.tensor_tensor(u3, a3, iv3.broadcast_to([128, nblk, 16]), ALU.mult)
                uu = AACT[:, 4096:4096 + Wd]
                acc = AACT[:, 0:Wd]
                tmp = AW[:, 12288:12288 + Wd]
                vector.tensor_scalar(acc, uu, 0.25, None, ALU.is_gt)
                vector.tensor_scalar(tmp, uu, 0.75, None, ALU.is_gt)
                vector.tensor_tensor(acc, acc, tmp, ALU.add)
                vector.tensor_scalar(tmp, uu, 1.25, None, ALU.is_gt)
                vector.tensor_tensor(acc, acc, tmp, ALU.add)
                vector.tensor_scalar(tmp, uu, 1.75, None, ALU.is_gt)
                vector.tensor_tensor(acc, acc, tmp, ALU.add)
                vector.tensor_scalar(acc, acc, 0.5, None, ALU.mult)
                vector.tensor_scalar(tmp, uu, 2.5, None, ALU.is_gt)
                vector.tensor_tensor(acc, acc, tmp, ALU.add)
                vector.tensor_scalar(tmp, uu, 3.5, None, ALU.is_gt)
                vector.tensor_tensor(acc, acc, tmp, ALU.add)
                vector.tensor_scalar(tmp, uu, 5.0, 2.0, ALU.is_gt, ALU.mult)
                vector.tensor_tensor(acc, acc, tmp, ALU.add)
                sc3 = blkq[:, 0:nblk].rearrange("p (a b) -> p a b", b=1)
                a3b = acc.rearrange("p (a b) -> p a b", b=16)
                vector.tensor_tensor(a3b, a3b, sc3.broadcast_to([128, nblk, 16]), ALU.mult)
                vector.wait_ge(sems["sE"], sg)
                ins = vector.tensor_tensor(xq, acc, AACT[:, 8192:8192 + Wd], ALU.mult)
                ins.then_inc(sems["vE"], 1)
            emit("vector", q_v2)
            return bump("vE")

        ph6_start = {"vE": cnt["vE"], "spl": cnt["spl"], "tE": cnt["tE"],
                     "sE": cnt["sE"]}
        agi_off = [0]
        q_results = []
        for ci in range(7):
            Wd = 1024 if ci < 6 else INNER1
            nblk = Wd // 16

            def f6_ldc(sync, ci=ci, Wd=Wd, p6=ph6_start["tE"], pv=q_results[-1] if q_results else 0):
                if ci == 0:
                    sync.wait_ge(sems["tE"], p6)
                if pv:
                    sync.wait_ge(sems["vE"], pv)
                sync.dma_start(out=AW[:, 0:Wd],
                               in_=ap["w1shard"][ci, :, 0:Wd]).then_inc(sems["ldw"], 16)
            emit("sync", f6_ldc)
            ldc = bump("ldw", 16)

            qv = emit_fp4_quant(AW[:, 0:Wd], nblk, Wd, AW[:, 4096:4096 + Wd],
                                sq_wait=("ldw", ldc), name=f"w1c{ci}")
            q_results.append(qv)

            def f6_spc(gpsimd, ci=ci, Wd=Wd, qv=qv, off=agi_off[0]):
                gpsimd.wait_ge(sems["vE"], qv)
                gpsimd.dma_start(out=w1agi[off:off + 128 * Wd]
                                 .rearrange("(p f) -> p f", p=128),
                                 in_=AW[:, 4096:4096 + Wd]).then_inc(sems["spl"], 16)
            emit("gpsimd", f6_spc)
            bump("spl", 16)
            agi_off[0] += 128 * Wd

        w1q_spl = cnt["spl"]

        def f6_ag(gpsimd, need=w1q_spl):
            gpsimd.wait_ge(sems["spl"], need)
            ins = gpsimd.collective_compute(
                "AllGather", ALU.bypass,
                replica_groups=[list(range(NCORES))],
                ins=[w1agi.opt()],
                outs=[w1ago.opt()],
            )
            ins.then_inc(sems["spl"], 1)
        emit("gpsimd", f6_ag)
        ag_done = bump("spl", 1)

        RANKB = 6 * 128 * 1024 + 128 * INNER1

        def w1_src(rank, slot, Wd):
            off = rank * RANKB
            for s2 in range(slot):
                off += 128 * (1024 if s2 < 6 else INNER1)
            return w1ago[off:off + 128 * Wd].rearrange("(p f) -> p f", p=128)

        tr_jobs = []
        for m, tgt in ((0, wg1q_sp), (1, wu1q_sp)):
            for c in range(22):
                rank, sl = c // 3, (c % 3) + m * 3
                tr_jobs.append((tgt, c, rank, sl, 1024))
        for r in range(8):
            tr_jobs.append((wd1q_sp, r, r, 6, INNER1))

        prev_trv = 0
        prev_tsp = 0
        for (tgt, c, rank, sl, Wd) in tr_jobs:
            nj = Wd // 128
            def f6_ldt(sync, rank=rank, sl=sl, Wd=Wd, agd=ag_done, ptv=prev_trv):
                sync.wait_ge(sems["spl"], agd)
                if ptv:
                    sync.wait_ge(sems["vE"], ptv)
                sync.dma_start(out=AW[:, 0:Wd], in_=w1_src(rank, sl, Wd)
                               ).then_inc(sems["ldw"], 16)
            emit("sync", f6_ldt)
            ldt = bump("ldw", 16)

            def f6_trw(tensor, Wd=Wd, nj=nj, ld=ldt, ptv=prev_trv):
                tensor.wait_ge(sems["ldw"], ld)
                if ptv:
                    tensor.wait_ge(sems["vE"], ptv)
                ins = None
                for j in range(nj):
                    ins = tensor.transpose(
                        psum[j % 6][:, (j // 6 % 4) * 128:(j // 6 % 4) * 128 + 128],
                        AW[:, j * 128:(j + 1) * 128], ident[:, :])
                ins.then_inc(sems["tE"], 1)
            emit("tensor", f6_trw)
            trw = bump("tE")

            def f6_tev(vector, Wd=Wd, nj=nj, tr=trw, psp=prev_tsp):
                vector.wait_ge(sems["tE"], tr)
                if psp:
                    vector.wait_ge(sems["spl"], psp)
                for j in range(nj):
                    vector.tensor_copy(AW[:, 4096 + j * 128: 4096 + (j + 1) * 128],
                                       psum[j % 6][:, (j // 6 % 4) * 128:(j // 6 % 4) * 128 + 128])
                vector.tensor_copy(ABF[:, 0:Wd], AW[:, 4096:4096 + Wd])
                vector.tensor_copy(padb[:, :], ident[:, :])
                vector.tensor_tensor(AW[:, 8192:8192 + Wd], AW[:, 4096:4096 + Wd],
                                     ABF[:, 0:Wd], ALU.subtract)
                vector.tensor_copy(padb[:, :], ident[:, :])
                ins = vector.tensor_copy(ABF[:, 4096:4096 + Wd], AW[:, 8192:8192 + Wd])
                ins.then_inc(sems["vE"], 1)
            emit("vector", f6_tev)
            prev_trv = bump("vE")

            def f6_tsp(gpsimd, tgt=tgt, c=c, Wd=Wd, nj=nj, tv=prev_trv):
                gpsimd.wait_ge(sems["vE"], tv)
                for hilo in range(2):
                    gpsimd.dma_start(
                        out=tgt[hilo, 0:nj, :, c * 128:(c + 1) * 128]
                        .rearrange("j p f -> p j f"),
                        in_=ABF[:, hilo * 4096: hilo * 4096 + Wd]
                        .rearrange("p (j f) -> p j f", f=128),
                    ).then_inc(sems["spl"], 16)
            emit("gpsimd", f6_tsp)
            prev_tsp = bump("spl", 32)

        w1t_spl = cnt["spl"]
        prev_w_use = 0
        prev_dn_use = 0
        prev_dn_v = 0
        prev_hsp = 0

        prev6 = {"vE": cnt["vE"], "spl": cnt["spl"], "sE": cnt["sE"], "tE": cnt["tE"]}
        ICB = [(i * 512, 512) for i in range(5)] + [(2560, 256)]
        for gg in range(4):
            xq_evs = []
            prev_xtr = prev6["vE"] if gg == 0 else prev_dn_v
            for t4 in range(4):
                t = gg * 4 + t4

                def f6_ldu(sync, t=t, pv=prev_xtr, sneed=cnt["spl"]):
                    sync.wait_ge(sems["spl"], sneed)
                    if pv:
                        sync.wait_ge(sems["vE"], pv)
                    sync.dma_start(out=AS[:, 0:1024], in_=u_sp[t, :, :]).then_inc(sems["ldx"], 16)
                emit("sync", f6_ldu)
                ldu = bump("ldx", 16)

                qv = emit_fp4_quant(AS[:, 0:1024], 64, 1024, AS[:, 1024:2048],
                                    sq_wait=("ldx", ldu), name=f"u{t}")

                def f6_xtr(tensor, qv=qv, pv=prev_xtr):
                    tensor.wait_ge(sems["vE"], qv)
                    if pv:
                        tensor.wait_ge(sems["vE"], pv)
                    ins = None
                    for j in range(8):
                        ins = tensor.transpose(psum[j // 4][:, (j % 4) * 128:(j % 4) * 128 + 128],
                                               AS[:, 1024 + j * 128:1024 + (j + 1) * 128],
                                               ident[:, :])
                    ins.then_inc(sems["tE"], 1)
                emit("tensor", f6_xtr)
                xtr = bump("tE")

                def f6_xev(vector, t4=t4, tr=xtr):
                    vector.wait_ge(sems["tE"], tr)
                    for half in range(2):
                        vector.tensor_copy(AS[:, 2048 + half * 512:2048 + half * 512 + 512],
                                           psum[half][:, :])
                    vector.tensor_copy(ABF[:, t4 * 2048: t4 * 2048 + 1024], AS[:, 2048:3072])
                    vector.tensor_copy(padb[:, :], ident[:, :])
                    vector.tensor_tensor(AS[:, 0:1024], AS[:, 2048:3072],
                                         ABF[:, t4 * 2048: t4 * 2048 + 1024], ALU.subtract)
                    vector.tensor_copy(padb[:, :], ident[:, :])
                    ins = vector.tensor_copy(ABF[:, t4 * 2048 + 1024: t4 * 2048 + 2048],
                                             AS[:, 0:1024])
                    ins.then_inc(sems["vE"], 1)
                emit("vector", f6_xev)
                prev_xtr = bump("vE")
                xq_evs.append(prev_xtr)

                if DEBUG and t == 2:
                    def f6_dbgx(gpsimd, xe=qv):
                        gpsimd.wait_ge(sems["vE"], xe)
                        gpsimd.dma_start(out=ap["dbg_xq_t2"],
                                         in_=AS[:, 1024:2048]).then_inc(sems["spl"], 16)
                    emit("gpsimd", f6_dbgx)
                    bump("spl", 16)

            prev_hev = 0
            for (ic0, icw) in ICB:
                for mat, tgt_sp in ((0, wg1q_sp), (1, wu1q_sp)):
                    for hilo_pass in range(2):
                        def f6_ldw1(sync, tgt_sp=tgt_sp, hl=hilo_pass, ic0=ic0,
                                    icw=icw, pw=prev_w_use, xe=xq_evs[-1]):
                            sync.wait_ge(sems["vE"], xe)
                            if pw:
                                sync.wait_ge(sems["tE"], pw)
                            sync.dma_start(
                                out=nxtg[:, 0:8 * icw].rearrange("p (kc w) -> p kc w", w=icw),
                                in_=tgt_sp[hl, :, :, ic0:ic0 + icw]
                                .rearrange("kc p w -> p kc w"),
                            ).then_inc(sems["ldw"], 16)
                        emit("sync", f6_ldw1)
                        ldw1 = bump("ldw", 16)

                        def f6_gmm(tensor, mat=mat, hl=hilo_pass, ic0=ic0, icw=icw,
                                   ld=ldw1, phe=prev_hev, first=(hilo_pass == 0)):
                            tensor.wait_ge(sems["ldw"], ld)
                            if first and phe:
                                tensor.wait_ge(sems["vE"], phe)
                            ins = None
                            for t4 in range(4):
                                pb = psum[mat * 4 + t4]
                                if hl == 0:
                                    for xs_ in range(2):
                                        for kc in range(8):
                                            lh = ABF[:, t4 * 2048 + xs_ * 1024 + kc * 128:
                                                     t4 * 2048 + xs_ * 1024 + kc * 128 + 128]
                                            ins = tensor.matmul(
                                                pb[:, 0:icw], lh,
                                                nxtg[:, kc * icw:(kc + 1) * icw],
                                                start=(xs_ == 0 and kc == 0), stop=False)
                                else:
                                    for kc in range(8):
                                        lh = ABF[:, t4 * 2048 + kc * 128:
                                                 t4 * 2048 + kc * 128 + 128]
                                        ins = tensor.matmul(
                                            pb[:, 0:icw], lh,
                                            nxtg[:, kc * icw:(kc + 1) * icw],
                                            start=False, stop=(kc == 7))
                            ins.then_inc(sems["tE"], 1)
                        emit("tensor", f6_gmm)
                        prev_w_use = bump("tE")
                gu_done = prev_w_use

                def f6_sil1(scalar, ic0=ic0, icw=icw, gd=gu_done):
                    scalar.wait_ge(sems["tE"], gd)
                    ins = None
                    for t4 in range(4):
                        ins = scalar.activation(AACT[:, t4 * 512:t4 * 512 + icw],
                                                psum[t4][:, 0:icw], AF.Silu)
                    ins.then_inc(sems["sE"], 1)
                emit("scalar", f6_sil1)
                sil1 = bump("sE")

                def f6_h2(vector, ic0=ic0, icw=icw, sd=sil1):
                    vector.wait_ge(sems["sE"], sd)
                    ins = None
                    for t4 in range(4):
                        ins = vector.tensor_tensor(
                            AW[:, t4 * 2816 + ic0: t4 * 2816 + ic0 + icw],
                            psum[4 + t4][:, 0:icw],
                            AACT[:, t4 * 512:t4 * 512 + icw], ALU.mult)
                    ins.then_inc(sems["vE"], 1)
                emit("vector", f6_h2)
                prev_hev = bump("vE")

            prev_htr = prev_hev
            for t4 in range(4):
                t = gg * 4 + t4
                qv2 = emit_fp4_quant(AW[:, t4 * 2816:(t4 + 1) * 2816], 176, INNER1,
                                     AS[:, 0:2816],
                                     sq_wait=("vE", prev_htr), name=f"h2q{t}")

                def f6_htr(tensor, qv=qv2, pv=prev_htr):
                    tensor.wait_ge(sems["vE"], qv)
                    ins = None
                    for j in range(22):
                        ins = tensor.transpose(
                            psum[j % 6][:, (j // 6 % 4) * 128:(j // 6 % 4) * 128 + 128],
                            AS[:, j * 128:(j + 1) * 128], ident[:, :])
                    ins.then_inc(sems["tE"], 1)
                emit("tensor", f6_htr)
                htr = bump("tE")

                def f6_hev(vector, tr=htr, psp=prev_hsp):
                    vector.wait_ge(sems["tE"], tr)
                    if psp:
                        vector.wait_ge(sems["spl"], psp)
                    for j in range(22):
                        vector.tensor_copy(AS[:, j * 128:(j + 1) * 128],
                                           psum[j % 6][:, (j // 6 % 4) * 128:(j // 6 % 4) * 128 + 128])
                    vector.tensor_copy(ABF[:, 0:2816], AS[:, 0:2816])
                    vector.tensor_copy(padb[:, :], ident[:, :])
                    vector.tensor_tensor(AS[:, 2816:5632], AS[:, 0:2816],
                                         ABF[:, 0:2816], ALU.subtract)
                    vector.tensor_copy(padb[:, :], ident[:, :])
                    ins = vector.tensor_copy(ABF[:, 2816:5632], AS[:, 2816:5632])
                    ins.then_inc(sems["vE"], 1)
                emit("vector", f6_hev)
                hev = bump("vE")
                prev_htr = hev

                def f6_hsp(gpsimd, t=t, he=hev):
                    gpsimd.wait_ge(sems["vE"], he)
                    for hilo in range(2):
                        gpsimd.dma_start(
                            out=hq_sp[hilo, 0:22, :, t * 128:(t + 1) * 128]
                            .rearrange("j p f -> p j f"),
                            in_=ABF[:, hilo * 2816: hilo * 2816 + 2816]
                            .rearrange("p (j f) -> p j f", f=128),
                        ).then_inc(sems["spl"], 16)
                emit("gpsimd", f6_hsp)
                prev_hsp = bump("spl", 32)

            def f6_lddn(sync, gg=gg, hs=prev_hsp, pdu=prev_dn_use):
                sync.wait_ge(sems["spl"], hs)
                if pdu:
                    sync.wait_ge(sems["tE"], pdu)
                pass
            dn_base = cnt["tE"]
            for kc in range(22):
                def f6_ldr(sync, gg=gg, kc=kc, hs=prev_hsp, base=dn_base,
                           pdu=prev_dn_use):
                    if kc == 0:
                        sync.wait_ge(sems["spl"], hs)
                        if pdu:
                            sync.wait_ge(sems["tE"], pdu)
                    if kc >= 2:
                        sync.wait_ge(sems["tE"], base + kc - 1)
                    for hilo in range(2):
                        sync.dma_start(
                            out=ABF[:, (kc % 2) * 5120 + hilo * 512:
                                    (kc % 2) * 5120 + hilo * 512 + 512],
                            in_=hq_sp[hilo, kc, :, gg * 512:(gg + 1) * 512],
                        ).then_inc(sems["ldw"], 16)
                    for hilo in range(2):
                        sync.dma_start(
                            out=ABF[:, (kc % 2) * 5120 + 1024 + hilo * 1024:
                                    (kc % 2) * 5120 + 1024 + hilo * 1024 + 1024],
                            in_=wd1q_sp[hilo, kc, :, :],
                        ).then_inc(sems["ldw"], 16)
                emit("sync", f6_ldr)
                ldr = bump("ldw", 64)

                def f6_dmm(tensor, kc=kc, ld=ldr, phe=prev_htr):
                    tensor.wait_ge(sems["ldw"], ld)
                    if kc == 0:
                        tensor.wait_ge(sems["vE"], phe)
                    b0 = (kc % 2) * 5120
                    ins = None
                    for oc in range(8):
                        pb = psum[oc]
                        ins = tensor.matmul(pb[:, :],
                                            ABF[:, b0 + 1024 + oc * 128: b0 + 1024 + oc * 128 + 128],
                                            ABF[:, b0 + 0: b0 + 512],
                                            start=(kc == 0), stop=False)
                        ins = tensor.matmul(pb[:, :],
                                            ABF[:, b0 + 1024 + oc * 128: b0 + 1024 + oc * 128 + 128],
                                            ABF[:, b0 + 512: b0 + 1024],
                                            start=False, stop=False)
                        ins = tensor.matmul(pb[:, :],
                                            ABF[:, b0 + 2048 + oc * 128: b0 + 2048 + oc * 128 + 128],
                                            ABF[:, b0 + 0: b0 + 512],
                                            start=False, stop=(kc == 21))
                    ins.then_inc(sems["tE"], 1)
                emit("tensor", f6_dmm)
                bump("tE")
            dn_done = cnt["tE"]
            prev_dn_use = dn_done

            def f6_sil2(scalar, dn=dn_done):
                scalar.wait_ge(sems["tE"], dn)
                ins = None
                for oc in range(8):
                    ins = scalar.activation(AACT[:, oc * 512:(oc + 1) * 512],
                                            psum[oc][:, :], AF.Silu)
                ins.then_inc(sems["sE"], 1)
            emit("scalar", f6_sil2)
            sil2 = bump("sE")

            prev_gev = 0
            for oc in range(2):
                def f6_gtr(tensor, oc=oc, sd=sil2, pg=prev_gev):
                    tensor.wait_ge(sems["sE"], sd)
                    if pg:
                        tensor.wait_ge(sems["vE"], pg)
                    ins = None
                    for o4 in range(4):
                        occ = oc * 4 + o4
                        for t4 in range(4):
                            ins = tensor.transpose(
                                psum[o4][:, t4 * 128: t4 * 128 + 128],
                                AACT[:, occ * 512 + t4 * 128: occ * 512 + t4 * 128 + 128],
                                ident[:, :])
                    ins.then_inc(sems["tE"], 1)
                emit("tensor", f6_gtr)
                gtr = bump("tE")

                def f6_gev(vector, oc=oc, tr=gtr):
                    vector.wait_ge(sems["tE"], tr)
                    ins = None
                    for o4 in range(4):
                        occ = oc * 4 + o4
                        for t4 in range(4):
                            ins = vector.tensor_copy(
                                AS[:, t4 * 1024 + occ * 128: t4 * 1024 + occ * 128 + 128],
                                psum[o4][:, t4 * 128: t4 * 128 + 128])
                    ins.then_inc(sems["vE"], 1)
                emit("vector", f6_gev)
                prev_gev = bump("vE")
            prev_dn_v = prev_gev

            def f6_gsp(gpsimd, gg=gg, gv=prev_dn_v):
                gpsimd.wait_ge(sems["vE"], gv)
                for t4 in range(4):
                    gpsimd.dma_start(out=g2_sp[gg * 4 + t4, :, :],
                                     in_=AS[:, t4 * 1024:(t4 + 1) * 1024]
                                     ).then_inc(sems["spl"], 16)
            emit("gpsimd", f6_gsp)
            prev_g2sp = bump("spl", 64)

        ph6_spl = cnt["spl"]
        if DEBUG:
            def f6_dbgg(gpsimd, need=ph6_spl):
                gpsimd.wait_ge(sems["spl"], need)
                gpsimd.dma_start(out=ap["dbg_g2_t1"], in_=g2_sp[1, :, :]).then_inc(sems["spl"], 16)
            emit("gpsimd", f6_dbgg)
            bump("spl", 16)

        bar = {k: cnt[k] for k in ("tE", "vE", "sE", "spl", "ldx", "ldw")}
        PI = math.pi

        def PS(i, n=1):
            return AACT[:, 4096 + 256 * i:4096 + 256 * (i + n)]

        PRK_LQR, PRK_LQI, PRK_CLR, PRK_CLI = 12032, 12288, 12544, 12800

        def p7_ld(sync, b=bar):
            sync.wait_ge(sems["tE"], b["tE"])
            sync.wait_ge(sems["vE"], b["vE"])
            sync.wait_ge(sems["sE"], b["sE"])
            sync.wait_ge(sems["spl"], b["spl"])
            sync.dma_start(out=PS(0), in_=ap["ssm_AR2"]).then_inc(sems["ldw"], 16)
            sync.dma_start(out=PS(1), in_=ap["ssm_AI2"]).then_inc(sems["ldw"], 16)
            sync.dma_start(out=PS(2), in_=ap["ssm_CR2"]).then_inc(sems["ldw"], 16)
            sync.dma_start(out=PS(3), in_=ap["ssm_CI2"]).then_inc(sems["ldw"], 16)
            sync.dma_start(out=PS(4), in_=ap["ssm_LD2"]).then_inc(sems["ldw"], 16)
            sync.dma_start(out=mtab[0:1, 0:1024], in_=ap["ssm_aim1"]).then_inc(sems["ldw"], 16)
            sync.dma_start(out=mtab[0:1, 1024:1536],
                           in_=ap["ssm_lrows"].rearrange("a b -> (a b)")
                           .rearrange("(p f) -> p f", p=1)).then_inc(sems["ldw"], 16)
            sync.dma_start(out=mtab[0:1, 1536:1537], in_=ap["ssm_D"]).then_inc(sems["ldw"], 16)
            sync.dma_start(out=mtab[0:1, 1537:2561], in_=ap["ssm_logdt"]).then_inc(sems["ldw"], 16)
            sync.dma_start(out=AW[0:1, 3072:4096], in_=ap["ssm_are0"]).then_inc(sems["ldw"], 16)
            sync.dma_start(out=ften[:, 0:4], in_=ap["ssm_h4mask"]).then_inc(sems["ldw"], 16)
            sync.dma_start(out=blkq[:, 0:128], in_=ap["F_re"][:, 0:128]).then_inc(sems["ldw"], 16)
            sync.dma_start(out=blkq[:, 128:256], in_=ap["F_im"][:, 0:128]).then_inc(sems["ldw"], 16)
            sync.dma_start(out=blkq[:, 256:257], in_=ap["ssm_flast"]).then_inc(sems["ldw"], 16)
            sync.dma_start(out=blkq[:, 258:386], in_=ap["G_re"][0:128, :]).then_inc(sems["ldw"], 16)
            sync.dma_start(out=blkq[:, 386:514], in_=ap["G_im"][0:128, :]).then_inc(sems["ldw"], 16)
            sync.dma_start(out=aev[0:1, 0:128], in_=ap["G_re"][128:129, :]).then_inc(sems["ldw"], 16)
        emit("sync", p7_ld)
        p7_ld_done = bump("ldw", 16 * 17)

        def p7_sc1(scalar, ld=p7_ld_done, b=bar):
            scalar.wait_ge(sems["ldw"], ld)
            scalar.wait_ge(sems["vE"], b["vE"])
            scalar.activation(btab[0:1, 0:1024], mtab[0:1, 1537:2561], AF.Exp)
            scalar.activation(AW[0:1, 2048:3072], AW[0:1, 3072:4096], AF.Exp)
            ins = scalar.activation(PS(5), PS(4), AF.Exp)
            ins.then_inc(sems["sE"], 1)
        emit("scalar", p7_sc1)
        p7s1 = bump("sE")

        def p7_v1(vector, s1=p7s1, b=bar):
            vector.wait_ge(sems["sE"], s1)
            vector.wait_ge(sems["tE"], b["tE"])
            vector.tensor_tensor(AW[0:1, 0:1024], btab[0:1, 0:1024], mtab[0:1, 0:1024], ALU.mult)
            ins = vector.scalar_tensor_tensor(AW[0:1, 1024:2048], btab[0:1, 0:1024], -1.0,
                                              AW[0:1, 2048:3072], ALU.mult, ALU.mult)
            ins.then_inc(sems["vE"], 1)
        emit("vector", p7_v1)
        p7v1 = bump("vE")

        def p7_t1(tensor, v1=p7v1):
            tensor.wait_ge(sems["vE"], v1)
            ins = tensor.matmul(psum[2][:, 0:1], mtab[0:1, 1280:1408], mtab[0:1, 1536:1537],
                                start=True, stop=True)
            ins.then_inc(sems["tE"], 1)
        emit("tensor", p7_t1)
        p7t1 = bump("tE")

        def p7_v2(vector, t1=p7t1):
            vector.wait_ge(sems["tE"], t1)
            ins = vector.tensor_copy(rsx[:, 0:1], psum[2][:, 0:1])
            ins.then_inc(sems["vE"], 1)
        emit("vector", p7_v2)
        p7v2 = bump("vE")

        def p7_sc2(scalar, v2=p7v2):
            scalar.wait_ge(sems["vE"], v2)
            ins = scalar.activation(PS(6), PS(0), AF.Exp)
            ins.then_inc(sems["sE"], 1)
        emit("scalar", p7_sc2)
        p7s2 = bump("sE")

        def p7_v3(vector, s2=p7s2):
            vector.wait_ge(sems["sE"], s2)
            vector.scalar_tensor_tensor(PS(0), PS(5), -1.0, PS(6), ALU.mult, ALU.mult)
            ins = vector.tensor_tensor(PS(7), PS(5), PS(1), ALU.mult)
            ins.then_inc(sems["vE"], 1)
        emit("vector", p7_v3)
        p7v3 = bump("vE")

        def p7_sc3(scalar, v3=p7v3):
            scalar.wait_ge(sems["vE"], v3)
            ins = scalar.activation(PS(8), PS(0), AF.Exp)
            ins.then_inc(sems["sE"], 1)
        emit("scalar", p7_sc3)
        p7s3 = bump("sE")

        def p7_v4(vector, s3=p7s3):
            vector.wait_ge(sems["sE"], s3)
            vector.tensor_scalar(PS(9), PS(7), PI, None, ALU.is_ge)
            vector.tensor_scalar(PS(10), PS(7), 3 * PI, None, ALU.is_ge)
            vector.tensor_scalar(PS(11), PS(7), 5 * PI, None, ALU.is_ge)
            vector.tensor_tensor(PS(12), PS(9), PS(10), ALU.add)
            vector.tensor_tensor(PS(9), PS(12), PS(11), ALU.add)
            ins = vector.scalar_tensor_tensor(PS(10), PS(9), -2 * PI, PS(7),
                                              ALU.mult, ALU.add)
            ins.then_inc(sems["vE"], 1)
        emit("vector", p7_v4)
        p7v4 = bump("vE")

        def p7_sc4(scalar, v4=p7v4):
            scalar.wait_ge(sems["vE"], v4)
            scalar.activation(PS(7), PS(10), AF.Sin)
            scalar.activation(PS(11), PS(10), AF.Sin, scale=0.5)
            ins = scalar.activation(PS(12), PS(11), AF.Square)
            ins.then_inc(sems["sE"], 1)
        emit("scalar", p7_sc4)
        p7s4 = bump("sE")

        def p7_v5(vector, s4=p7s4):
            vector.wait_ge(sems["sE"], s4)
            vector.tensor_scalar(PS(11), PS(12), -2.0, 1.0, ALU.mult, ALU.add)
            vector.tensor_tensor(PS(9), PS(8), PS(11), ALU.mult)
            ins = vector.tensor_tensor(PS(10), PS(8), PS(7), ALU.mult)
            ins.then_inc(sems["vE"], 1)
        emit("vector", p7_v5)
        p7v5 = bump("vE")

        def p7_sc5(scalar, v5=p7v5):
            scalar.wait_ge(sems["vE"], v5)
            scalar.activation(PS(8), PS(6), AF.Square)
            ins = scalar.activation(PS(12), PS(1), AF.Square)
            ins.then_inc(sems["sE"], 1)
        emit("scalar", p7_sc5)
        p7s5 = bump("sE")

        def p7_v6(vector, s5=p7s5):
            vector.wait_ge(sems["sE"], s5)
            ins = vector.tensor_tensor(PS(8), PS(8), PS(12), ALU.add)
            ins.then_inc(sems["vE"], 1)
        emit("vector", p7_v6)
        p7v6 = bump("vE")

        def p7_sc6(scalar, v6=p7v6):
            scalar.wait_ge(sems["vE"], v6)
            scalar.activation(PS(12), PS(8), AF.Ln)
            ins = scalar.activation(PS(8), PS(12), AF.Exp, scale=-1.0)
            ins.then_inc(sems["sE"], 1)
        emit("scalar", p7_sc6)
        p7s6 = bump("sE")

        def p7_v7(vector, s6=p7s6):
            vector.wait_ge(sems["sE"], s6)
            vector.tensor_scalar(PS(12), PS(9), 1.0, None, ALU.subtract)
            vector.scalar_tensor_tensor(PS(13), PS(12), -1.0, PS(6), ALU.mult, ALU.mult)
            vector.tensor_tensor(PS(14), PS(10), PS(1), ALU.mult)
            vector.tensor_tensor(PS(13), PS(13), PS(14), ALU.add)
            vector.scalar_tensor_tensor(PS(14), PS(12), -1.0, PS(1), ALU.mult, ALU.mult)
            vector.scalar_tensor_tensor(PS(15), PS(10), -1.0, PS(6), ALU.mult, ALU.mult)
            vector.tensor_tensor(PS(14), PS(14), PS(15), ALU.add)
            vector.tensor_tensor(PS(13), PS(13), PS(8), ALU.mult)
            vector.tensor_tensor(PS(14), PS(14), PS(8), ALU.mult)
            vector.tensor_tensor(PS(6), PS(2), PS(13), ALU.mult)
            vector.tensor_tensor(PS(8), PS(3), PS(14), ALU.mult)
            vector.tensor_tensor(PS(6), PS(6), PS(8), ALU.subtract)
            vector.tensor_tensor(PS(8), PS(2), PS(14), ALU.mult)
            vector.tensor_tensor(PS(12), PS(3), PS(13), ALU.mult)
            vector.tensor_tensor(PS(8), PS(8), PS(12), ALU.add)
            ins = vector.tensor_scalar(PS(8), PS(8), -1.0, None, ALU.mult)
            ins.then_inc(sems["vE"], 1)
        emit("vector", p7_v7)
        p7v7 = bump("vE")

        for it in range(7):
            si_c, si_s = (11, 7) if it % 2 == 0 else (2, 3)
            di_c, di_s = (2, 3) if it % 2 == 0 else (11, 7)

            def p7_dsc(scalar, sc=si_c, ss=si_s, pv=cnt["vE"], it=it, v7=p7v7):
                scalar.wait_ge(sems["vE"], pv if it else v7)
                scalar.activation(PS(13), PS(sc), AF.Square)
                ins = scalar.activation(PS(14), PS(ss), AF.Square)
                ins.then_inc(sems["sE"], 1)
            emit("scalar", p7_dsc)
            dsq = bump("sE")

            def p7_dv(vector, sc=si_c, ss=si_s, dc=di_c, ds=di_s, sq=dsq):
                vector.wait_ge(sems["sE"], sq)
                vector.tensor_tensor(PS(dc), PS(13), PS(14), ALU.subtract)
                vector.tensor_tensor(PS(13), PS(sc), PS(ss), ALU.mult)
                ins = vector.tensor_scalar(PS(ds), PS(13), 2.0, None, ALU.mult)
                ins.then_inc(sems["vE"], 1)
            emit("vector", p7_dv)
            bump("vE")
        lq_v = cnt["vE"]

        def p7_sc7(scalar, lv=lq_v):
            scalar.wait_ge(sems["vE"], lv)
            ins = scalar.activation(PS(12), PS(0), AF.Exp, scale=128.0)
            ins.then_inc(sems["sE"], 1)
        emit("scalar", p7_sc7)
        p7s7 = bump("sE")

        def p7_v8(vector, s7=p7s7):
            vector.wait_ge(sems["sE"], s7)
            vector.tensor_tensor(PS(4), PS(12), PS(2), ALU.mult)
            vector.tensor_tensor(PS(5), PS(12), PS(3), ALU.mult)
            vector.tensor_tensor(PS(13), PS(6), PS(9), ALU.mult)
            vector.tensor_tensor(PS(14), PS(8), PS(10), ALU.mult)
            vector.tensor_tensor(PS(13), PS(13), PS(14), ALU.add)
            vector.tensor_tensor(PS(14), PS(8), PS(9), ALU.mult)
            vector.tensor_tensor(PS(15), PS(6), PS(10), ALU.mult)
            vector.tensor_tensor(PS(14), PS(14), PS(15), ALU.subtract)
            vector.tensor_copy(AACT[:, PRK_LQR:PRK_LQR + 256], PS(4))
            vector.tensor_copy(AACT[:, PRK_LQI:PRK_LQI + 256], PS(5))
            vector.tensor_copy(AACT[:, PRK_CLR:PRK_CLR + 256], PS(13))
            ins = vector.tensor_copy(AACT[:, PRK_CLI:PRK_CLI + 256], PS(14))
            ins.then_inc(sems["vE"], 1)
        emit("vector", p7_v8)
        p7v8 = bump("vE")

        CBD = 8960

        def p7_bd(vector, v8=p7v8):
            vector.wait_ge(sems["vE"], v8)
            msk = ften[:, 0:4].rearrange("p (a f) -> p a f", a=1).broadcast_to([128, 256, 4])
            in0r = PS(6).rearrange("p (q x) -> p q x", x=1).broadcast_to([128, 256, 4])
            in0i = PS(8).rearrange("p (q x) -> p q x", x=1).broadcast_to([128, 256, 4])
            vector.tensor_tensor(AACT[:, CBD:CBD + 1024].rearrange("p (q f) -> p q f", f=4),
                                 in0r, msk, ALU.mult)
            ins = vector.tensor_tensor(AACT[:, CBD + 1024:CBD + 2048]
                                       .rearrange("p (q f) -> p q f", f=4),
                                       in0i, msk, ALU.mult)
            ins.then_inc(sems["vE"], 1)
        emit("vector", p7_bd)
        p7bd = bump("vE")

        PL = {"pf": 0, "pr": 1024, "lf": 2048, "lr": 3072}
        plane_jobs = [("pf", 1024, 0), ("pr", 1024 + 128, 0),
                      ("lf", 1024, 1024), ("lr", 1024 + 128, 1024)]
        pl_prev = p7v2
        for pn, lrow0, rrow0 in plane_jobs:
            def p7_plm(tensor, l0=lrow0, r0=rrow0, pe=pl_prev):
                tensor.wait_ge(sems["vE"], pe)
                tensor.matmul(psum[0][:, 0:512], mtab[0:1, l0:l0 + 128],
                              AW[0:1, r0:r0 + 512], start=True, stop=True)
                ins = tensor.matmul(psum[1][:, 0:512], mtab[0:1, l0:l0 + 128],
                                    AW[0:1, r0 + 512:r0 + 1024], start=True, stop=True)
                ins.then_inc(sems["tE"], 1)
            emit("tensor", p7_plm)
            plm = bump("tE")

            def p7_ple(vector, pn=pn, t=plm):
                vector.wait_ge(sems["tE"], t)
                vector.tensor_copy(AACT[:, PL[pn]:PL[pn] + 512], psum[0][:, 0:512])
                ins = vector.tensor_copy(AACT[:, PL[pn] + 512:PL[pn] + 1024], psum[1][:, 0:512])
                ins.then_inc(sems["vE"], 1)
            emit("vector", p7_ple)
            pl_prev = bump("vE")

        def p7_wrap(vector, pe=pl_prev):
            vector.wait_ge(sems["vE"], pe)
            wA = AACT[:, 6144:7168]
            wB = AACT[:, 7168:8192]
            for pn in ("pf", "pr"):
                src = AACT[:, PL[pn]:PL[pn] + 1024]
                vector.tensor_scalar(wA, src, PI, None, ALU.is_ge)
                for m in range(1, 7):
                    vector.tensor_scalar(wB, src, (2 * m + 1) * PI, None, ALU.is_ge)
                    vector.tensor_tensor(wA, wA, wB, ALU.add)
                vector.scalar_tensor_tensor(src, wA, -2 * PI, src, ALU.mult, ALU.add)
            ins = vector.tensor_copy(padb[:, :], ident[:, :])
            ins.then_inc(sems["vE"], 1)
        emit("vector", p7_wrap)
        planes_done = bump("vE")

        MPL, TBF, TRG, UGB, EBF = 4096, 4608, 5120, 5888, 7936
        T_COSF, T_SINF, T_COSR, T_SINR, T_RF, T_RR = (TRG, TRG + 128, TRG + 256,
                                                      TRG + 384, TRG + 512, TRG + 640)
        KPL = 11008

        def slab_n(tc_lo, tc_n, n):
            v = AW[:, :].rearrange("p (tc h n) -> p tc h n", tc=4, n=32)
            return v[:, tc_lo:tc_lo + tc_n, :, n]

        prev7 = {"init_v": 0, "tr_last": 0, "tc_last": 0, "tcv_last": 0}
        bank_hist = {i: 0 for i in range(6)}
        bank_tag = {i: "sE" for i in range(6)}
        ebuf_hist = [(0, 0), (0, 0)]

        for g in range(8):
            def s2_u(sync, g=g, ptc=prev7["tc_last"], pd=planes_done):
                if ptc:
                    sync.wait_ge(sems["tE"], ptc)
                else:
                    sync.wait_ge(sems["vE"], pd)
                for c in range(16):
                    sync.dma_start(out=AACT[:, UGB + 128 * c:UGB + 128 * c + 128],
                                   in_=u_sp[c, :, 128 * g:128 * g + 128]).then_inc(sems["ldx"], 16)
            emit("sync", s2_u)
            ug_ld = bump("ldx", 256)

            def s2_scA(scalar, g=g, pd=planes_done, pv=prev7["init_v"]):
                scalar.wait_ge(sems["vE"], pv if pv else pd)
                c0 = 128 * g
                scalar.activation(AACT[:, T_COSF:T_COSF + 128],
                                  AACT[:, PL["pf"] + c0:PL["pf"] + c0 + 128], AF.Sin, scale=0.5)
                scalar.activation(AACT[:, T_RF:T_RF + 128], AACT[:, T_COSF:T_COSF + 128], AF.Square)
                scalar.activation(AACT[:, T_SINF:T_SINF + 128],
                                  AACT[:, PL["pf"] + c0:PL["pf"] + c0 + 128], AF.Sin)
                scalar.activation(AACT[:, T_COSR:T_COSR + 128],
                                  AACT[:, PL["pr"] + c0:PL["pr"] + c0 + 128], AF.Sin, scale=0.5)
                scalar.activation(AACT[:, T_RR:T_RR + 128], AACT[:, T_COSR:T_COSR + 128], AF.Square)
                ins = scalar.activation(AACT[:, T_SINR:T_SINR + 128],
                                        AACT[:, PL["pr"] + c0:PL["pr"] + c0 + 128], AF.Sin)
                ins.then_inc(sems["sE"], 1)
            emit("scalar", s2_scA)
            trigA = bump("sE")

            def s2_vA(vector, sa=trigA):
                vector.wait_ge(sems["sE"], sa)
                vector.tensor_scalar(AACT[:, T_COSF:T_COSF + 128], AACT[:, T_RF:T_RF + 128],
                                     -2.0, 1.0, ALU.mult, ALU.add)
                ins = vector.tensor_scalar(AACT[:, T_COSR:T_COSR + 128], AACT[:, T_RR:T_RR + 128],
                                           -2.0, 1.0, ALU.mult, ALU.add)
                ins.then_inc(sems["vE"], 1)
            emit("vector", s2_vA)
            vA = bump("vE")

            def s2_scB(scalar, g=g, va=vA):
                scalar.wait_ge(sems["vE"], va)
                c0 = 128 * g
                scalar.activation(AACT[:, T_RF:T_RF + 128],
                                  AACT[:, PL["lf"] + c0:PL["lf"] + c0 + 128], AF.Exp)
                ins = scalar.activation(AACT[:, T_RR:T_RR + 128],
                                        AACT[:, PL["lr"] + c0:PL["lr"] + c0 + 128], AF.Exp)
                ins.then_inc(sems["sE"], 1)
            emit("scalar", s2_scB)
            scB = bump("sE")

            def s2_vB(vector, g=g, sb=scB, ptr=prev7["tr_last"], ptc=prev7["tc_last"]):
                vector.wait_ge(sems["sE"], sb)
                if ptc:
                    vector.wait_ge(sems["tE"], ptc)
                vector.tensor_scalar(AACT[:, MPL + 0:MPL + 128], AACT[:, T_COSF:T_COSF + 128],
                                     2.0, None, ALU.mult)
                vector.tensor_copy(AACT[:, MPL + 128:MPL + 256], AACT[:, MPL + 0:MPL + 128])
                vector.tensor_scalar(AACT[:, MPL + 256:MPL + 384], AACT[:, T_COSR:T_COSR + 128],
                                     2.0, None, ALU.mult)
                vector.tensor_copy(AACT[:, MPL + 384:MPL + 512], AACT[:, MPL + 256:MPL + 384])
                vector.tensor_copy(slab_n(0, 1, 0)[:, 0, :], AACT[:, T_RF:T_RF + 128])
                vector.memset(slab_n(1, 1, 0)[:, 0, :], 0.0)
                vector.tensor_tensor(slab_n(0, 1, 1)[:, 0, :], AACT[:, T_RF:T_RF + 128],
                                     AACT[:, T_COSF:T_COSF + 128], ALU.mult)
                vector.tensor_tensor(slab_n(1, 1, 1)[:, 0, :], AACT[:, T_RF:T_RF + 128],
                                     AACT[:, T_SINF:T_SINF + 128], ALU.mult)
                mpl_f = AACT[:, MPL:MPL + 256].rearrange("p (tc h) -> p tc h", tc=2)
                tb_f = AACT[:, TBF:TBF + 256].rearrange("p (tc h) -> p tc h", tc=2)
                ins = None
                for n in range(2, 32):
                    ins = vector.tensor_tensor(tb_f, mpl_f, slab_n(0, 2, n - 1), ALU.mult)
                    ins = vector.scalar_tensor_tensor(slab_n(0, 2, n), tb_f, 1.0,
                                                      slab_n(0, 2, n - 2), ALU.mult, ALU.subtract)
                ins.then_inc(sems["vE"], 1)
            emit("vector", s2_vB)
            fwd_done = bump("vE")

            def s2_vC(vector, g=g):
                vector.tensor_copy(slab_n(2, 1, 0)[:, 0, :], AACT[:, T_RR:T_RR + 128])
                vector.memset(slab_n(3, 1, 0)[:, 0, :], 0.0)
                vector.tensor_tensor(slab_n(2, 1, 1)[:, 0, :], AACT[:, T_RR:T_RR + 128],
                                     AACT[:, T_COSR:T_COSR + 128], ALU.mult)
                vector.tensor_tensor(slab_n(3, 1, 1)[:, 0, :], AACT[:, T_RR:T_RR + 128],
                                     AACT[:, T_SINR:T_SINR + 128], ALU.mult)
                mpl_r = AACT[:, MPL + 256:MPL + 512].rearrange("p (tc h) -> p tc h", tc=2)
                tb_r = AACT[:, TBF + 256:TBF + 512].rearrange("p (tc h) -> p tc h", tc=2)
                ins = None
                for n in range(2, 32):
                    ins = vector.tensor_tensor(tb_r, mpl_r, slab_n(2, 2, n - 1), ALU.mult)
                    ins = vector.scalar_tensor_tensor(slab_n(2, 2, n), tb_r, 1.0,
                                                      slab_n(2, 2, n - 2), ALU.mult, ALU.subtract)
                ins.then_inc(sems["vE"], 1)
            emit("vector", s2_vC)
            rev_done = bump("vE")
            prev7["init_v"] = rev_done

            for b in range(16):
                bank = b % 6
                half = b % 2

                def s2_tr(tensor, g=g, b=b, bank=bank, fd=fwd_done,
                          bh=bank_hist[bank], bt=bank_tag[bank]):
                    tensor.wait_ge(sems["vE"], fd)
                    if bh:
                        tensor.wait_ge(sems[bt], bh)
                    ins = None
                    for qq in range(2):
                        q = 2 * b + qq
                        for comp in range(2):
                            ins = tensor.transpose(
                                psum[bank][:, (2 * qq + comp) * 128:(2 * qq + comp) * 128 + 128],
                                AW[:, comp * 4096 + q * 128:comp * 4096 + q * 128 + 128],
                                ident[:, :])
                    ins.then_inc(sems["tE"], 1)
                emit("tensor", s2_tr)
                trb = bump("tE")

                def s2_ev(scalar, bank=bank, half=half, t=trb, eh=tuple(ebuf_hist[half])):
                    scalar.wait_ge(sems["tE"], max(t, eh[1]))
                    if eh[0]:
                        scalar.wait_ge(sems["spl"], eh[0])
                    ins = scalar.activation(AACT[:, EBF + half * 512:EBF + half * 512 + 512],
                                            psum[bank][:, :], AF.Copy)
                    ins.then_inc(sems["sE"], 1)
                emit("scalar", s2_ev)
                evb = bump("sE")
                bank_hist[bank] = evb
                bank_tag[bank] = "sE"

                def s2_km(tensor, g=g, b=b, half=half, ev=evb, bd=p7bd):
                    tensor.wait_ge(sems["sE"], ev)
                    if g == 0 and b == 0:
                        tensor.wait_ge(sems["vE"], bd)
                    ins = None
                    kb = psum[6 + g // 4]
                    for qq in range(2):
                        q = 2 * b + qq
                        qg = 32 * g + q
                        col = (g % 4) * 128 + q * 4
                        e0 = EBF + half * 512 + qq * 256
                        ins = tensor.matmul(kb[:, col:col + 4],
                                            AACT[:, e0:e0 + 128],
                                            AACT[:, CBD + qg * 4:CBD + qg * 4 + 4],
                                            start=True, stop=False)
                        ins = tensor.matmul(kb[:, col:col + 4],
                                            AACT[:, e0 + 128:e0 + 256],
                                            AACT[:, CBD + 1024 + qg * 4:CBD + 1024 + qg * 4 + 4],
                                            start=False, stop=True)
                    ins.then_inc(sems["tE"], 1)
                emit("tensor", s2_km)
                kmb = bump("tE")

                def s2_sp(gpsimd, g=g, b=b, half=half, ev=evb):
                    gpsimd.wait_ge(sems["sE"], ev)
                    gpsimd.dma_start(out=fwdT_sp[g, b, :, :],
                                     in_=AACT[:, EBF + half * 512:EBF + half * 512 + 512]
                                     ).then_inc(sems["spl"], 16)
                emit("gpsimd", s2_sp)
                spb = bump("spl", 16)
                ebuf_hist[half] = (spb, kmb)
            prev7["tr_last"] = cnt["tE"]

            for tb in range(4):
                bA = (2 * tb) % 6
                bB = (2 * tb + 1) % 6

                def s2_tc(tensor, g=g, tb=tb, bA=bA, bB=bB, rd=rev_done, ug=ug_ld,
                          hA=bank_hist[bA], tA=bank_tag[bA], hB=bank_hist[bB], tB=bank_tag[bB]):
                    tensor.wait_ge(sems["vE"], rd)
                    tensor.wait_ge(sems["ldx"], ug)
                    if hA:
                        tensor.wait_ge(sems[tA], hA)
                    if hB:
                        tensor.wait_ge(sems[tB], hB)
                    ins = None
                    ugv = AACT[:, UGB:UGB + 2048].rearrange("p (c f) -> p c f", f=128)
                    for qq in range(8):
                        q = 8 * tb + qq
                        rhs = ugv[:, :, 4 * q:4 * q + 4]
                        ins = tensor.matmul(psum[bA][:, qq * 64:qq * 64 + 64],
                                            AW[:, 2 * 4096 + q * 128:2 * 4096 + q * 128 + 128],
                                            rhs, start=True, stop=True)
                        ins = tensor.matmul(psum[bB][:, qq * 64:qq * 64 + 64],
                                            AW[:, 3 * 4096 + q * 128:3 * 4096 + q * 128 + 128],
                                            rhs, start=True, stop=True)
                    ins.then_inc(sems["tE"], 1)
                emit("tensor", s2_tc)
                tcb = bump("tE")

                def s2_tcv(vector, g=g, tb=tb, bA=bA, bB=bB, t=tcb):
                    vector.wait_ge(sems["tE"], t)
                    ins = None
                    for comp, bk in ((0, bA), (1, bB)):
                        src = psum[bk][:, :].rearrange("p (q c f) -> p q c f", q=8, c=16, f=4)
                        dst0 = AS[:, comp * 4096 + (32 * g + 8 * tb) * 16:
                                  comp * 4096 + (32 * g + 8 * tb) * 16 + 128]
                        dst = dst0.rearrange("p (q c) -> p q c", c=16)
                        for h4 in range(4):
                            ins = vector.tensor_copy(dst[32 * h4:32 * h4 + 32, :, :],
                                                     src[32 * h4:32 * h4 + 32, :, :, h4])
                    ins.then_inc(sems["vE"], 1)
                emit("vector", s2_tcv)
                tcv = bump("vE")
                bank_hist[bA] = tcv
                bank_tag[bA] = "vE"
                bank_hist[bB] = tcv
                bank_tag[bB] = "vE"
            prev7["tc_last"] = cnt["tE"]
            prev7["tcv_last"] = cnt["vE"]

        s2_tE_end = cnt["tE"]
        s2_vE_end = cnt["vE"]
        s2_tcv_end = prev7["tcv_last"]

        if DEBUG:
            def s2_dbg(gpsimd, te=s2_tE_end, ve=s2_vE_end):
                gpsimd.wait_ge(sems["tE"], te)
                gpsimd.wait_ge(sems["vE"], ve)
                gpsimd.dma_start(out=ap["dbg_ssm_slabf"], in_=AW[:, 0:512]).then_inc(sems["spl"], 16)
                gpsimd.dma_start(out=ap["dbg_ssm_slabr"], in_=AW[:, 8192:8704]).then_inc(sems["spl"], 16)
                gpsimd.dma_start(out=ap["dbg_ssm_ug"],
                                 in_=AACT[:, 5888:5888 + 512]).then_inc(sems["spl"], 16)
            emit("gpsimd", s2_dbg)
            s2dbg = bump("spl", 48)
        else:
            s2dbg = 0

        def s3_kev(vector, te=s2_tE_end):
            vector.wait_ge(sems["tE"], te)
            vector.tensor_scalar(AACT[:, KPL:KPL + 512], psum[6][:, :], 2.0, None, ALU.mult)
            ins = vector.tensor_scalar(AACT[:, KPL + 512:KPL + 1024], psum[7][:, :],
                                       2.0, None, ALU.mult)
            ins.then_inc(sems["vE"], 1)
        emit("vector", s3_kev)
        kev = bump("vE")

        KFRE, KFIM = 11008, 9216

        def s3_kf(tensor, kv=kev, tcv=s2_tcv_end):
            tensor.wait_ge(sems["vE"], max(kv, tcv))
            ins = None
            for hh in range(2):
                rhs = AACT[:, KPL + 512 * hh:KPL + 512 * hh + 512]
                ins = tensor.matmul(psum[0 + hh][:, :], blkq[:, 0:128], rhs, start=True, stop=True)
                ins = tensor.matmul(psum[2 + hh][:, :], blkq[:, 128:256], rhs, start=True, stop=True)
                ins = tensor.matmul(psum[4 + hh][0:1, :], blkq[:, 256:257], rhs, start=True, stop=True)
            ins.then_inc(sems["tE"], 1)
        emit("tensor", s3_kf)
        kf_mm = bump("tE")

        def s3_kfe(scalar, t=kf_mm):
            scalar.wait_ge(sems["tE"], t)
            scalar.activation(AACT[:, KFRE:KFRE + 512], psum[0][:, :], AF.Copy)
            scalar.activation(AACT[:, KFRE + 512:KFRE + 1024], psum[1][:, :], AF.Copy)
            scalar.activation(AACT[:, KFIM:KFIM + 512], psum[2][:, :], AF.Copy)
            scalar.activation(AACT[:, KFIM + 512:KFIM + 1024], psum[3][:, :], AF.Copy)
            scalar.activation(mtab[0:1, 0:512], psum[4][0:1, :], AF.Copy)
            ins = scalar.activation(mtab[0:1, 512:1024], psum[5][0:1, :], AF.Copy)
            ins.then_inc(sems["sE"], 1)
        emit("scalar", s3_kfe)
        kf_done = bump("sE")

        LQR, LQI, CLR, CLI = PRK_LQR, PRK_LQI, PRK_CLR, PRK_CLI
        SRE, SIM = 0, 256
        T1, T2, T3, T4 = 512, 768, 1024, 1280
        AGS, BKA, BKB = 1536, 2048, 2304
        AGB0, AGB1 = 2560, 3072

        def s3_rst(sync, ve=s2_vE_end):
            sync.wait_ge(sems["vE"], ve)
            sync.dma_start(out=AACT[:, BKA:BKA + 256], in_=ap["ssm_blkA"]).then_inc(sems["ldw"], 16)
            sync.dma_start(out=AACT[:, BKB:BKB + 256], in_=ap["ssm_blkB"]).then_inc(sems["ldw"], 16)
        emit("sync", s3_rst)
        rst_ld = bump("ldw", 16 * 2)

        def tre_slot(comp, c):
            return AS[:, comp * 4096:(comp + 1) * 4096]\
                .rearrange("p (q s) -> p q s", s=16)[:, :, c]

        def aw_slot(comp, s):
            return AW[:, comp * 4096:(comp + 1) * 4096]\
                .rearrange("p (q s) -> p q s", s=16)[:, :, s]

        def s3_scan1(vector, ld=rst_ld, tcv=s2_tcv_end):
            vector.wait_ge(sems["ldw"], ld)
            vector.wait_ge(sems["vE"], tcv)
            vector.tensor_copy(AACT[:, SRE:SRE + 256], tre_slot(0, 0))
            vector.tensor_copy(AACT[:, SIM:SIM + 256], tre_slot(1, 0))
            for c in range(1, 16):
                vector.tensor_tensor(AACT[:, T1:T1 + 256], AACT[:, LQR:LQR + 256],
                                     AACT[:, SRE:SRE + 256], ALU.mult)
                vector.tensor_tensor(AACT[:, T2:T2 + 256], AACT[:, LQI:LQI + 256],
                                     AACT[:, SIM:SIM + 256], ALU.mult)
                vector.tensor_tensor(AACT[:, T3:T3 + 256], AACT[:, LQI:LQI + 256],
                                     AACT[:, SRE:SRE + 256], ALU.mult)
                vector.tensor_tensor(AACT[:, T4:T4 + 256], AACT[:, LQR:LQR + 256],
                                     AACT[:, SIM:SIM + 256], ALU.mult)
                vector.tensor_tensor(AACT[:, T1:T1 + 256], AACT[:, T1:T1 + 256],
                                     AACT[:, T2:T2 + 256], ALU.subtract)
                vector.tensor_tensor(AACT[:, T3:T3 + 256], AACT[:, T3:T3 + 256],
                                     AACT[:, T4:T4 + 256], ALU.add)
                vector.tensor_tensor(AACT[:, SRE:SRE + 256], AACT[:, T1:T1 + 256],
                                     tre_slot(0, c), ALU.add)
                vector.tensor_tensor(AACT[:, SIM:SIM + 256], AACT[:, T3:T3 + 256],
                                     tre_slot(1, c), ALU.add)
            vector.tensor_copy(AACT[:, AGS:AGS + 256], AACT[:, SRE:SRE + 256])
            ins = vector.tensor_copy(AACT[:, AGS + 256:AGS + 512], AACT[:, SIM:SIM + 256])
            ins.then_inc(sems["vE"], 1)
        emit("vector", s3_scan1)
        scan1 = bump("vE")

        def s3_ag(gpsimd, sc=scan1, base=cnt["spl"]):
            gpsimd.wait_ge(sems["vE"], sc)
            gpsimd.dma_start(out=sag_i.rearrange("(p f) -> p f", p=128),
                             in_=AACT[:, AGS:AGS + 512]).then_inc(sems["spl"], 16)
            gpsimd.wait_ge(sems["spl"], base + 16)
            ins = gpsimd.collective_compute(
                "AllGather", ALU.bypass,
                replica_groups=[[0, 1], [2, 3], [4, 5], [6, 7]],
                ins=[sag_i.opt()],
                outs=[sag_o.opt()],
            )
            ins.then_inc(sems["spl"], 1)
            gpsimd.wait_ge(sems["spl"], base + 17)
            gpsimd.dma_start(out=AACT[:, AGB0:AGB0 + 512],
                             in_=sag_o[0:65536].rearrange("(p f) -> p f", p=128)
                             ).then_inc(sems["spl"], 16)
            gpsimd.dma_start(out=AACT[:, AGB1:AGB1 + 512],
                             in_=sag_o[65536:131072].rearrange("(p f) -> p f", p=128)
                             ).then_inc(sems["spl"], 16)
        emit("gpsimd", s3_ag)
        ag_done7 = bump("spl", 49)

        def s3_sinit(vector, agd=ag_done7, te=s2_tE_end, sd=s2dbg):
            vector.wait_ge(sems["spl"], agd)
            if sd:
                vector.wait_ge(sems["spl"], sd)
            vector.wait_ge(sems["tE"], te)
            for comp in range(2):
                vector.tensor_tensor(AACT[:, T1:T1 + 256],
                                     AACT[:, AGB0 + comp * 256:AGB0 + comp * 256 + 256],
                                     AACT[:, BKA:BKA + 256], ALU.mult)
                vector.tensor_tensor(AACT[:, T2:T2 + 256],
                                     AACT[:, AGB1 + comp * 256:AGB1 + comp * 256 + 256],
                                     AACT[:, BKB:BKB + 256], ALU.mult)
                vector.tensor_tensor(aw_slot(comp, 0), AACT[:, T1:T1 + 256],
                                     AACT[:, T2:T2 + 256], ALU.add)
            ins = vector.tensor_copy(padb[:, :], ident[:, :])
            ins.then_inc(sems["vE"], 1)
        emit("vector", s3_sinit)
        sinit = bump("vE")

        def s3_scan2(vector, si=sinit):
            vector.wait_ge(sems["vE"], si)
            ins = None
            for c in range(15):
                vector.tensor_tensor(AACT[:, T1:T1 + 256], AACT[:, LQR:LQR + 256],
                                     aw_slot(0, c), ALU.mult)
                vector.tensor_tensor(AACT[:, T2:T2 + 256], AACT[:, LQI:LQI + 256],
                                     aw_slot(1, c), ALU.mult)
                vector.tensor_tensor(AACT[:, T3:T3 + 256], AACT[:, LQI:LQI + 256],
                                     aw_slot(0, c), ALU.mult)
                vector.tensor_tensor(AACT[:, T4:T4 + 256], AACT[:, LQR:LQR + 256],
                                     aw_slot(1, c), ALU.mult)
                vector.tensor_tensor(AACT[:, T1:T1 + 256], AACT[:, T1:T1 + 256],
                                     AACT[:, T2:T2 + 256], ALU.subtract)
                vector.tensor_tensor(AACT[:, T3:T3 + 256], AACT[:, T3:T3 + 256],
                                     AACT[:, T4:T4 + 256], ALU.add)
                vector.tensor_tensor(aw_slot(0, c + 1), AACT[:, T1:T1 + 256],
                                     tre_slot(0, c), ALU.add)
                ins = vector.tensor_tensor(aw_slot(1, c + 1), AACT[:, T3:T3 + 256],
                                           tre_slot(1, c), ALU.add)
            ins.then_inc(sems["vE"], 1)
        emit("vector", s3_scan2)
        scan2 = bump("vE")

        if DEBUG:
            def s3_dbg(gpsimd, sc=scan2):
                gpsimd.wait_ge(sems["vE"], sc)
                gpsimd.dma_start(out=ap["dbg_ssm_tre"], in_=AS[:, 0:512]).then_inc(sems["spl"], 16)
            emit("gpsimd", s3_dbg)
            dbg_tre = bump("spl", 16)
        else:
            dbg_tre = 0

        def s3_fold(vector, sc=scan2, dt=dbg_tre):
            vector.wait_ge(sems["vE"], sc)
            if dt:
                vector.wait_ge(sems["spl"], dt)
            awr = AW[:, 0:4096].rearrange("p (q s) -> p q s", s=16)
            awi = AW[:, 4096:8192].rearrange("p (q s) -> p q s", s=16)
            asr = AS[:, 0:4096].rearrange("p (q s) -> p q s", s=16)
            asi = AS[:, 4096:8192].rearrange("p (q s) -> p q s", s=16)
            clr = AACT[:, CLR:CLR + 256].rearrange("p (q x) -> p q x", x=1)\
                .broadcast_to([128, 256, 16])
            cli = AACT[:, CLI:CLI + 256].rearrange("p (q x) -> p q x", x=1)\
                .broadcast_to([128, 256, 16])
            vector.tensor_tensor(asr, awr, clr, ALU.mult)
            vector.tensor_tensor(asi, awi, cli, ALU.mult)
            vector.tensor_tensor(asr, asr, asi, ALU.add)
            vector.tensor_tensor(asi, awr, cli, ALU.mult)
            vector.tensor_tensor(awr, awi, clr, ALU.mult)
            vector.tensor_tensor(awi, asi, awr, ALU.subtract)
            ins = vector.tensor_copy(awr, asr)
            ins.then_inc(sems["vE"], 1)
        emit("vector", s3_fold)
        fold_done = bump("vE")

        ybank_hist = {i: 0 for i in range(8)}
        GBUF, BDR, BDI = 0, 8192, 8704
        prev_s4mm = 0
        fillc = 0
        for g in range(8):
            def s4_ld(sync, g=g, fd=fold_done, pm=prev_s4mm, kfd=kf_done):
                sync.wait_ge(sems["vE"], fd)
                sync.wait_ge(sems["sE"], kfd)
                if pm:
                    sync.wait_ge(sems["tE"], pm)
                for b in range(16):
                    sync.dma_start(out=AACT[:, GBUF + 512 * b:GBUF + 512 * b + 512],
                                   in_=fwdT_sp[g, b, :, :]).then_inc(sems["ldw"], 16)
            emit("sync", s4_ld)
            gld = bump("ldw", 256)

            for bb in range(4):
                bank = fillc % 8
                fillc += 1

                def s4_bd(vector, g=g, bb=bb, fd=fold_done, pm=prev_s4mm):
                    if pm:
                        vector.wait_ge(sems["tE"], pm)
                    vector.wait_ge(sems["vE"], fd)
                    q0 = 32 * g + 8 * bb
                    msk = ften[:, 0:4].rearrange("p (a f c) -> p a f c", a=1, c=1)\
                        .broadcast_to([128, 8, 4, 16])
                    ins = None
                    for comp, dst in ((0, BDR), (1, BDI)):
                        in0 = AW[:, comp * 4096 + q0 * 16:comp * 4096 + q0 * 16 + 128]\
                            .rearrange("p (q x s) -> p q x s", x=1, s=16)\
                            .broadcast_to([128, 8, 4, 16])
                        ins = vector.tensor_tensor(
                            AACT[:, dst:dst + 512].rearrange("p (q f c) -> p q f c", f=4, c=16),
                            in0, msk, ALU.mult)
                    ins.then_inc(sems["vE"], 1)
                emit("vector", s4_bd)
                bdv = bump("vE")

                def s4_mm(tensor, g=g, bb=bb, bank=bank, bv=bdv, gl=gld,
                          bh=ybank_hist[bank]):
                    tensor.wait_ge(sems["vE"], bv)
                    tensor.wait_ge(sems["ldw"], gl)
                    if bh:
                        tensor.wait_ge(sems["vE"], bh)
                    ins = None
                    for qq in range(8):
                        q = 8 * bb + qq
                        eb = (q // 2) * 512 + (q % 2) * 256
                        ins = tensor.matmul(psum[bank][:, qq * 64:qq * 64 + 64],
                                            AACT[:, GBUF + eb:GBUF + eb + 128],
                                            AACT[:, BDR + qq * 64:BDR + qq * 64 + 64],
                                            start=True, stop=False)
                        ins = tensor.matmul(psum[bank][:, qq * 64:qq * 64 + 64],
                                            AACT[:, GBUF + eb + 128:GBUF + eb + 256],
                                            AACT[:, BDI + qq * 64:BDI + qq * 64 + 64],
                                            start=False, stop=True)
                    ins.then_inc(sems["tE"], 1)
                emit("tensor", s4_mm)
                mmv = bump("tE")
                prev_s4mm = mmv

                def s4_ev(vector, g=g, bb=bb, bank=bank, t=mmv):
                    vector.wait_ge(sems["tE"], t)
                    h0 = 128 * g + 32 * bb
                    if h0 < 512:
                        dst = AW[:, 8192 + h0 * 16:8192 + h0 * 16 + 512]
                    else:
                        dst = AS[:, (h0 - 512) * 16:(h0 - 512) * 16 + 512]
                    ins = vector.tensor_scalar(dst, psum[bank][:, :], 2.0, None, ALU.mult)
                    ins.then_inc(sems["vE"], 1)
                emit("vector", s4_ev)
                yev = bump("vE")
                ybank_hist[bank] = yev
        s4_vE_end = cnt["vE"]
        s4_tE_end = cnt["tE"]

        if DEBUG:
            def s4_dbg(gpsimd, ve=s4_vE_end, kfd=kf_done):
                gpsimd.wait_ge(sems["vE"], ve)
                gpsimd.wait_ge(sems["sE"], kfd)
                gpsimd.dma_start(out=ap["dbg_ssm_sp"], in_=AW[:, 0:512]).then_inc(sems["spl"], 16)
                gpsimd.dma_start(out=ap["dbg_ssm_y0"], in_=AW[:, 8192:9216]).then_inc(sems["spl"], 16)
                gpsimd.dma_start(out=ap["dbg_ssm_kf"],
                                 in_=AACT[:, 11008:11008 + 1024]).then_inc(sems["spl"], 16)
            emit("gpsimd", s4_dbg)
            bump("spl", 48)

        VR0, VR1, VI0, VI1 = 0, 512, 1024, 1536
        UXR, VXR = 2048, 3072
        TVa, TVb = 4096, 4608
        X1T, G2T, UT, YT = 5120, 6144, 7168, 8192
        DUB = 4096

        s5_prev_v = s4_vE_end
        s5_prev_stf = cnt["stf"]
        for c in range(NT_OWN):
            def s5_ld(sync, c=c, pv=s5_prev_v, kfd=kf_done, pstf=s5_prev_stf):
                sync.wait_ge(sems["vE"], pv)
                sync.wait_ge(sems["sE"], kfd)
                if pstf:
                    sync.wait_ge(sems["stf"], pstf)
                sync.dma_start(out=AACT[:, X1T:X1T + 1024], in_=x1_sp[c, :, :]).then_inc(sems["ldx"], 16)
                sync.dma_start(out=AACT[:, G2T:G2T + 1024], in_=g2_sp[c, :, :]).then_inc(sems["ldx"], 16)
                sync.dma_start(out=AACT[:, UT:UT + 1024], in_=u_sp[c, :, :]).then_inc(sems["ldx"], 16)
            emit("sync", s5_ld)
            ld5 = bump("ldx", 48)

            def s5_mm1(tensor, c=c, ld=ld5, pv=s5_prev_v):
                tensor.wait_ge(sems["ldx"], ld)
                tensor.wait_ge(sems["vE"], pv)
                tensor.matmul(psum[4][0:1, :], blkq[:, 256:257], AACT[:, UT:UT + 512],
                              start=True, stop=True)
                tensor.matmul(psum[5][0:1, :], blkq[:, 256:257], AACT[:, UT + 512:UT + 1024],
                              start=True, stop=True)
                tensor.matmul(psum[0][:, :], blkq[:, 0:128], AACT[:, UT:UT + 512],
                              start=True, stop=True)
                tensor.matmul(psum[1][:, :], blkq[:, 0:128], AACT[:, UT + 512:UT + 1024],
                              start=True, stop=True)
                tensor.matmul(psum[2][:, :], blkq[:, 128:256], AACT[:, UT:UT + 512],
                              start=True, stop=True)
                ins = tensor.matmul(psum[3][:, :], blkq[:, 128:256], AACT[:, UT + 512:UT + 1024],
                                    start=True, stop=True)
                ins.then_inc(sems["tE"], 1)
            emit("tensor", s5_mm1)
            mm1 = bump("tE")

            def s5_pw(vector, c=c, t=mm1):
                vector.wait_ge(sems["tE"], t)
                vector.tensor_copy(AACT[0:1, UXR:UXR + 512], psum[4][0:1, :])
                vector.tensor_copy(AACT[0:1, UXR + 512:UXR + 1024], psum[5][0:1, :])
                for hh, (bre, bim, vre, vim) in enumerate(((0, 2, VR0, VI0), (1, 3, VR1, VI1))):
                    kfr = AACT[:, KFRE + 512 * hh:KFRE + 512 * hh + 512]
                    kfi = AACT[:, KFIM + 512 * hh:KFIM + 512 * hh + 512]
                    vector.tensor_tensor(AACT[:, TVa:TVa + 512], psum[bre][:, :], kfr, ALU.mult)
                    vector.tensor_tensor(AACT[:, TVb:TVb + 512], psum[bim][:, :], kfi, ALU.mult)
                    vector.tensor_tensor(AACT[:, vre:vre + 512], AACT[:, TVa:TVa + 512],
                                         AACT[:, TVb:TVb + 512], ALU.subtract)
                    vector.tensor_tensor(AACT[:, TVa:TVa + 512], psum[bre][:, :], kfi, ALU.mult)
                    vector.tensor_tensor(AACT[:, TVb:TVb + 512], psum[bim][:, :], kfr, ALU.mult)
                    vector.tensor_tensor(AACT[:, vim:vim + 512], AACT[:, TVa:TVa + 512],
                                         AACT[:, TVb:TVb + 512], ALU.add)
                ins = vector.tensor_tensor(AACT[0:1, VXR:VXR + 1024], AACT[0:1, UXR:UXR + 1024],
                                           mtab[0:1, 0:1024], ALU.mult)
                ins.then_inc(sems["vE"], 1)
            emit("vector", s5_pw)
            pw = bump("vE")

            def s5_mm2(tensor, c=c, v=pw):
                tensor.wait_ge(sems["vE"], v)
                ins = None
                for hh, (vre, vim, yb) in enumerate(((VR0, VI0, 6), (VR1, VI1, 7))):
                    ins = tensor.matmul(psum[yb][:, :], blkq[:, 258:386],
                                        AACT[:, vre:vre + 512], start=True, stop=False)
                    ins = tensor.matmul(psum[yb][:, :], blkq[:, 386:514],
                                        AACT[:, vim:vim + 512], start=False, stop=False)
                    ins = tensor.matmul(psum[yb][:, :], aev[0:1, 0:128],
                                        AACT[0:1, VXR + 512 * hh:VXR + 512 * hh + 512],
                                        start=False, stop=True)
                ins.then_inc(sems["tE"], 1)
            emit("tensor", s5_mm2)
            mm2 = bump("tE")

            def s5_fin_v(vector, c=c, t=mm2, ld=ld5):
                vector.wait_ge(sems["tE"], t)
                vector.wait_ge(sems["ldx"], ld)
                ya = AW[:, 8192:16384].rearrange("p (h s) -> p h s", s=16)[:, :, c]
                yb = AS[:, 0:8192].rearrange("p (h s) -> p h s", s=16)[:, :, c]
                vector.tensor_tensor(AACT[:, YT:YT + 512], psum[6][:, :], ya, ALU.add)
                vector.tensor_tensor(AACT[:, YT + 512:YT + 1024], psum[7][:, :], yb, ALU.add)
                ins = vector.scalar_tensor_tensor(AACT[:, DUB:DUB + 1024], AACT[:, UT:UT + 1024],
                                                  rsx[:, 0:1], AACT[:, YT:YT + 1024],
                                                  ALU.mult, ALU.add)
                ins.then_inc(sems["vE"], 1)
            emit("vector", s5_fin_v)
            duv = bump("vE")

            def s5_gelu(scalar, v=duv):
                scalar.wait_ge(sems["vE"], v)
                ins = scalar.activation(AACT[:, UT:UT + 1024], AACT[:, DUB:DUB + 1024], AF.Gelu)
                ins.then_inc(sems["sE"], 1)
            emit("scalar", s5_gelu)
            gel = bump("sE")

            def s5_gate(vector, s=gel):
                vector.wait_ge(sems["sE"], s)
                vector.tensor_tensor(AACT[:, DUB:DUB + 1024], AACT[:, UT:UT + 1024],
                                     AACT[:, G2T:G2T + 1024], ALU.mult)
                ins = vector.tensor_tensor(AACT[:, G2T:G2T + 1024], AACT[:, DUB:DUB + 1024],
                                           AACT[:, X1T:X1T + 1024], ALU.add)
                ins.then_inc(sems["vE"], 1)
            emit("vector", s5_gate)
            gate = bump("vE")
            s5_prev_v = gate

            def s5_out(gpsimd, c=c, v=gate):
                gpsimd.wait_ge(sems["vE"], v)
                gpsimd.dma_start(out=ap["out_x"][c * 128:(c + 1) * 128, :],
                                 in_=AACT[:, G2T:G2T + 1024]).then_inc(sems["stf"], 16)
            emit("gpsimd", s5_out)
            s5_prev_stf = bump("stf", 16)

        def s5_wait(gpsimd, fin=cnt["stf"]):
            gpsimd.wait_ge(sems["stf"], fin)
        emit("gpsimd", s5_wait)

        with nc.Block() as block:
            @block.sync
            def _(sync):
                for fn in ops["sync"]:
                    fn(sync)

            @block.tensor
            def _(tensor):
                for fn in ops["tensor"]:
                    fn(tensor)

            @block.vector
            def _(vector):
                for fn in ops["vector"]:
                    fn(vector)

            @block.scalar
            def _(scalar):
                for fn in ops["scalar"]:
                    fn(scalar)

            @block.gpsimd
            def _(gpsimd):
                for fn in ops["gpsimd"]:
                    fn(gpsimd)

    return nc, dbgs


_CACHE = {}


def kernel(**inputs):
    if "nc" not in _CACHE:
        _CACHE["nc"], _CACHE["dbgs"] = build_program()
    nc, dbgs = _CACHE["nc"], _CACHE["dbgs"]
    consts = _host_consts()
    x = np.ascontiguousarray(inputs["L0_attn_x"], dtype=np.float32)
    wqkv = np.ascontiguousarray(inputs["L0_attn_Wqkv"], dtype=np.float32)
    wgT = np.ascontiguousarray(inputs["L0_ffn_w_gate"].T, dtype=np.float32)
    wuT = np.ascontiguousarray(inputs["L0_ffn_w_up"].T, dtype=np.float32)
    wdT = np.ascontiguousarray(inputs["L0_ffn_w_down"].T, dtype=np.float32)
    wout = np.ascontiguousarray(inputs["L0_attn_Wout"], dtype=np.float32)
    wg1 = np.asarray(inputs["L1_ffn_w_gate"], dtype=np.float32)
    wu1 = np.asarray(inputs["L1_ffn_w_up"], dtype=np.float32)
    wd1 = np.asarray(inputs["L1_ffn_w_down"], dtype=np.float32)
    wg1p = np.zeros((3072, D), dtype=np.float32); wg1p[0:INNER1] = wg1
    wu1p = np.zeros((3072, D), dtype=np.float32); wu1p[0:INNER1] = wu1

    def traj2(a):
        return np.ascontiguousarray(
            np.asarray(a, np.float32).reshape(256, 4, 32).transpose(1, 2, 0).reshape(128, 256))

    log_dt = np.asarray(inputs["L1_attn_log_dt"], np.float32)
    ssm_logdt = np.ascontiguousarray(log_dt.reshape(1, 1024))
    ssm_aim1 = np.ascontiguousarray(
        np.asarray(inputs["L1_attn_A_imag"], np.float32)[:, 1].reshape(1, 1024))
    ssm_are0 = np.ascontiguousarray(
        np.asarray(inputs["L1_attn_A_real"], np.float32)[:, 0].reshape(1, 1024))
    ssm_AR2 = traj2(inputs["L1_attn_A_real"])
    ssm_AI2 = traj2(inputs["L1_attn_A_imag"])
    ssm_CR2 = traj2(inputs["L1_attn_C_real"])
    ssm_CI2 = traj2(inputs["L1_attn_C_imag"])
    ssm_LD2 = np.ascontiguousarray(
        np.repeat(log_dt.reshape(256, 4).T, 32, axis=0))
    ssm_D = np.asarray(inputs["L1_attn_D"], np.float32).reshape(1, 1)
    jj = np.arange(128, dtype=np.float32)
    ssm_lrows = np.stack([jj, 127.0 - jj, np.ones(128, np.float32),
                          np.zeros(128, np.float32)]).astype(np.float32)
    h4m = np.zeros((128, 4), dtype=np.float32)
    for h4 in range(4):
        h4m[32 * h4:32 * h4 + 32, h4] = 1.0
    fcol = np.ascontiguousarray(np.cos(
        2.0 * np.pi * np.arange(128) * 128.0 / 256.0).astype(np.float32).reshape(128, 1))

    in_maps = []
    for c in range(NCORES):
        b, half = c // 2, c % 2
        xv = np.zeros((NV * 128, D), dtype=np.float32)
        if half == 0:
            xv[32 * 128:, :] = x[b, 0:2048, :]
        else:
            xv[16 * 128:, :] = x[b, :, :]
        blkA = np.full((128, 256), 1.0 if half == 1 else 0.0, dtype=np.float32)
        blkB = np.zeros((128, 256), dtype=np.float32)
        in_maps.append({"xv": xv, "wqkv": wqkv, "identity": consts["identity"],
                        "attn_mask": consts["attn_mask"],
                        "attn_bias": _attn_bias_host(half),
                        "attn_fden": _attn_fden_host(half),
                        "wout": wout,
                        "wgT": wgT, "wuT": wuT, "wdT": wdT,
                        "w1shard": _w1shard(wg1p, wu1p, wd1, c),
                        "ssm_logdt": ssm_logdt, "ssm_aim1": ssm_aim1,
                        "ssm_are0": ssm_are0, "ssm_AR2": ssm_AR2,
                        "ssm_AI2": ssm_AI2, "ssm_CR2": ssm_CR2,
                        "ssm_CI2": ssm_CI2, "ssm_LD2": ssm_LD2,
                        "ssm_flast": fcol, "ssm_D": ssm_D,
                        "ssm_lrows": ssm_lrows, "ssm_h4mask": h4m,
                        "ssm_blkA": blkA, "ssm_blkB": blkB,
                        "F_re": consts["F_re"], "F_im": consts["F_im"],
                        "G_re": consts["G_re"], "G_im": consts["G_im"]})

    res = run_bass_kernel_spmd(nc, in_maps, core_ids=list(range(NCORES)))
    _CACHE["results"] = res.results
    outs = [res.results[c]["out_x"] for c in range(NCORES)]
    full = np.zeros((B, T, D), dtype=np.float32)
    for c in range(NCORES):
        b, half = c // 2, c % 2
        full[b, half * 2048:(half + 1) * 2048, :] = outs[c]
    return full
